# version 01
# baseline (speedup 1.0000x reference)
import numpy as np
import ml_dtypes
import concourse.bass as bass
import concourse.mybir as mybir
from concourse.bass_utils import run_bass_kernel_spmd
from contextlib import ExitStack
from concourse.bass_isa import ReduceOp

F32 = mybir.dt.float32
BF16 = mybir.dt.bfloat16
AF = mybir.ActivationFunctionType
ALU = mybir.AluOpType
AX = mybir.AxisListType
BF = ml_dtypes.bfloat16

T = 2048
K = 4096
NEG = -1e30

_DBG = []
_STAGES = 99
_DBG_RES = {}
_DBGQG = 0


class Sem:
    def __init__(self, nc, es, name):
        self.h = es.enter_context(nc.semaphore(name))
        self.n = 0

    def inc(self, instr, kk=1):
        instr.then_inc(self.h, kk)
        self.n += kk
        return instr


class EngProxy:
    def __init__(self, section, eng):
        self._section = section
        self._eng = eng

    def __getattr__(self, name):
        m = getattr(self._eng, name)
        if not callable(m):
            return m
        def call(*a, **kw):
            out = [None]
            def f(_e):
                out[0] = m(*a, **kw)
            self._section(f)
            return out[0]
        return call


class Ker:
    def __init__(self):
        self.nc = bass.Bass("TRN2", target_bir_lowering=False,
                            detect_race_conditions=False)
        self.es = ExitStack()
        nc, es = self.nc, self.es
        self.t, self.v, self.s, self.g, self.y = (
            nc.tensor, nc.vector, nc.scalar, nc.gpsimd, nc.sync)
        self.LDS = Sem(nc, es, "lds")
        self.LDG = Sem(nc, es, "ldg")
        self.T = Sem(nc, es, "t")
        self.V = Sem(nc, es, "v")
        self.S = Sem(nc, es, "s")
        self.G = Sem(nc, es, "g")
        self.CC = Sem(nc, es, "cc")
        self.ST = Sem(nc, es, "st")
        self.GATE = Sem(nc, es, "gate")
        self.all_sems = [self.LDS, self.LDG, self.T, self.V, self.S, self.G, self.CC]
        self._dbg_outs = {}

    def bind(self, blk):
        self.t = EngProxy(blk.tensor, self.nc.tensor)
        self.v = EngProxy(blk.vector, self.nc.vector)
        self.s = EngProxy(blk.scalar, self.nc.scalar)
        self.g = EngProxy(blk.gpsimd, self.nc.gpsimd)
        self.y = EngProxy(blk.sync, self.nc.sync)
        return self.t, self.v, self.s, self.g, self.y

    def sb(self, name, shape, dtype=F32, stack=None):
        return (stack or self.es).enter_context(
            self.nc.sbuf_tensor(name, list(shape), dtype))

    def es_ps(self, stack, name, shape, dtype=F32):
        return stack.enter_context(self.nc.psum_tensor(name, list(shape), dtype))

    _psn = [0]
    def ps(self, shape=(128, 512), dtype=F32, name=None):
        self._psn[0] += 1
        return self.es.enter_context(self.nc.psum_tensor(
            name or f"psb{self._psn[0]}", list(shape), dtype))

    def barrier(self):
        for e in (self.t, self.v, self.s, self.g, self.y):
            for sm in self.all_sems:
                if sm.n:
                    e.wait_ge(sm.h, sm.n)

    def w(self, eng, *sems):
        for sm in sems:
            if sm.n:
                eng.wait_ge(sm.h, sm.n)

    def constf(self, val, parts=128):
        key = (float(val), parts)
        if not hasattr(self, "_consts"):
            self._consts = {}
        if key not in self._consts:
            tname = f"c{len(self._consts)}"
            tt = self.sb(tname, (parts, 1))
            self.V.inc(self.v.memset(tt[:], float(val)))
            self._consts[key] = tt
        return self._consts[key][:]

    def gate(self, eng, *sems):
        self.w(eng, *sems)
        self.GATE.inc(eng.nop())
        return (self.GATE.h, self.GATE.n)

    def dbg_out(self, name, shape, dtype=F32):
        if name not in _DBG:
            return None
        ap = self.nc.dram_tensor("dbg_" + name, list(shape), dtype,
                                 kind="ExternalOutput").ap()
        self._dbg_outs[name] = ap
        return ap

    def dump(self, name, src_ap, col=None):
        ap = self._dbg_outs.get(name)
        if ap is None:
            return
        if col is not None:
            ap = ap[col[0]:col[1]] if isinstance(col, tuple) else ap[col]
        self.w(self.y, self.V, self.S, self.T, self.G)
        self.ST.inc(self.y.dma_start(out=ap, in_=src_ap), 16)


def build():
    k = Ker()
    nc, es = k.nc, k.es
    t, v, s, g, y = k.t, k.v, k.s, k.g, k.y

    def din(name, shape, dtype=F32):
        return nc.dram_tensor(name, list(shape), dtype, kind="ExternalInput").ap()

    xT_keys = din("xT_keys", (1024, K))
    sc_mask = din("sc_mask", (128, 32))
    sc_mask_row = din("sc_mask_row", (1, K))
    diag_mask = din("diag_mask", (2, 128, 256))
    wjT = din("wjT", (1024, 1024))
    wjvT = din("wjvT", (1024, 1024))
    woT = din("woT", (1024, 1024))
    eye_f = din("eye_f", (128, 128))
    wqkv_i = [din(f"wqkv{l}", (1024, 1536)) for l in range(2)]
    wqkvp_i = [din(f"wqkvp{l}", (1024, 1024)) for l in range(2)]
    wo_i = [din(f"wol{l}", (512, 1024)) for l in range(2)]
    qsr = [din(f"qs{l}", (1, 64)) for l in range(2)]
    qspr = [din(f"qsp{l}", (1, 64)) for l in range(2)]
    ksr = [din(f"ks{l}", (1, 64)) for l in range(2)]
    kspr = [din(f"ksp{l}", (1, 64)) for l in range(2)]
    rcq_t = din("rcq_t", (2, 128, T), BF16)
    rck_t = din("rck_t", (2, 128, 2560), BF16)
    wmask_i = din("wmask", (16, 128, 640))
    cz_i = din("cz", (22, 128, 128), BF16)
    gate_i = din("gate128", (128, 1))
    lng_i = [din(f"lng{l}", (128, 8)) for l in range(2)]
    lnb_i = [din(f"lnb{l}", (128, 8)) for l in range(2)]
    w1_i = [din(f"w1r{l}", (1024, 5632)) for l in range(2)]
    b1_i = [din(f"b1f{l}", (128, 44)) for l in range(2)]
    w2_i = [din(f"w2r{l}", (2816, 1024)) for l in range(2)]
    b2_i = [din(f"b2f{l}", (128, 8)) for l in range(2)]
    p2h_i = din("p2h_i", (2, 128))
    hd2_i = din("hd2_i", (128, 2))
    qscol_i = [din(f"qscol{l}", (128, 4)) for l in range(2)]
    lngr_i = [din(f"lngr{l}", (1, 1024)) for l in range(2)]
    lnbr_i = [din(f"lnbr{l}", (1, 1024)) for l in range(2)]
    lnw_i = din("lnw_f", (128, 8))
    lnb_i = din("lnb_f", (128, 8))
    inpj_i = din("inpj", (1024, 2048))
    xpj_i = din("xpj", (1024, 96))
    dtw_i = din("dtwc", (64, 1024))
    opj_i = din("opj", (1024, 1024))
    alog_i = din("alog_f", (128, 128))
    dcol_i = din("dcol_f", (128, 8))
    cbq_i = din("cb_f", (128, 8))
    dtb_i = din("dtb_f", (128, 8))
    cw_i = din("cw_f", (128, 32))
    w1t_i = din("w1t_r", (1024, 4096))
    w2t_i = din("w2t_r", (4096, 1024))
    w1slab_i = din("w1slab", (128, 4096))
    w2slab_i = din("w2slab", (512, 1024))
    b1q_i = din("b1q_f", (128, 32))
    b2q_i = din("b2q_f", (128, 8))
    esel_i = din("esel_b", (32, 4096), BF16)
    eyeb_i = din("eyeb", (128, 128), BF16)
    g5_i = din("g5c", (128, 2))
    out_fm = nc.dram_tensor("out_fm", [1024, T], F32, kind="ExternalOutput").ap()

    val_hbm = nc.dram_tensor("val_hbm", [K, 1024], BF16, kind="Internal").ap()
    ab_hbm = nc.dram_tensor("ab_hbm", [1, 2 * T], F32, kind="Internal").ap()
    xext_hbm = nc.dram_tensor("xext_hbm", [1024, 2561], BF16, kind="Internal").ap()
    s_hbm = nc.dram_tensor("s_hbm", [1024, T], BF16, kind="Internal").ap()
    cc_in = nc.dram_tensor("cc_in", [1024, 516], F32, kind="Internal").ap()
    cc_out = nc.dram_tensor("cc_out", [2, 1024, 516], F32, kind="Internal").ap()

    onesrow = k.sb("onesrow", (1, 128))
    onescol_b = k.sb("onescol_b", (128, 1), BF16)
    onescol_f = k.sb("onescol_f", (128, 1))
    eye_sb = k.sb("eye_sb", (128, 128))
    scmask_sb = k.sb("scmask_sb", (128, 32))
    sc_f = k.sb("sc_f", (128, 32))
    scG = k.sb("scG", (128, 32))
    scT_sb = k.sb("scT_sb", (32, 128))
    grow = k.sb("grow", (128, 4))


    es_pb = ExitStack()
    pb = [k.es_ps(es_pb, f'pb{i}', (128, 512)) for i in range(8)]

    sccols = k.sb("sccols", (128, 4))
    k.constf(1e-5, 1)
    k.constf(12582912.0, 1)
    k.constf(12582912.0, 128)
    k.constf(1.0, 128)
    mxrow = k.sb("mxrow", (128, 2))
    scr = k.sb("scr", (1, 16))
    qsrow = k.sb("qsrow", (1, 8))
    q2row = k.sb("q2row", (1, 4))
    qpar = k.sb("qpar", (128, 8))
    qpar2 = k.sb("qpar2", (128, 4))
    g5_sb = k.sb("g5_sb", (128, 2))
    es_vs = ExitStack()
    vstage = k.sb("vstage", (128, 2 * 1024), BF16, stack=es_vs)
    es_a = ExitStack()
    xnb = k.sb("xnb", (128, 8 * K), BF16, stack=es_a)
    es_1 = ExitStack()

    with nc.Block() as blk:
        t, v, s, g, y = k.bind(blk)

        k.V.inc(v.memset(onesrow[:], 1.0))
        k.V.inc(v.memset(onescol_b[:], 1.0))
        k.V.inc(v.memset(onescol_f[:], 1.0))
        k.LDS.inc(y.dma_start(out=scmask_sb[:], in_=sc_mask), 16)
        k.LDS.inc(y.dma_start(out=eye_sb[:], in_=eye_f), 16)
        xslab = [k.sb("xslab0", (128, 8 * 512), stack=es_1)]
        sqb = [k.sb(f"sqb{p}", (128, 512), stack=es_1) for p in range(2)]
        lnrow = k.sb("lnrow", (1, 4 * 512), stack=es_1)
        rowA = k.sb("rowA", (1, 512), stack=es_1)
        rowB = k.sb("rowB", (1, 512), stack=es_1)
        tscr = k.sb("tscr", (128, 512), stack=es_1)

        for sl in range(8):
            xb = xslab[0]
            k.w(y, k.V, k.S, k.T)
            for i in range(8):
                k.LDS.inc(y.dma_start(
                    out=xb[:, 512 * i:512 * (i + 1)],
                    in_=xT_keys[128 * i:128 * (i + 1), 512 * sl:512 * (sl + 1)]), 16)
            k.w(t, k.LDS, k.V, k.S)
            for i in range(8):
                mm = t.matmul(pb[0][0:1, :], onescol_f[:],
                              xb[:, 512 * i:512 * (i + 1)],
                              start=(i == 0), stop=(i == 7))
            k.T.inc(mm)
            for i in range(8):
                sq = sqb[i % 2]
                k.w(s, k.T)
                k.S.inc(s.activation(sq[:], xb[:, 512 * i:512 * (i + 1)], AF.Square))
                k.w(t, k.S)
                k.T.inc(t.matmul(pb[1][0:1, :], onescol_f[:], sq[:],
                                 start=(i == 0), stop=(i == 7)))
            MU, M2, MSQ, VAR = (lnrow[0:1, 512 * i:512 * (i + 1)] for i in range(4))
            AA, BB = rowA[:], rowB[:]
            k.w(s, k.T)
            k.S.inc(s.activation(MU, pb[0][0:1, :], AF.Copy, scale=1.0 / 1024))
            k.S.inc(s.activation(M2, pb[1][0:1, :], AF.Copy, scale=1.0 / 1024))
            k.w(v, k.S)
            k.V.inc(v.tensor_mul(MSQ, MU, MU))
            k.V.inc(v.tensor_sub(VAR, M2, MSQ))
            k.w(s, k.V)
            k.S.inc(s.activation(MSQ, VAR, AF.Ln, bias=k.constf(1e-5, 1)))
            k.S.inc(s.activation(AA, MSQ, AF.Exp, scale=-0.5))
            k.w(v, k.S)
            k.V.inc(v.tensor_mul(BB, MU, AA))
            k.V.inc(v.tensor_scalar_mul(BB, BB, -1.0))
            if sl >= 4:
                k.w(y, k.S, k.V)
                k.LDS.inc(y.dma_start(
                    out=ab_hbm[0:1, 512 * (sl - 4):512 * (sl - 3)], in_=AA), 16)
                k.LDS.inc(y.dma_start(
                    out=ab_hbm[0:1, T + 512 * (sl - 4):T + 512 * (sl - 3)], in_=BB), 16)
            k.w(t, k.V, k.S)
            k.T.inc(t.matmul(pb[2][:, 0:512], onesrow[:], AA, start=True, stop=True))
            k.T.inc(t.matmul(pb[3][:, 0:512], onesrow[:], BB, start=True, stop=True))
            k.w(v, k.T)
            for i in range(8):
                v.tensor_mul(tscr[:], xb[:, 512 * i:512 * (i + 1)], pb[2][:, 0:512])
                k.V.inc(v.tensor_add(
                    xnb[:, K * i + 512 * sl:K * i + 512 * (sl + 1)], tscr[:], pb[3][:, 0:512]))
        for i in range(8):
            k.w(y, k.V)
            k.LDS.inc(y.dma_start(out=xext_hbm[128 * i:128 * (i + 1), :],
                                  in_=xnb[:, K * i + 1535:K * i + 4096]), 16)
        xn_dbg = k.dbg_out("xn", (1024, K), BF16)
        if xn_dbg is not None:
            for i in range(8):
                k.dump("xn", xnb[:, K * i:K * (i + 1)], col=(128 * i, 128 * (i + 1)))

        if _STAGES < 2:
            finalize(k, out_fm)
            return k

        es_1.close()
        w3 = k.sb("w3", (128, 3 * 8192), BF16, stack=es_a)
        gh, gn = k.gate(g, k.T, k.V, k.S)
        for wi, wsrc in enumerate((wjT, wjvT, woT)):
            for i in range(8):
                dd = g.dma_start(
                    out=w3[:, 8192 * wi + 1024 * i:8192 * wi + 1024 * (i + 1)],
                    in_=wsrc[128 * i:128 * (i + 1), :])
                dd._wait_ge(gh, gn)
                k.LDG.inc(dd, 16)
        sqa = [k.sb(f"sqa{p}", (128, 512), stack=es_a) for p in range(2)]
        k.w(t, k.LDG)
        for j in range(32):
            for wi, pbank in ((0, (pb[0], pb[1])), (1, (pb[4], pb[5]))):
                for h in range(2):
                    k.w(t, k.V, k.S)
                    for i in range(8):
                        mm = t.matmul(
                            pbank[h][:, 0:512],
                            xnb[:, K * i + 128 * j:K * i + 128 * (j + 1)],
                            w3[:, 8192 * wi + 1024 * i + 512 * h:
                               8192 * wi + 1024 * i + 512 * (h + 1)],
                            start=(i == 0), stop=(i == 7))
                    k.T.inc(mm)
            k.w(s, k.T)
            k.S.inc(s.activation(sqa[0][:], pb[0][:, 0:512], AF.Square,
                                 accum_out=sccols[:, 0:1]))
            k.S.inc(s.activation(sqa[1][:], pb[1][:, 0:512], AF.Square,
                                 accum_out=sccols[:, 1:2]))
            k.w(v, k.S)
            v.tensor_add(sccols[:, 2:3], sccols[:, 0:1], sccols[:, 1:2])
            k.V.inc(v.tensor_scalar_mul(sc_f[:, j:j + 1], sccols[:, 2:3], 1.0 / 32.0))
            k.w(v, k.T, k.LDS)
            k.V.inc(v.tensor_copy(vstage[:, 1024 * (j % 2):1024 * (j % 2) + 512],
                                  pb[4][:, 0:512]))
            k.w(s, k.T, k.LDS)
            k.S.inc(s.copy(vstage[:, 1024 * (j % 2) + 512:1024 * (j % 2) + 1024],
                           pb[5][:, 0:512]))
            k.w(y, k.V, k.S)
            k.LDS.inc(y.dma_start(out=val_hbm[128 * j:128 * (j + 1), :],
                                  in_=vstage[:, 1024 * (j % 2):1024 * (j % 2) + 1024]), 16)
        sc_dbg = k.dbg_out("sc", (128, 32))
        if sc_dbg is not None:
            k.dump("sc", sc_f[:])

        k.barrier()
        es_a.close()
        es_3 = ExitStack()
        scq_bc = k.sb("scq_bc", (128, K), stack=es_3)
        mrow = k.sb("mrow", (1, K), stack=es_3)
        scrow = k.sb("scrow", (1, K), stack=es_3)
        dmask_sb = k.sb("dmask_sb", (128, 2 * 256), stack=es_3)
        linv_sb = k.sb("linv_sb", (128, 256), stack=es_3)
        sres = k.sb("sres", (128, 2 * 256), BF16, stack=es_3)
        for i_ in range(2):
            dd = y.dma_start(out=dmask_sb[:, 256 * i_:256 * (i_ + 1)],
                             in_=diag_mask[i_])
            dd._wait_ge(gh, gn)
            k.LDS.inc(dd, 16)
        gh, gn = k.gate(y, k.T, k.V, k.S, k.G)
        dd = y.dma_start(out=mrow[:], in_=sc_mask_row)
        dd._wait_ge(gh, gn)
        k.LDS.inc(dd, 16)
        k.w(t, k.V, k.S)
        k.T.inc(t.transpose(pb[6][0:32, 0:128], sc_f[:], eye_sb[:]))
        k.w(s, k.T)
        k.S.inc(s.copy(scT_sb[:], pb[6][0:32, 0:128]))
        k.w(y, k.S)
        k.LDS.inc(y.dma_start(out=scrow[:], in_=scT_sb[:]), 16)
        k.w(v, k.LDS)
        k.V.inc(v.tensor_add(mrow[:], scrow[:], mrow[:]))
        k.V.inc(v.tensor_reduce(grow[0:1, 0:1], mrow[:], AX.X, ALU.max))
        k.w(t, k.V)
        k.T.inc(t.matmul(pb[6][:, 0:1], onesrow[:], grow[0:1, 0:1],
                         start=True, stop=True))
        k.w(s, k.T)
        k.S.inc(s.copy(grow[:, 1:2], pb[6][:, 0:1]))
        k.w(v, k.S)
        k.V.inc(v.tensor_scalar(scG[:], sc_f[:], grow[:, 1:2], None, ALU.subtract))
        for j in range(32):
            bank = pb[6 + (j % 2)]
            k.w(t, k.S, k.LDS)
            k.T.inc(t.matmul(bank[:, 0:128], onesrow[:],
                             scrow[0:1, 128 * j:128 * (j + 1)],
                             start=True, stop=True))
            k.w(s, k.T)
            k.S.inc(s.copy(scq_bc[:, 128 * j:128 * (j + 1)], bank[:, 0:128]))

        dscG = k.dbg_out("scG", (128, 32))
        if dscG is not None:
            k.dump("scG", scG[:])
        dpt = k.dbg_out("pt0", (128, 32 * 256), BF16)
        dptp = k.dbg_out("ptpre", (128, 32 * 256), BF16)
        dacc = k.dbg_out("acc0", (128, 8 * 256), BF16)
        dlin = k.dbg_out("linv0", (128, 256))
        ptbuf = k.sb("ptbuf", (128, 32 * 256), BF16, stack=es_3)
        k.V.inc(v.memset(ptbuf[:], 0.0))
        accb = k.sb("accb", (128, 8 * 256), BF16, stack=es_3)
        expo = k.sb("expo", (128, 256), stack=es_3)
        woT_sb = k.sb("woT_sb", (128, 8192), BF16, stack=es_3)
        gh, gn = k.gate(g, k.T, k.V, k.S)
        for i in range(8):
            dd = g.dma_start(out=woT_sb[:, 1024 * i:1024 * (i + 1)],
                             in_=woT[128 * i:128 * (i + 1), :])
            dd._wait_ge(gh, gn)
            k.LDG.inc(dd, 16)
        woTb = woT_sb
        k.w(t, k.LDG)
        for qg in range(8):
            njt = 16 + 2 * qg + 2
            qcol = 2048 + 256 * qg
            for j in range(njt):
                k.w(v, k.S)
                ts = v.tensor_scalar(expo[:], scq_bc[:, qcol:qcol + 256],
                                     scG[:, j:j + 1], scmask_sb[:, j:j + 1],
                                     ALU.mult, ALU.add)
                if j >= 16 + 2 * qg:
                    o = j - (16 + 2 * qg)
                    ts = v.tensor_add(expo[:], expo[:],
                                      dmask_sb[:, 256 * o:256 * (o + 1)])
                k.V.inc(ts)
                k.w(s, k.V)
                if qg and j == 0:
                    k.w(s, k.T)
                k.S.inc(s.activation(ptbuf[:, 256 * j:256 * (j + 1)], expo[:], AF.Exp))
            if qg == _DBGQG:
                k.dump("ptpre", ptbuf[:])
            k.w(t, k.S, k.V)
            for j in range(njt):
                mm = t.matmul(pb[2][0:1, 0:256], onescol_b[:],
                              ptbuf[:, 256 * j:256 * (j + 1)],
                              start=(j == 0), stop=(j == njt - 1))
            k.T.inc(mm)
            k.w(v, k.T)
            k.V.inc(v.reciprocal(linv_sb[0:1, 0:256], pb[2][0:1, 0:256]))
            k.w(t, k.V)
            k.T.inc(t.matmul(pb[3][:, 0:256], onesrow[:], linv_sb[0:1, 0:256],
                             start=True, stop=True))
            k.w(s, k.T)
            k.S.inc(s.copy(linv_sb[:, 0:256], pb[3][:, 0:256]))
            k.w(t, k.S)
            for j in range(njt):
                k.w(y, k.T)
                k.LDS.inc(y.dma_start(out=vstage[:, 1024 * (j % 2):1024 * (j % 2 + 1)],
                                      in_=val_hbm[128 * j:128 * (j + 1), :]), 16)
                k.w(t, k.LDS)
                for m in range(8):
                    mm = t.matmul(pb[m][:, 0:256],
                                  vstage[:, 1024 * (j % 2) + 128 * m:1024 * (j % 2) + 128 * (m + 1)],
                                  ptbuf[:, 256 * j:256 * (j + 1)],
                                  start=(j == 0), stop=(j == njt - 1))
                k.T.inc(mm)
            k.w(v, k.T)
            for m in range(8):
                k.V.inc(v.tensor_copy(accb[:, 256 * m:256 * (m + 1)],
                                      pb[m][:, 0:256]))
            if qg == _DBGQG:
                k.dump("pt0", ptbuf[:])
                k.dump("acc0", accb[:])
                k.dump("linv0", linv_sb[:, 0:256])
            for m in range(8):
                k.w(t, k.V, k.S)
                bank = pb[6 + (m % 2)]
                for i in range(8):
                    mm = t.matmul(bank[:, 0:256], woTb[:, 1024 * i + 128 * m:1024 * i + 128 * (m + 1)],
                                  accb[:, 256 * i:256 * (i + 1)],
                                  start=(i == 0), stop=(i == 7))
                k.T.inc(mm)
                k.w(v, k.T, k.LDS)
                k.V.inc(v.tensor_mul(sres[:, 256 * (m % 2):256 * (m % 2 + 1)],
                                     bank[:, 0:256], linv_sb[:, 0:256]))
                k.w(y, k.V)
                k.LDS.inc(y.dma_start(
                    out=s_hbm[128 * m:128 * (m + 1), 256 * qg:256 * (qg + 1)],
                    in_=sres[:, 256 * (m % 2):256 * (m % 2 + 1)]), 16)
        s_dbg = k.dbg_out("s0", (1024, T), BF16)
        if s_dbg is not None:
            k.w(k.y, k.LDS)
            k.ST.inc(k.y.dma_start(out=s_dbg, in_=s_hbm), 16)



        k.barrier()
        es_3.close()
        es_vs.close()
        es_pb.close()
        es_f = ExitStack()
        xi = k.sb("xi", (128, 8 * 2049), stack=es_f)
        xslf = k.sb("xslf", (128, 8 * 513), BF16, stack=es_f)
        escr = k.sb("escr", (128, 2 * 512), stack=es_f)
        sqn = k.sb("sqn", (128, 513), stack=es_f)
        p2h = k.sb("p2h", (2, 128), stack=es_f)
        hd2 = k.sb("hd2", (128, 2), stack=es_f)
        gsc = k.sb("gsc", (128, 1), stack=es_f)
        qscols = k.sb("qscols", (128, 4), stack=es_f)
        gtile = k.sb("gtile", (128, 2 * 512), BF16, stack=es_f)
        ggscr = k.sb("ggscr", (128, 512), stack=es_f)
        gcol = k.sb("gcol", (128, 8), stack=es_f)
        lgr = k.sb("lgr", (1, 2 * 1024), stack=es_f)
        b1sb = k.sb("b1sb", (128, 44), stack=es_f)
        b2sb = k.sb("b2sb", (128, 8), stack=es_f)

        gated_hbm = nc.dram_tensor("gated_hbm", [4, 22, 128, 512], BF16,
                                   kind="Internal").ap()
        x1_hbm = nc.dram_tensor("x1_hbm", [1024, T], F32, kind="Internal").ap()
        cc0_in = [nc.dram_tensor(f"cc0i{l_}", [1024, 1], F32, kind="Internal").ap()
                  for l_ in range(2)]
        cc0_out = [nc.dram_tensor(f"cc0o{l_}", [2, 1024, 1], F32, kind="Internal").ap()
                   for l_ in range(2)]
        cc1_in = nc.dram_tensor("cc1i", [1024, 513], BF16, kind="Internal").ap()
        cc1_out = nc.dram_tensor("cc1o", [2, 1024, 513], BF16, kind="Internal").ap()

        ghx, gnx = k.gate(y, k.T, k.V, k.S, k.G)
        for nm_ap, dstt in ((p2h_i, p2h), (gate_i, gsc), (hd2_i, hd2)):
            dd = y.dma_start(out=dstt[:], in_=nm_ap)
            dd._wait_ge(ghx, gnx)
            k.LDS.inc(dd, 16)

        CZMAP = {}
        for qt_ in range(16):
            for u_ in range(5):
                if qt_ < 4:
                    CZMAP[(qt_, u_)] = qt_ * 5 + u_
                elif u_ == 0:
                    CZMAP[(qt_, u_)] = 20
                elif u_ == 4:
                    CZMAP[(qt_, u_)] = 21

        ps_init = ExitStack()
        pA0 = k.es_ps(ps_init, "pA0", (128, 512))
        pB0 = k.es_ps(ps_init, "pB0", (128, 512))

        es_ab = ExitStack()
        abrows = k.sb("abrows", (1, 2 * T), stack=es_ab)
        gh0, gn0 = k.gate(y, k.T, k.V, k.S)
        dd = y.dma_start(out=abrows[:], in_=ab_hbm)
        dd._wait_ge(gh0, gn0)
        k.LDS.inc(dd, 16)
        for sl in range(4):
            k.w(t, k.V, k.S, k.LDS)
            k.T.inc(t.matmul(pA0[:, 0:512], onesrow[:],
                             abrows[0:1, 512 * sl:512 * (sl + 1)],
                             start=True, stop=True))
            k.T.inc(t.matmul(pB0[:, 0:512], onesrow[:],
                             abrows[0:1, T + 512 * sl:T + 512 * (sl + 1)],
                             start=True, stop=True))
            for i in range(8):
                k.w(y, k.V)
                k.LDS.inc(y.dma_start(
                    out=escr[:, 0:512],
                    in_=xT_keys[128 * i:128 * (i + 1),
                                2048 + 512 * sl:2048 + 512 * (sl + 1)]), 16)
                k.w(v, k.T, k.LDS)
                v.tensor_mul(escr[:, 512:1024], escr[:, 0:512], pA0[:, 0:512])
                k.V.inc(v.tensor_add(
                    xi[:, 2049 * i + 1 + 512 * sl:2049 * i + 1 + 512 * (sl + 1)],
                    escr[:, 512:1024], pB0[:, 0:512]))
        for i in range(8):
            k.V.inc(v.memset(xi[:, 2049 * i:2049 * i + 1], 0.0))
        k.barrier()
        es_ab.close()
        ps_init.close()

        for l in range(2):
            es_al = ExitStack()
            qkw = k.sb(f"qkw{l}", (128, 8 * 2560), BF16, stack=es_al)
            kat = k.sb(f"kat{l}", (65, 8 * 12 * 128), BF16, stack=es_al)
            vat = k.sb(f"vat{l}", (128, 12 * 520), BF16, stack=es_al)
            qat = k.sb(f"qat{l}", (65, 8 * 512), BF16, stack=es_al)
            rcs = k.sb(f"rcs{l}", (128, 4 * 512), BF16, stack=es_al)
            wmk = k.sb(f"wmk{l}", (128, 640), stack=es_al)
            czq = k.sb(f"czq{l}", (128, 5 * 128), BF16, stack=es_al)
            nsb = k.sb(f"nsb{l}", (2, 512), stack=es_al)
            ptb = k.sb(f"ptb{l}", (128, 640), BF16, stack=es_al)
            obuf = k.sb(f"obuf{l}", (65, 1024), BF16, stack=es_al)
            lrow = k.sb(f"lrow{l}", (1, 1024), stack=es_al)
            negma = k.sb(f"negma{l}", (128, 8), stack=es_al)
            nrow8 = k.sb(f"nrow8{l}", (8, 128), BF16, stack=es_al)
            o_fm = k.sb(f"o_fm{l}", (128, 4 * 512), BF16, stack=es_al)
            wo_sb = k.sb(f"wo_sb{l}", (128, 4 * 1024), BF16, stack=es_al)
            k.V.inc(v.memset(kat[:], 0.0))
            k.V.inc(v.memset(kat[64:65, :], 1.0))
            k.V.inc(v.memset(vat[:], 0.0))
            k.V.inc(v.memset(qat[:], 0.0))
            ghf, gnf = k.gate(g, k.T, k.V, k.S)
            for i in range(8):
                dd = g.dma_start(out=qkw[:, 2560 * i:2560 * i + 1536],
                                 in_=wqkv_i[l][128 * i:128 * (i + 1), :])
                dd._wait_ge(ghf, gnf)
                k.LDG.inc(dd, 16)
                dd = g.dma_start(out=qkw[:, 2560 * i + 1536:2560 * (i + 1)],
                                 in_=wqkvp_i[l][128 * i:128 * (i + 1), :])
                dd._wait_ge(ghf, gnf)
                k.LDG.inc(dd, 16)
            for i in range(4):
                dd = g.dma_start(out=wo_sb[:, 1024 * i:1024 * (i + 1)],
                                 in_=wo_i[l][128 * i:128 * (i + 1), :])
                dd._wait_ge(ghf, gnf)
                k.LDG.inc(dd, 16)
            k.w(y, k.V, k.S, k.T)
            k.LDS.inc(y.dma_start(out=qscols[:], in_=qscol_i[l]), 16)
            k.LDS.inc(y.dma_start(out=lgr[0:1, 0:1024], in_=lngr_i[l]), 16)
            k.LDS.inc(y.dma_start(out=lgr[0:1, 1024:2048], in_=lnbr_i[l]), 16)
            k.LDS.inc(y.dma_start(out=b1sb[:], in_=b1_i[l]), 16)
            k.LDS.inc(y.dma_start(out=b2sb[:], in_=b2_i[l]), 16)

            for half in range(2):
                for esi in range(3):
                    esl = 2 * half + esi
                    lt0 = 4 * esl - 8 * half
                    k.w(y, k.T, k.V, k.S, k.CC)
                    for i in range(8):
                        if l == 0 or esl > 0:
                            k.LDS.inc(y.dma_start(
                                out=xslf[:, 513 * i:513 * (i + 1)],
                                in_=xext_hbm[128 * i:128 * (i + 1),
                                             512 * esl:512 * esl + 513]), 16)
                        else:
                            k.LDS.inc(y.dma_start(
                                out=xslf[:, 513 * i:513 * (i + 1)],
                                in_=cc1_out[0, 128 * i:128 * (i + 1), :]), 16)
                    if esl == 0:
                        k.w(v, k.LDS)
                        for i in range(8):
                            k.V.inc(v.tensor_scalar_mul(
                                xslf[:, 513 * i:513 * (i + 1)],
                                xslf[:, 513 * i:513 * (i + 1)], gsc[:]))
                    elif esl == 1:
                        k.w(v, k.LDS)
                        for i in range(8):
                            k.V.inc(v.tensor_scalar_mul(
                                xslf[:, 513 * i:513 * i + 1],
                                xslf[:, 513 * i:513 * i + 1], gsc[:]))
                    k.w(y, k.V, k.S)
                    k.LDS.inc(y.dma_start(out=rcs[:, 1024:1536],
                                          in_=rck_t[0, :, 512 * esl:512 * (esl + 1)]), 16)
                    k.LDS.inc(y.dma_start(out=rcs[:, 1536:2048],
                                          in_=rck_t[1, :, 512 * esl:512 * (esl + 1)]), 16)
                    build_q = (esl >= 1) and (esl - 1 in (2 * half, 2 * half + 1))
                    osl = esl - 1
                    if build_q:
                        k.LDS.inc(y.dma_start(out=rcs[:, 0:512],
                                              in_=rcq_t[0, :, 512 * osl:512 * (osl + 1)]), 16)
                        k.LDS.inc(y.dma_start(out=rcs[:, 512:1024],
                                              in_=rcq_t[1, :, 512 * osl:512 * (osl + 1)]), 16)
                    k.w(v, k.LDS)
                    k.V.inc(v.tensor_scalar(rcs[:, 1024:1536], rcs[:, 1024:1536],
                                            qscols[:, 2:3], None, ALU.mult))
                    k.V.inc(v.tensor_scalar(rcs[:, 1536:2048], rcs[:, 1536:2048],
                                            qscols[:, 3:4], None, ALU.mult))
                    if build_q:
                        k.V.inc(v.tensor_scalar(rcs[:, 0:512], rcs[:, 0:512],
                                                qscols[:, 0:1], 8.0, ALU.mult, ALU.mult))
                        k.V.inc(v.tensor_scalar(rcs[:, 512:1024], rcs[:, 512:1024],
                                                qscols[:, 1:2], 8.0, ALU.mult, ALU.mult))
                    ps_b = ExitStack()
                    _pn = f"{l}_{half}_{esi}"
                    pA = k.es_ps(ps_b, "pA" + _pn, (128, 512))
                    pB = k.es_ps(ps_b, "pB" + _pn, (128, 512))
                    pC = k.es_ps(ps_b, "pC" + _pn, (2, 512))
                    pD = k.es_ps(ps_b, "pD" + _pn, (128, 512))
                    pE = k.es_ps(ps_b, "pE" + _pn, (128, 512))
                    for side in range(2):
                        if side == 1 and not build_q:
                            continue
                        wof = 512 if side == 0 else 0
                        tabo = 1024 if side == 0 else 0
                        for ft in range(4):
                            k.w(t, k.V, k.S, k.LDS, k.LDG)
                            for i in range(8):
                                off = 1 if i < 4 else 0
                                mm = t.matmul(pA[:, 0:512],
                                              qkw[:, 2560 * i + wof + 128 * ft:
                                                  2560 * i + wof + 128 * (ft + 1)],
                                              xslf[:, 513 * i + off:513 * i + off + 512],
                                              start=(i == 0), stop=(i == 7))
                            k.T.inc(mm)
                            for i in range(8):
                                off = 1 if i < 4 else 0
                                mm = t.matmul(pB[:, 0:512],
                                              qkw[:, 2560 * i + 1536 + wof + 128 * ft:
                                                  2560 * i + 1536 + wof + 128 * (ft + 1)],
                                              xslf[:, 513 * i + off:513 * i + off + 512],
                                              start=(i == 0), stop=(i == 7))
                            k.T.inc(mm)
                            k.w(s, k.T)
                            k.S.inc(s.activation(sqn[:, 0:512], pA[:, 0:512], AF.Square))
                            k.w(t, k.S)
                            k.T.inc(t.matmul(pC[:], hd2[:], sqn[:, 0:512],
                                             start=True, stop=True))
                            k.w(s, k.T)
                            k.S.inc(s.copy(nsb[:], pC[:]))
                            k.w(v, k.S)
                            k.V.inc(v.tensor_scalar_max(nsb[:], nsb[:], 1e-24))
                            k.w(s, k.V)
                            k.S.inc(s.activation(nsb[:], nsb[:], AF.Ln))
                            k.S.inc(s.activation(nsb[:], nsb[:], AF.Exp, scale=-0.5))
                            k.w(t, k.S)
                            k.T.inc(t.matmul(pD[:, 0:512], p2h[:], nsb[:],
                                             start=True, stop=True))
                            k.w(v, k.T)
                            v.tensor_mul(escr[:, 0:512], pA[:, 0:512],
                                         rcs[:, tabo:tabo + 512])
                            v.tensor_mul(escr[:, 512:1024], pB[:, 0:512],
                                         rcs[:, tabo + 512:tabo + 1024])
                            v.tensor_add(escr[:, 0:512], escr[:, 0:512],
                                         escr[:, 512:1024])
                            k.V.inc(v.tensor_mul(escr[:, 0:512], escr[:, 0:512],
                                                 pD[:, 0:512]))
                            for hh in range(2):
                                h = 2 * ft + hh
                                if side == 0:
                                    dstt = kat[0:64, 1536 * h + 128 * lt0:
                                               1536 * h + 128 * lt0 + 512]
                                else:
                                    dstt = qat[0:64, 512 * h:512 * (h + 1)]
                                k.V.inc(v.tensor_copy(
                                    dstt, escr[64 * hh:64 * (hh + 1), 0:512]))
                    for tt_ in range(4):
                        k.w(t, k.V, k.S, k.LDS, k.LDG)
                        for i in range(8):
                            off = 1 if i < 4 else 0
                            mm = t.matmul(pE[:, 0:512],
                                          xslf[:, 513 * i + off + 128 * tt_:
                                              513 * i + off + 128 * (tt_ + 1)],
                                          qkw[:, 2560 * i + 1024:2560 * i + 1536],
                                          start=(i == 0), stop=(i == 7))
                        k.T.inc(mm)
                        k.w(v, k.T)
                        lt = lt0 + tt_
                        dstv = vat[:, 520 * lt:520 * (lt + 1)].rearrange(
                            "p (h c) -> p h c", c=65)[:, :, 0:64]
                        v.tensor_copy(dstv,
                                      pE[:, 0:512].rearrange("p (h c) -> p h c", c=64))
                        k.V.inc(v.memset(
                            vat[:, 520 * lt:520 * (lt + 1)].rearrange(
                                "p (h c) -> p h c", c=65)[:, :, 64:65], 1.0))

                    k.barrier()
                    ps_b.close()
                    if not build_q:
                        continue
                    psa = ExitStack()
                    pSIM = k.es_ps(psa, "pSIM" + _pn, (128, 2048))
                    pST = pSIM
                    pPV = k.es_ps(psa, "pPV" + _pn, (65, 128))
                    pBC = k.es_ps(psa, "pBC" + _pn, (64, 1024))
                    for qt in range(4 * osl, 4 * osl + 4):
                        qlo = 128 * (qt % 4)
                        blo = 128 * (qt - 8 * half)
                        k.w(y, k.V, k.S, k.T)
                        k.LDS.inc(y.dma_start(out=wmk[:], in_=wmask_i[qt]), 16)
                        for u_ in range(5):
                            ci_ = CZMAP.get((qt, u_))
                            if ci_ is not None:
                                k.LDS.inc(y.dma_start(
                                    out=czq[:, 128 * u_:128 * (u_ + 1)],
                                    in_=cz_i[ci_]), 16)
                        for hp in range(4):
                            k.w(t, k.V, k.S, k.LDS)
                            for hh in range(2):
                                h = 2 * hp + hh
                                t.matmul(pSIM[:, 1024 * hh:1024 * hh + 512],
                                         qat[0:64, 512 * h + qlo:512 * h + qlo + 128],
                                         kat[0:64, 1536 * h + blo:1536 * h + blo + 512],
                                         start=True, stop=True)
                                mm2 = t.matmul(
                                    pSIM[:, 1024 * hh + 512:1024 * hh + 640],
                                    qat[0:64, 512 * h + qlo:512 * h + qlo + 128],
                                    kat[0:64, 1536 * h + blo + 512:1536 * h + blo + 640],
                                    start=True, stop=True)
                            k.T.inc(mm2)
                            k.w(v, k.T)
                            for hh in range(2):
                                v.tensor_add(pSIM[:, 1024 * hh:1024 * hh + 640],
                                             pSIM[:, 1024 * hh:1024 * hh + 640],
                                             wmk[:])
                            k.V.inc(v.tensor_reduce(
                                negma[:, 2 * hp:2 * hp + 2],
                                pSIM[:, 0:2048].rearrange(
                                    "p (a c) -> p a c", c=1024)[:, :, 0:640],
                                AX.X, ALU.max, negate=True))
                        k.w(t, k.V)
                        k.T.inc(t.transpose(pPV[0:8, 0:128], negma[:], eye_sb[:]))
                        k.w(s, k.T, k.LDS)
                        k.S.inc(s.copy(nrow8[:], pPV[0:8, 0:128]))
                        ghn, gnn = k.gate(y, k.S, k.T)
                        dd = y.dma_start(
                            out=qat[64:65, :].rearrange(
                                "p (h c) -> p h c", c=512)[:, :, qlo:qlo + 128],
                            in_=nrow8[:])
                        dd._wait_ge(ghn, gnn)
                        k.LDS.inc(dd, 16)
                        for h in range(8):
                            k.w(t, k.S, k.V, k.LDS)
                            for u in range(5):
                                mm = t.matmul(
                                    pST[:, 128 * u:128 * (u + 1)],
                                    kat[0:65, 1536 * h + blo + 128 * u:
                                        1536 * h + blo + 128 * (u + 1)],
                                    qat[0:65, 512 * h + qlo:512 * h + qlo + 128],
                                    start=(u in (0, 4)), stop=(u in (3, 4)))
                            k.T.inc(mm)
                            k.w(s, k.T)
                            k.S.inc(s.activation(ptb[:], pST[:, 0:640], AF.Exp))
                            k.w(v, k.S)
                            vv = None
                            for u in range(5):
                                ci = CZMAP.get((qt, u))
                                if ci is None:
                                    continue
                                vv = v.tensor_mul(ptb[:, 128 * u:128 * (u + 1)],
                                                  ptb[:, 128 * u:128 * (u + 1)],
                                                  czq[:, 128 * u:128 * (u + 1)])
                            k.V.inc(vv)
                            k.w(t, k.V)
                            for u in range(5):
                                lt = qt - 8 * half + u
                                mm = t.matmul(
                                    pPV[:, 0:128],
                                    vat[:, 520 * lt + 65 * h:520 * lt + 65 * (h + 1)],
                                    ptb[:, 128 * u:128 * (u + 1)],
                                    start=(u == 0), stop=(u == 4))
                            k.T.inc(mm)
                            k.w(s, k.T)
                            k.S.inc(s.copy(obuf[:, 128 * h:128 * (h + 1)], pPV[:]))
                        k.w(v, k.S)
                        k.V.inc(v.reciprocal(lrow[:], obuf[64:65, :]))
                        k.w(t, k.V)
                        k.T.inc(t.matmul(pBC[:, 0:512], onesrow[0:1, 0:64],
                                         lrow[0:1, 0:512], start=True, stop=True))
                        k.T.inc(t.matmul(pBC[:, 512:1024], onesrow[0:1, 0:64],
                                         lrow[0:1, 512:1024], start=True, stop=True))
                        k.w(v, k.T)
                        for par in range(2):
                            inap = obuf[0:64, :].rearrange(
                                "p (a b c) -> p a b c", b=2, c=128)[:, :, par, :]
                            lbap = pBC[0:64, :].rearrange(
                                "p (a b c) -> p a b c", b=2, c=128)[:, :, par, :]
                            oap = o_fm[64 * par:64 * (par + 1), :].rearrange(
                                "p (a c) -> p a c", c=512)[:, :, qlo:qlo + 128]
                            k.V.inc(v.tensor_mul(oap, inap, lbap))
                    if l == 0 and osl == 0:
                        dof = k.dbg_out("ofm0", (128, 4 * 512), BF16)
                        if dof is not None:
                            k.dump("ofm0", o_fm[:])
                        dqa = k.dbg_out("qat0", (65, 8 * 512), BF16)
                        if dqa is not None:
                            k.dump("qat0", qat[:])
                        dka = k.dbg_out("kat0", (65, 8 * 12 * 128), BF16)
                        if dka is not None:
                            k.dump("kat0", kat[:])
                        dva = k.dbg_out("vat0", (128, 12 * 520), BF16)
                        if dva is not None:
                            k.dump("vat0", vat[:])
                    psa.close()
                    ps_wo = ExitStack()
                    pWA = k.es_ps(ps_wo, "pWA" + _pn, (128, 512))
                    pWB = k.es_ps(ps_wo, "pWB" + _pn, (128, 512))
                    for m in range(8):
                        bank = pWA if m % 2 == 0 else pWB
                        k.w(t, k.V, k.S, k.LDG)
                        for i in range(4):
                            mm = t.matmul(
                                bank[:, 0:512],
                                wo_sb[:, 1024 * i + 128 * m:1024 * i + 128 * (m + 1)],
                                o_fm[:, 512 * i:512 * (i + 1)],
                                start=(i == 0), stop=(i == 3))
                        k.T.inc(mm)
                        k.w(v, k.T)
                        k.V.inc(v.tensor_add(
                            xi[:, 2049 * m + 1 + 512 * osl:2049 * m + 1 + 512 * (osl + 1)],
                            xi[:, 2049 * m + 1 + 512 * osl:2049 * m + 1 + 512 * (osl + 1)],
                            bank[:, 0:512]))
                    k.barrier()
                    ps_wo.close()

            datn = k.dbg_out(f"attn{l}", (1024, T))
            if datn is not None:
                for i in range(8):
                    k.w(k.y, k.V, k.S, k.T)
                    k.ST.inc(k.y.dma_start(out=datn[128 * i:128 * (i + 1), :],
                                           in_=xi[:, 2049 * i + 1:2049 * (i + 1)]), 16)
            k.w(y, k.V, k.S, k.T)
            for i in range(8):
                k.LDS.inc(y.dma_start(out=cc0_in[l][128 * i:128 * (i + 1), :],
                                      in_=xi[:, 2049 * i + 2048:2049 * i + 2049]), 16)
            k.w(g, k.LDS)
            k.CC.inc(g.collective_compute(
                "AllGather", ALU.bypass, [[0, 1], [2, 3], [4, 5], [6, 7]],
                [cc0_in[l].opt()], [cc0_out[l].opt()]))
            k.w(y, k.CC)
            for i in range(8):
                k.LDS.inc(y.dma_start(out=gcol[:, i:i + 1],
                                      in_=cc0_out[l][0, 128 * i:128 * (i + 1), :]), 16)
            k.w(v, k.LDS)
            k.V.inc(v.tensor_scalar_mul(gcol[:], gcol[:], gsc[:]))
            for i in range(8):
                k.V.inc(v.tensor_copy(xi[:, 2049 * i:2049 * i + 1], gcol[:, i:i + 1]))

            k.barrier()
            es_al.close()
            es_fl = ExitStack()
            pA = k.es_ps(es_fl, f"pFA{l}", (128, 512))
            pB = k.es_ps(es_fl, f"pFB{l}", (128, 512))
            pR1 = k.es_ps(es_fl, f"pR1{l}", (1, 512))
            pR2 = k.es_ps(es_fl, f"pR2{l}", (1, 512))
            xfb = k.sb(f"xfb{l}", (128, 8 * 2048), BF16, stack=es_fl)
            lnrow2 = k.sb(f"lnrow2{l}", (1, 4 * 512), stack=es_fl)
            rowA2 = k.sb(f"rowA2{l}", (1, 512), stack=es_fl)
            rowB2 = k.sb(f"rowB2{l}", (1, 512), stack=es_fl)
            onesr512 = k.sb(f"onesr512{l}", (1, 512), stack=es_fl)
            gat22 = k.sb(f"gat22{l}", (128, 22 * 512), BF16, stack=es_fl)
            w1t = k.sb(f"w1t{l}", (128, 2 * 1024), BF16, stack=es_fl)
            w2t = k.sb(f"w2t{l}", (128, 22 * 128), BF16, stack=es_fl)
            k.V.inc(v.memset(onesr512[:], 1.0))
            for sl in (1, 2, 3, 0):
                k.w(t, k.V, k.S)
                for i in range(8):
                    off = 1 if i < 4 else 0
                    mm = t.matmul(pR1[:], onescol_f[:],
                                  xi[:, 2049 * i + off + 512 * sl:
                                      2049 * i + off + 512 * sl + 512],
                                  start=(i == 0), stop=(i == 7))
                k.T.inc(mm)
                for i in range(8):
                    off = 1 if i < 4 else 0
                    k.w(s, k.T)
                    k.S.inc(s.activation(sqn[:, 0:512],
                                         xi[:, 2049 * i + off + 512 * sl:
                                             2049 * i + off + 512 * sl + 512],
                                         AF.Square))
                    k.w(t, k.S)
                    k.T.inc(t.matmul(pR2[:], onescol_f[:], sqn[:, 0:512],
                                     start=(i == 0), stop=(i == 7)))
                MU, M2, MSQ, VAR = (lnrow2[0:1, 512 * i_:512 * (i_ + 1)]
                                    for i_ in range(4))
                k.w(s, k.T)
                k.S.inc(s.activation(MU, pR1[:], AF.Copy, scale=1.0 / 1024))
                k.S.inc(s.activation(M2, pR2[:], AF.Copy, scale=1.0 / 1024))
                k.w(v, k.S)
                k.V.inc(v.tensor_mul(MSQ, MU, MU))
                k.V.inc(v.tensor_sub(VAR, M2, MSQ))
                k.w(s, k.V)
                k.S.inc(s.activation(MSQ, VAR, AF.Ln, bias=k.constf(1e-5, 1)))
                k.S.inc(s.activation(rowA2[:], MSQ, AF.Exp, scale=-0.5))
                k.w(v, k.S)
                k.V.inc(v.tensor_mul(rowB2[:], MU, rowA2[:]))
                k.V.inc(v.tensor_scalar_mul(rowB2[:], rowB2[:], -1.0))
                for i in range(8):
                    off = 1 if i < 4 else 0
                    k.w(t, k.V, k.S)
                    k.T.inc(t.matmul(pA[:, 0:512], lgr[0:1, 128 * i:128 * (i + 1)],
                                     rowA2[:], start=True, stop=True))
                    t.matmul(pB[:, 0:512], lgr[0:1, 128 * i:128 * (i + 1)],
                             rowB2[:], start=True, stop=False)
                    k.T.inc(t.matmul(pB[:, 0:512],
                                     lgr[0:1, 1024 + 128 * i:1024 + 128 * (i + 1)],
                                     onesr512[:], start=False, stop=True))
                    k.w(v, k.T)
                    v.tensor_mul(escr[:, 0:512],
                                 xi[:, 2049 * i + off + 512 * sl:
                                     2049 * i + off + 512 * sl + 512],
                                 pA[:, 0:512])
                    k.V.inc(v.tensor_add(
                        xfb[:, 2048 * i + 512 * sl:2048 * i + 512 * (sl + 1)],
                        escr[:, 0:512], pB[:, 0:512]))

            for m in range(22):
                ghw, gnw = k.gate(g, k.T)
                for sidx, soff in ((0, 0), (1, 2816)):
                    dd = g.dma_start(
                        out=w1t[:, 1024 * sidx:1024 * (sidx + 1)].rearrange(
                            "p (a c) -> p a c", c=128),
                        in_=w1_i[l][:, soff + 128 * m:soff + 128 * (m + 1)].rearrange(
                            "(a p) c -> p a c", p=128))
                    dd._wait_ge(ghw, gnw)
                    k.LDG.inc(dd, 16)
                for sl in range(4):
                    k.w(t, k.LDG, k.V, k.S)
                    for i in range(8):
                        mm = t.matmul(pA[:, 0:512], w1t[:, 128 * i:128 * (i + 1)],
                                      xfb[:, 2048 * i + 512 * sl:2048 * i + 512 * (sl + 1)],
                                      start=(i == 0), stop=(i == 7))
                    k.T.inc(mm)
                    for i in range(8):
                        mm = t.matmul(pB[:, 0:512],
                                      w1t[:, 1024 + 128 * i:1024 + 128 * (i + 1)],
                                      xfb[:, 2048 * i + 512 * sl:2048 * i + 512 * (sl + 1)],
                                      start=(i == 0), stop=(i == 7))
                    k.T.inc(mm)
                    k.w(s, k.T)
                    k.S.inc(s.activation(ggscr[:, 0:512], pB[:, 0:512], AF.Gelu,
                                         bias=b1sb[:, 22 + m:23 + m]))
                    k.w(v, k.S, k.LDS)
                    k.V.inc(v.scalar_tensor_tensor(
                        gtile[:, 512 * (sl % 2):512 * (sl % 2) + 512],
                        pA[:, 0:512], b1sb[:, m:m + 1],
                        ggscr[:, 0:512], ALU.add, ALU.mult))
                    k.w(y, k.V)
                    k.LDS.inc(y.dma_start(
                        out=gated_hbm[sl, m],
                        in_=gtile[:, 512 * (sl % 2):512 * (sl % 2) + 512]), 16)

            for m2 in range(8):
                ghw, gnw = k.gate(g, k.T)
                dd = g.dma_start(
                    out=w2t[:].rearrange("p (a c) -> p a c", c=128),
                    in_=w2_i[l][:, 128 * m2:128 * (m2 + 1)].rearrange(
                        "(a p) c -> p a c", p=128))
                dd._wait_ge(ghw, gnw)
                k.LDG.inc(dd, 16)
                for sl in range(4):
                    k.w(y, k.T, k.V)
                    k.LDS.inc(y.dma_start(
                        out=gat22[:].rearrange("p (a c) -> p a c", c=512),
                        in_=gated_hbm[sl].transpose([1, 0, 2])), 16)
                    k.w(t, k.LDS, k.LDG, k.V)
                    for kk2 in range(22):
                        mm = t.matmul(pA[:, 0:512], w2t[:, 128 * kk2:128 * (kk2 + 1)],
                                      gat22[:, 512 * kk2:512 * (kk2 + 1)],
                                      start=(kk2 == 0), stop=(kk2 == 21))
                    k.T.inc(mm)
                    k.w(v, k.T)
                    k.V.inc(v.scalar_tensor_tensor(
                        xi[:, 2049 * m2 + 1 + 512 * sl:2049 * m2 + 1 + 512 * (sl + 1)],
                        pA[:, 0:512], b2sb[:, m2:m2 + 1],
                        xi[:, 2049 * m2 + 1 + 512 * sl:2049 * m2 + 1 + 512 * (sl + 1)],
                        ALU.add, ALU.add))

            k.barrier()
            es_fl.close()
            if l == 0:
                for i in range(8):
                    k.w(s, k.V, k.T)
                    k.S.inc(s.copy(xslf[:, 513 * i:513 * (i + 1)],
                                   xi[:, 2049 * i + 1536:2049 * i + 2049]))
                k.w(y, k.S)
                for i in range(8):
                    k.LDS.inc(y.dma_start(out=cc1_in[128 * i:128 * (i + 1), :],
                                          in_=xslf[:, 513 * i:513 * (i + 1)]), 16)
                k.w(g, k.LDS)
                k.CC.inc(g.collective_compute(
                    "AllGather", ALU.bypass, [[0, 1], [2, 3], [4, 5], [6, 7]],
                    [cc1_in.opt()], [cc1_out.opt()]))
                for seg in range(4):
                    for i in range(8):
                        k.w(s, k.V, k.LDS)
                        k.S.inc(s.copy(xslf[:, 513 * i:513 * i + 512],
                                       xi[:, 2049 * i + 1 + 512 * seg:
                                           2049 * i + 1 + 512 * (seg + 1)]))
                    k.w(y, k.S)
                    for i in range(8):
                        k.LDS.inc(y.dma_start(
                            out=xext_hbm[128 * i:128 * (i + 1),
                                         513 + 512 * seg:513 + 512 * (seg + 1)],
                            in_=xslf[:, 513 * i:513 * i + 512]), 16)
            dffn = k.dbg_out("ffn1", (1024, T)) if l == 0 else k.dbg_out("ffn2", (1024, T))
            if dffn is not None:
                for i in range(8):
                    k.w(k.y, k.V, k.S, k.T)
                    k.ST.inc(k.y.dma_start(out=dffn[128 * i:128 * (i + 1), :],
                                           in_=xi[:, 2049 * i + 1:2049 * (i + 1)]), 16)
        xn1_hbm = nc.dram_tensor("xn1_hbm", [1024, T], F32, kind="Internal").ap()
        z_hbm = nc.dram_tensor("z_hbm", [1024, K], BF16, kind="Internal").ap()
        xc_hbm = nc.dram_tensor("xc_hbm", [1024, K], BF16, kind="Internal").ap()
        h_hbm = nc.dram_tensor("h_hbm", [4096, T], F32, kind="Internal").ap()
        cc2_in = nc.dram_tensor("cc2i", [1024, T], BF16, kind="Internal").ap()
        cc2_out = nc.dram_tensor("cc2o", [2, 1024, T], BF16, kind="Internal").ap()
        cc3_in = nc.dram_tensor("cc3i", [1, 4], F32, kind="Internal").ap()
        cc3_out = nc.dram_tensor("cc3o", [1, 4], F32, kind="Internal").ap()
        cc4_in = nc.dram_tensor("cc4i", [1024, K], F32, kind="Internal").ap()
        cc4_out = nc.dram_tensor("cc4o", [1024, K], F32, kind="Internal").ap()
        cc5_in = nc.dram_tensor("cc5i", [96, K], F32, kind="Internal").ap()
        cc5_out = nc.dram_tensor("cc5o", [96, K], F32, kind="Internal").ap()
        cc6_in = nc.dram_tensor("cc6i", [1, 2], F32, kind="Internal").ap()
        cc6_out = nc.dram_tensor("cc6o", [1, 2], F32, kind="Internal").ap()

        def ldb(pairs, *sems):
            gh_, gn_ = k.gate(y, k.LDS, *sems)
            for dst_, src_ in pairs:
                dd_ = y.dma_start(out=dst_, in_=src_)
                dd_._wait_ge(gh_, gn_)
                k.LDS.inc(dd_, 16)

        k.barrier()
        es_4 = ExitStack()
        x1sl = k.sb("x1sl", (128, 8 * 512), stack=es_4)
        xnsl = k.sb("xnsl", (128, 512), stack=es_4)
        xnbf = k.sb("xnbf", (128, 512), BF16, stack=es_4)
        ssb4 = k.sb("ssb4", (128, 512), BF16, stack=es_4)
        xsb4 = k.sb("xsb4", (128, 512), stack=es_4)
        lnr4 = k.sb("lnr4", (1, 4 * 512), stack=es_4)
        rA4 = k.sb("rA4", (1, 512), stack=es_4)
        rB4 = k.sb("rB4", (1, 512), stack=es_4)
        wstg = k.sb("wstg", (128, 4096), stack=es_4)
        ps4 = ExitStack()
        pR1 = k.es_ps(ps4, "pR14", (1, 512))
        pR2 = k.es_ps(ps4, "pR24", (1, 512))
        pA4 = k.es_ps(ps4, "pA44", (128, 512))
        pB4 = k.es_ps(ps4, "pB44", (128, 512))
        k.V.inc(v.memset(mxrow[:], -1e30))
        for sl in range(4):
            for i in range(8):
                ldb([(xsb4[:], xT_keys[128 * i:128 * (i + 1),
                                       2048 + 512 * sl:2048 + 512 * (sl + 1)]),
                     (ssb4[:], s_hbm[128 * i:128 * (i + 1),
                                     512 * sl:512 * (sl + 1)])],
                    k.V, k.S)
                k.w(s, k.V)
                k.S.inc(s.activation(escr[:, 0:512],
                                     xi[:, 2049 * i + 1 + 512 * sl:
                                         2049 * i + 1 + 512 * (sl + 1)], AF.Silu))
                k.w(v, k.S, k.LDS)
                v.tensor_mul(escr[:, 512:1024], ssb4[:], escr[:, 0:512])
                k.V.inc(v.tensor_add(x1sl[:, 512 * i:512 * (i + 1)],
                                     xsb4[:], escr[:, 512:1024]))
                k.w(t, k.V, k.S)
                k.T.inc(t.matmul(pR1[:], onescol_f[:],
                                 x1sl[:, 512 * i:512 * (i + 1)],
                                 start=(i == 0), stop=(i == 7)))
                k.w(s, k.T)
                k.S.inc(s.activation(sqn[:, 0:512],
                                     x1sl[:, 512 * i:512 * (i + 1)], AF.Square))
                k.w(t, k.S)
                k.T.inc(t.matmul(pR2[:], onescol_f[:], sqn[:, 0:512],
                                 start=(i == 0), stop=(i == 7)))
            MU, M2, MSQ, VAR = (lnr4[0:1, 512 * i_:512 * (i_ + 1)]
                                for i_ in range(4))
            k.w(s, k.T)
            k.S.inc(s.activation(MU, pR1[:], AF.Copy, scale=1.0 / 1024))
            k.S.inc(s.activation(M2, pR2[:], AF.Copy, scale=1.0 / 1024))
            k.w(v, k.S)
            k.V.inc(v.tensor_mul(MSQ, MU, MU))
            k.V.inc(v.tensor_sub(VAR, M2, MSQ))
            k.w(s, k.V)
            k.S.inc(s.activation(MSQ, VAR, AF.Ln, bias=k.constf(1e-5, 1)))
            k.S.inc(s.activation(rA4[:], MSQ, AF.Exp, scale=-0.5))
            k.w(v, k.S)
            k.V.inc(v.tensor_mul(rB4[:], MU, rA4[:]))
            k.V.inc(v.tensor_scalar_mul(rB4[:], rB4[:], -1.0))
            k.w(t, k.V)
            k.T.inc(t.matmul(pA4[:], onesrow[:], rA4[:], start=True, stop=True))
            k.T.inc(t.matmul(pB4[:], onesrow[:], rB4[:], start=True, stop=True))
            for i in range(8):
                k.w(v, k.T, k.LDS)
                v.tensor_mul(xnsl[:], x1sl[:, 512 * i:512 * (i + 1)], pA4[:])
                k.V.inc(v.tensor_add(xnsl[:], xnsl[:], pB4[:]))
                v.tensor_reduce(grow[:, 0:1], xnsl[:], AX.X, ALU.max)
                v.tensor_tensor(mxrow[:, 0:1], mxrow[:, 0:1], grow[:, 0:1],
                                ALU.max)
                v.tensor_reduce(grow[:, 1:2], xnsl[:], AX.X, ALU.min, negate=True)
                k.V.inc(v.tensor_tensor(mxrow[:, 1:2], mxrow[:, 1:2],
                                        grow[:, 1:2], ALU.max))
                k.w(s, k.V, k.LDS)
                k.S.inc(s.copy(xnbf[:], xnsl[:]))
                ghs, gns = k.gate(y, k.V, k.S)
                for dst, src in (
                        (x1_hbm[128 * i:128 * (i + 1), 512 * sl:512 * (sl + 1)],
                         x1sl[:, 512 * i:512 * (i + 1)]),
                        (xn1_hbm[128 * i:128 * (i + 1), 512 * sl:512 * (sl + 1)],
                         xnsl[:]),
                        (cc2_in[128 * i:128 * (i + 1), 512 * sl:512 * (sl + 1)],
                         xnbf[:])):
                    dd = y.dma_start(out=dst, in_=src)
                    dd._wait_ge(ghs, gns)
                    k.LDS.inc(dd, 16)
        dx1 = k.dbg_out("x1", (1024, T))
        if dx1 is not None:
            ghd, gnd = k.gate(y, k.LDS, k.V, k.S)
            dd = y.dma_start(out=dx1, in_=x1_hbm)
            dd._wait_ge(ghd, gnd)
            k.ST.inc(dd, 16)
        dxn1 = k.dbg_out("xn1", (1024, T))
        if dxn1 is not None:
            ghd, gnd = k.gate(y, k.LDS, k.V, k.S)
            dd = y.dma_start(out=dxn1, in_=xn1_hbm)
            dd._wait_ge(ghd, gnd)
            k.ST.inc(dd, 16)
        if _STAGES < 5:
            ps4.close()
            es_4.close()
            k.barrier()
            es_f.close()
            finalize(k, out_fm)
            return k

        k.w(t, k.V)
        k.T.inc(t.transpose(pA4[0:2, 0:128], mxrow[:], eye_sb[:]))
        k.w(v, k.T)
        k.V.inc(v.tensor_reduce(grow[0:2, 0:1], pA4[0:2, 0:128], AX.X, ALU.max))
        ldb([(wstg[:], w1slab_i)], k.V, k.S)
        k.w(v, k.LDS)
        k.V.inc(v.tensor_reduce(grow[:, 2:3], wstg[:], AX.X, ALU.max,
                                apply_absolute_value=True))
        ldb([(wstg[:], w2slab_i.rearrange("(a p) c -> p a c", p=128))], k.V)
        k.w(v, k.LDS)
        k.V.inc(v.tensor_reduce(grow[:, 3:4], wstg[:], AX.X, ALU.max,
                                apply_absolute_value=True))
        k.w(t, k.V)
        k.T.inc(t.transpose(pB4[0:2, 0:128], grow[:, 2:4], eye_sb[:]))
        k.w(v, k.T)
        k.V.inc(v.tensor_reduce(grow[0:2, 1:2], pB4[0:2, 0:128], AX.X, ALU.max))
        ghq, gnq = k.gate(y, k.V)
        for dst, src in ((qsrow[0:1, 0:1], grow[0:1, 1:2]),
                         (qsrow[0:1, 1:2], grow[1:2, 1:2]),
                         (qsrow[0:1, 2:3], grow[0:1, 0:1]),
                         (qsrow[0:1, 3:4], grow[1:2, 0:1])):
            dd = y.dma_start(out=dst, in_=src)
            dd._wait_ge(ghq, gnq)
            k.LDS.inc(dd, 16)
        ldb([(cc3_in, qsrow[0:1, 0:4])], k.V)
        k.w(g, k.LDS)
        k.CC.inc(g.collective_compute(
            "AllGather", ALU.bypass, [[0, 1], [2, 3], [4, 5], [6, 7]],
            [cc2_in.opt()], [cc2_out.opt()]))
        k.CC.inc(g.collective_compute(
            "AllReduce", ALU.max, [[0, 1, 2, 3, 4, 5, 6, 7]],
            [cc3_in.opt()], [cc3_out.opt()]))
        k.barrier()
        ps4.close()
        es_4.close()
        es_f.close()

        ldb([(scr[0:1, 0:4], cc3_out)], k.CC)
        k.w(v, k.LDS)
        v.tensor_scalar(scr[0:1, 4:5], scr[0:1, 0:1], 1.0 / 127, 1e-8,
                        ALU.mult, ALU.max)
        v.tensor_scalar(scr[0:1, 5:6], scr[0:1, 1:2], 1.0 / 127, 1e-8,
                        ALU.mult, ALU.max)
        v.reciprocal(scr[0:1, 6:7], scr[0:1, 4:5])
        v.reciprocal(scr[0:1, 7:8], scr[0:1, 5:6])
        v.tensor_tensor(scr[0:1, 8:9], scr[0:1, 2:3], scr[0:1, 3:4], ALU.add)
        v.tensor_scalar(scr[0:1, 8:9], scr[0:1, 8:9], 1.0 / 255, 1e-8,
                        ALU.mult, ALU.max)
        v.reciprocal(scr[0:1, 9:10], scr[0:1, 8:9])
        v.tensor_tensor(scr[0:1, 10:11], scr[0:1, 8:9], scr[0:1, 4:5], ALU.mult)
        k.V.inc(v.tensor_tensor(scr[0:1, 11:12], scr[0:1, 3:4], scr[0:1, 9:10],
                                ALU.mult))
        k.w(s, k.V)
        k.S.inc(s.activation(scr[0:1, 11:12], scr[0:1, 11:12], AF.Copy,
                             bias=12582912.0))
        k.w(v, k.S)
        v.tensor_scalar(scr[0:1, 11:12], scr[0:1, 11:12], 12582912.0, None,
                        ALU.subtract)
        v.tensor_scalar(scr[0:1, 12:13], scr[0:1, 11:12], -128.0, -128.0,
                        ALU.add, ALU.max)
        v.tensor_scalar(scr[0:1, 12:13], scr[0:1, 12:13], 127.0, None, ALU.min)
        v.tensor_scalar(scr[0:1, 13:14], scr[0:1, 12:13], -1.0, -128.0,
                        ALU.mult, ALU.add)
        k.V.inc(v.tensor_scalar(scr[0:1, 14:15], scr[0:1, 12:13], -1.0, 127.0,
                                ALU.mult, ALU.add))
        ghp, gnp = k.gate(y, k.V)
        for j_, sc_ in enumerate((9, 13, 14, 10, 6, 7)):
            dd = y.dma_start(out=qsrow[0:1, j_:j_ + 1], in_=scr[0:1, sc_:sc_ + 1])
            dd._wait_ge(ghp, gnp)
            k.LDS.inc(dd, 16)
        ps5q = ExitStack()
        pQ5 = k.es_ps(ps5q, "pQ5", (128, 8))
        k.w(t, k.LDS, k.S)
        k.T.inc(t.matmul(pQ5[:], onesrow[:], qsrow[:], start=True, stop=True))
        k.w(s, k.T)
        k.S.inc(s.copy(qpar[:], pQ5[:]))
        k.barrier()
        ps5q.close()

        es5x = ExitStack()
        xcdt = k.sb("xcdt", (128, 8 * K), BF16, stack=es5x)
        opj_sb = k.sb("opj_sb", (128, 8 * 1024), BF16, stack=es5x)
        lnw_c = k.sb("lnw_c", (128, 8), stack=es5x)
        lnb_c = k.sb("lnb_c", (128, 8), stack=es5x)
        dcol_c = k.sb("dcol_c", (128, 8), stack=es5x)
        cb_c = k.sb("cb_c", (128, 8), stack=es5x)
        dtb_c = k.sb("dtb_c", (128, 8), stack=es5x)
        cw_c = k.sb("cw_c", (128, 32), stack=es5x)
        A_sb = k.sb("A_sb", (128, 128), stack=es5x)
        carry_all = k.sb("carry_all", (128, 128), stack=es5x)
        eyeb_sb = k.sb("eyeb_sb", (128, 128), BF16, stack=es5x)
        ldb([(lnw_c[:], lnw_i), (lnb_c[:], lnb_i), (dcol_c[:], dcol_i),
             (cb_c[:], cbq_i), (dtb_c[:], dtb_i), (cw_c[:], cw_i),
             (A_sb[:], alog_i), (eyeb_sb[:], eyeb_i), (g5_sb[:], g5_i)],
            k.V, k.S, k.T)
        k.w(s, k.LDS)
        k.S.inc(s.activation(A_sb[:], A_sb[:], AF.Exp))
        k.w(v, k.S)
        k.V.inc(v.tensor_scalar_mul(A_sb[:], A_sb[:], -1.0))
        es5g = ExitStack()
        xg = k.sb("xg", (128, 8 * K), BF16, stack=es5g)
        ghg, gng = k.gate(y, k.CC, k.V, k.S, k.T)
        for i in range(8):
            for pp in range(2):
                dd = y.dma_start(
                    out=xg[:, K * i + T * pp:K * i + T * (pp + 1)],
                    in_=cc2_out[pp, 128 * i:128 * (i + 1), :])
                dd._wait_ge(ghg, gng)
                k.LDS.inc(dd, 16)
        es5g1 = ExitStack()
        sq5 = k.sb("sq5", (128, 512), stack=es5g1)
        lnr5 = k.sb("lnr5", (1, 4 * 512), stack=es5g1)
        rA5 = k.sb("rA5", (1, 512), stack=es5g1)
        rB5 = k.sb("rB5", (1, 512), stack=es5g1)
        xesc = k.sb("xesc", (128, 512), stack=es5g1)
        ps5 = ExitStack()
        pR1b = k.es_ps(ps5, "pR1b", (1, 512))
        pR2b = k.es_ps(ps5, "pR2b", (1, 512))
        pA5 = k.es_ps(ps5, "pA5", (128, 512))
        pB5 = k.es_ps(ps5, "pB5", (128, 512))
        for ch in range(8):
            k.w(t, k.V, k.S, k.LDS)
            for i in range(8):
                mm = t.matmul(pR1b[:], onescol_b[:],
                              xg[:, K * i + 512 * ch:K * i + 512 * (ch + 1)],
                              start=(i == 0), stop=(i == 7))
            k.T.inc(mm)
            for i in range(8):
                k.w(s, k.T)
                k.S.inc(s.activation(sq5[:],
                                     xg[:, K * i + 512 * ch:K * i + 512 * (ch + 1)],
                                     AF.Square))
                k.w(t, k.S)
                k.T.inc(t.matmul(pR2b[:], onescol_f[:], sq5[:],
                                 start=(i == 0), stop=(i == 7)))
            MU, M2, MSQ, VAR = (lnr5[0:1, 512 * i_:512 * (i_ + 1)]
                                for i_ in range(4))
            k.w(s, k.T)
            k.S.inc(s.activation(MU, pR1b[:], AF.Copy, scale=1.0 / 1024))
            k.S.inc(s.activation(M2, pR2b[:], AF.Copy, scale=1.0 / 1024))
            k.w(v, k.S)
            k.V.inc(v.tensor_mul(MSQ, MU, MU))
            k.V.inc(v.tensor_sub(VAR, M2, MSQ))
            k.w(s, k.V)
            k.S.inc(s.activation(MSQ, VAR, AF.Ln, bias=k.constf(1e-5, 1)))
            k.S.inc(s.activation(rA5[:], MSQ, AF.Exp, scale=-0.5))
            k.w(v, k.S)
            k.V.inc(v.tensor_mul(rB5[:], MU, rA5[:]))
            k.V.inc(v.tensor_scalar_mul(rB5[:], rB5[:], -1.0))
            k.w(t, k.V)
            k.T.inc(t.matmul(pA5[:], onesrow[:], rA5[:], start=True, stop=True))
            k.T.inc(t.matmul(pB5[:], onesrow[:], rB5[:], start=True, stop=True))
            for i in range(8):
                k.w(v, k.T)
                v.tensor_mul(xesc[:],
                             xg[:, K * i + 512 * ch:K * i + 512 * (ch + 1)],
                             pA5[:])
                v.tensor_add(xesc[:], xesc[:], pB5[:])
                k.V.inc(v.tensor_scalar(
                    xg[:, K * i + 512 * ch:K * i + 512 * (ch + 1)], xesc[:],
                    lnw_c[:, i:i + 1], lnb_c[:, i:i + 1], ALU.mult, ALU.add))
        k.barrier()
        ps5.close()
        es5g1.close()
        dxln = k.dbg_out("xln", (1024, K), BF16)
        if dxln is not None:
            for i in range(8):
                k.dump("xln", xg[:, K * i:K * (i + 1)],
                       col=(128 * i, 128 * (i + 1)))

        es5g2 = ExitStack()
        ipt = [k.sb(f"ipt{p_}", (128, 8 * 128), BF16, stack=es5g2)
               for p_ in range(2)]
        xitmp = k.sb("xitmp", (128, 3 + K), stack=es5g2)
        cvt = k.sb("cvt", (128, K), stack=es5g2)
        szt = k.sb("szt", (128, 512), BF16, stack=es5g2)
        ps5i = ExitStack()
        pIP = [k.es_ps(ps5i, f"pIP{p_}", (128, 512)) for p_ in range(2)]
        k.V.inc(v.memset(xitmp[:, 0:3], 0.0))
        for co in range(16):
            pp = co % 2
            ghw, gnw = k.gate(g, k.T)
            for i in range(8):
                dd = g.dma_start(out=ipt[pp][:, 128 * i:128 * (i + 1)],
                                 in_=inpj_i[128 * i:128 * (i + 1),
                                            128 * co:128 * (co + 1)])
                dd._wait_ge(ghw, gnw)
                k.LDG.inc(dd, 16)
            for ch in range(8):
                pI_ = pIP[ch % 2]
                k.w(t, k.LDG, k.S)
                for i in range(8):
                    mm = t.matmul(pI_[:], ipt[pp][:, 128 * i:128 * (i + 1)],
                                  xg[:, K * i + 512 * ch:K * i + 512 * (ch + 1)],
                                  start=(i == 0), stop=(i == 7))
                k.T.inc(mm)
                if co < 8:
                    k.w(s, k.T, k.V)
                    k.S.inc(s.copy(xitmp[:, 3 + 512 * ch:3 + 512 * (ch + 1)],
                                   pI_[:]))
                else:
                    k.w(s, k.T, k.LDS)
                    k.S.inc(s.activation(szt[:], pI_[:], AF.Silu))
                    ghz, gnz = k.gate(y, k.S)
                    dd = y.dma_start(
                        out=z_hbm[128 * (co - 8):128 * (co - 7),
                                  512 * ch:512 * (ch + 1)], in_=szt[:])
                    dd._wait_ge(ghz, gnz)
                    k.LDS.inc(dd, 16)
            if co < 8:
                k.w(v, k.S, k.LDS)
                v.tensor_scalar(cvt[:], xitmp[:, 3:3 + K],
                                cw_c[:, 4 * co + 3:4 * co + 4], None, ALU.mult)
                for s_ in (1, 2, 3):
                    vv = v.scalar_tensor_tensor(
                        cvt[:], xitmp[:, 3 - s_:3 - s_ + K],
                        cw_c[:, 4 * co + 3 - s_:4 * co + 4 - s_], cvt[:],
                        ALU.mult, ALU.add)
                k.V.inc(vv)
                k.w(s, k.V)
                k.S.inc(s.activation(xcdt[:, K * co:K * (co + 1)], cvt[:],
                                     AF.Silu, bias=cb_c[:, co:co + 1]))
                ghx5, gnx5 = k.gate(y, k.S)
                dd = y.dma_start(out=xc_hbm[128 * co:128 * (co + 1), :],
                                 in_=xcdt[:, K * co:K * (co + 1)])
                dd._wait_ge(ghx5, gnx5)
                k.LDS.inc(dd, 16)
        k.barrier()
        ps5i.close()
        es5g2.close()
        es5g.close()

        es5bc = ExitStack()
        bcbf = k.sb("bcbf", (32, K), BF16, stack=es5bc)
        esel_sb = k.sb("esel_sb", (32, 4096), BF16, stack=es5bc)
        ldb([(esel_sb[:], esel_i)], k.V, k.S, k.T)
        es5d = ExitStack()
        dbc_sb = k.sb("dbc_sb", (96, K), stack=es5d)
        dtr_bf = k.sb("dtr_bf", (64, K), BF16, stack=es5d)
        dbst = k.sb("dbst", (96, 512), stack=es5d)
        xpj_sb = k.sb("xpj_sb", (128, 8 * 96), BF16, stack=es5d)
        dtw_sb = k.sb("dtw_sb", (64, 8 * 128), BF16, stack=es5d)
        dtt1 = k.sb("dtt1", (128, 512), stack=es5d)
        dtt2 = k.sb("dtt2", (128, 512), stack=es5d)
        ghw, gnw = k.gate(g, k.T, k.V, k.S)
        for i in range(8):
            for dst, src in ((xpj_sb[:, 96 * i:96 * (i + 1)],
                              xpj_i[128 * i:128 * (i + 1), :]),
                             (opj_sb[:, 1024 * i:1024 * (i + 1)],
                              opj_i[128 * i:128 * (i + 1), :]),
                             (dtw_sb[:, 128 * i:128 * (i + 1)],
                              dtw_i[:, 128 * i:128 * (i + 1)])):
                dd = g.dma_start(out=dst, in_=src)
                dd._wait_ge(ghw, gnw)
                k.LDG.inc(dd, 16)
        ps5d = ExitStack()
        pDB = [k.es_ps(ps5d, f"pDB{p_}", (96, 512)) for p_ in range(2)]
        for ch in range(8):
            pD_ = pDB[ch % 2]
            k.w(t, k.S, k.LDG)
            for i in range(8):
                mm = t.matmul(pD_[:], xpj_sb[:, 96 * i:96 * (i + 1)],
                              xcdt[:, K * i + 512 * ch:K * i + 512 * (ch + 1)],
                              start=(i == 0), stop=(i == 7))
            k.T.inc(mm)
            k.w(s, k.T, k.LDS)
            k.S.inc(s.copy(dbst[:], pD_[:]))
            ghb, gnb = k.gate(y, k.S)
            dd = y.dma_start(out=cc5_in[:, 512 * ch:512 * (ch + 1)], in_=dbst[:])
            dd._wait_ge(ghb, gnb)
            k.LDS.inc(dd, 16)
        k.w(g, k.LDS)
        k.CC.inc(g.collective_compute(
            "AllReduce", ALU.add, [[0, 1], [2, 3], [4, 5], [6, 7]],
            [cc5_in.opt()], [cc5_out.opt()]))
        ldb([(dbc_sb[:], cc5_out)], k.CC)
        k.w(s, k.LDS)
        k.S.inc(s.copy(bcbf[:], dbc_sb[64:96, :]))
        k.S.inc(s.copy(dtr_bf[:], dbc_sb[0:64, :]))
        ddbc = k.dbg_out("dbc", (96, K))
        if ddbc is not None:
            k.dump("dbc", dbc_sb[:])
        dxc = k.dbg_out("xc", (1024, K), BF16)
        if dxc is not None:
            ghd, gnd = k.gate(y, k.LDS, k.V, k.S)
            dd = y.dma_start(out=dxc, in_=xc_hbm)
            dd._wait_ge(ghd, gnd)
            k.ST.inc(dd, 16)

        pDT = [k.es_ps(ps5d, f"pDT{p_}", (128, 512)) for p_ in range(2)]
        k.w(v, k.T, k.LDS)
        for c_ in range(8):
            for ch in range(8):
                pD_ = pDT[ch % 2]
                k.w(t, k.S, k.V)
                k.T.inc(t.matmul(pD_[:], dtw_sb[:, 128 * c_:128 * (c_ + 1)],
                                 dtr_bf[:, 512 * ch:512 * (ch + 1)],
                                 start=True, stop=True))
                k.w(s, k.T, k.V)
                s.activation(dtt1[:], pD_[:], AF.Relu, bias=dtb_c[:, c_:c_ + 1])
                s.activation(dtt2[:], pD_[:], AF.Abs, bias=dtb_c[:, c_:c_ + 1])
                s.activation(dtt2[:], dtt2[:], AF.Exp, scale=-1.0)
                k.S.inc(s.activation(dtt2[:], dtt2[:], AF.Ln,
                                     bias=k.constf(1.0, 128)))
                k.w(v, k.S)
                k.V.inc(v.tensor_add(
                    xcdt[:, K * c_ + 512 * ch:K * c_ + 512 * (ch + 1)],
                    dtt1[:], dtt2[:]))
        ddt = k.dbg_out("dt", (1024, K), BF16)
        if ddt is not None:
            for i in range(8):
                k.dump("dt", xcdt[:, K * i:K * (i + 1)],
                       col=(128 * i, 128 * (i + 1)))
        k.barrier()
        ps5d.close()
        es5d.close()
        if _STAGES < 6:
            finalize(k, out_fm)
            return k

        es5s = ExitStack()
        Bq = k.sb("Bq", (128, 16 * 1024), BF16, stack=es5s)
        Cq = k.sb("Cq", (128, 16 * 1024), BF16, stack=es5s)
        abuf = k.sb("abuf", (128, 1024), stack=es5s)
        bbuf = k.sb("bbuf", (128, 1024), stack=es5s)
        hbuf = k.sb("hbuf", (128, 1024), stack=es5s)
        hcb = k.sb("hcb", (128, 1024), BF16, stack=es5s)
        wqc = k.sb("wqc", (128, 1024), BF16, stack=es5s)
        xcrl = k.sb("xcrl", (128, 1024), BF16, stack=es5s)
        szrl = k.sb("szrl", (128, 1024), BF16, stack=es5s)
        yq = k.sb("yq", (128, 8 * 1024), BF16, stack=es5s)
        s1st = k.sb("s1st", (128, 512), stack=es5s)
        ps5s = ExitStack()
        pBC = k.es_ps(ps5s, "pBC", (128, 1024))
        pYS = k.es_ps(ps5s, "pYS", (128, 1024))
        pOP = [k.es_ps(ps5s, f"pOP{p_}", (128, 512)) for p_ in range(2)]
        k.V.inc(v.memset(carry_all[:], 0.0))
        for q in range(4):
            for n in range(16):
                k.w(t, k.S, k.V)
                for half_ in range(2):
                    mm = t.matmul(pBC[:, 512 * half_:512 * (half_ + 1)],
                                  esel_sb[:, 128 * n:128 * (n + 1)],
                                  bcbf[:, 1024 * q + 512 * half_:
                                       1024 * q + 512 * (half_ + 1)],
                                  start=True, stop=True)
                k.T.inc(mm)
                k.w(s, k.T)
                k.S.inc(s.copy(Bq[:, 1024 * n:1024 * (n + 1)], pBC[:]))
                k.w(t, k.S)
                for half_ in range(2):
                    mm = t.matmul(pBC[:, 512 * half_:512 * (half_ + 1)],
                                  esel_sb[:, 2048 + 128 * n:2048 + 128 * (n + 1)],
                                  bcbf[:, 1024 * q + 512 * half_:
                                       1024 * q + 512 * (half_ + 1)],
                                  start=True, stop=True)
                k.T.inc(mm)
                k.w(s, k.T)
                k.S.inc(s.copy(Cq[:, 1024 * n:1024 * (n + 1)], pBC[:]))
            for c_ in range(8):
                ldb([(xcrl[:], xc_hbm[128 * c_:128 * (c_ + 1),
                                      1024 * q:1024 * (q + 1)]),
                     (szrl[:], z_hbm[128 * c_:128 * (c_ + 1),
                                     1024 * q:1024 * (q + 1)])],
                    k.V, k.T)
                k.w(v, k.LDS, k.S)
                k.V.inc(v.tensor_mul(
                    wqc[:], xcdt[:, K * c_ + 1024 * q:K * c_ + 1024 * (q + 1)],
                    xcrl[:]))
                for n in range(16):
                    k.w(s, k.V)
                    k.S.inc(s.activation(
                        abuf[:], xcdt[:, K * c_ + 1024 * q:K * c_ + 1024 * (q + 1)],
                        AF.Exp, scale=A_sb[:, 16 * c_ + n:16 * c_ + n + 1]))
                    k.w(v, k.S, k.T)
                    v.tensor_mul(bbuf[:], wqc[:], Bq[:, 1024 * n:1024 * (n + 1)])
                    init = carry_all[:, 16 * c_ + n:16 * c_ + n + 1] if q else 0.0
                    v.tensor_tensor_scan(hbuf[:], abuf[:], bbuf[:], init,
                                         ALU.mult, ALU.add)
                    v.tensor_copy(carry_all[:, 16 * c_ + n:16 * c_ + n + 1],
                                  hbuf[:, 1023:1024])
                    k.V.inc(v.tensor_mul(hcb[:], hbuf[:],
                                         Cq[:, 1024 * n:1024 * (n + 1)]))
                    k.w(t, k.V)
                    for half_ in range(2):
                        mm = t.matmul(pYS[:, 512 * half_:512 * (half_ + 1)],
                                      eyeb_sb[:],
                                      hcb[:, 512 * half_:512 * (half_ + 1)],
                                      start=(n == 0), stop=(n == 15))
                    k.T.inc(mm)
                k.w(v, k.T, k.LDS)
                v.scalar_tensor_tensor(bbuf[:], xcrl[:], dcol_c[:, c_:c_ + 1],
                                       pYS[:], ALU.mult, ALU.add)
                k.V.inc(v.tensor_mul(yq[:, 1024 * c_:1024 * (c_ + 1)],
                                     bbuf[:], szrl[:]))
            for e_ in range(8):
                for half_ in range(2):
                    pO_ = pOP[half_]
                    k.w(t, k.V, k.S)
                    for c_ in range(8):
                        mm = t.matmul(
                            pO_[:],
                            opj_sb[:, 1024 * c_ + 128 * e_:
                                   1024 * c_ + 128 * (e_ + 1)],
                            yq[:, 1024 * c_ + 512 * half_:
                               1024 * c_ + 512 * (half_ + 1)],
                            start=(c_ == 0), stop=(c_ == 7))
                    k.T.inc(mm)
                    k.w(s, k.T, k.LDS)
                    k.S.inc(s.copy(s1st[:], pO_[:]))
                    gho, gno = k.gate(y, k.S)
                    dd = y.dma_start(
                        out=cc4_in[128 * e_:128 * (e_ + 1),
                                   1024 * q + 512 * half_:
                                   1024 * q + 512 * (half_ + 1)],
                        in_=s1st[:])
                    dd._wait_ge(gho, gno)
                    k.LDS.inc(dd, 16)
        k.w(g, k.LDS)
        k.CC.inc(g.collective_compute(
            "AllReduce", ALU.add, [[0, 1], [2, 3], [4, 5], [6, 7]],
            [cc4_in.opt()], [cc4_out.opt()]))
        k.barrier()
        ps5s.close()
        es5s.close()
        es5bc.close()
        es5x.close()
        ds1 = k.dbg_out("s1", (1024, T))
        if _STAGES < 7:
            if ds1 is not None:
                es_d7 = ExitStack()
                s1t_ = k.sb("s1t_", (128, 512), stack=es_d7)
                ra_ = k.sb("ra_", (128, 512), stack=es_d7)
                rb_ = k.sb("rb_", (128, 512), stack=es_d7)
                for e_ in range(8):
                    for ch in range(4):
                        ldb([(ra_[:], cc4_out[128 * e_:128 * (e_ + 1),
                                             512 * ch:512 * (ch + 1)]),
                             (rb_[:], cc4_out[128 * e_:128 * (e_ + 1),
                                             T + 512 * ch:T + 512 * (ch + 1)]),
                             (s1t_[:], xn1_hbm[128 * e_:128 * (e_ + 1),
                                               512 * ch:512 * (ch + 1)])],
                            k.V, k.S, k.CC)
                        k.w(v, k.LDS)
                        v.tensor_scalar(ra_[:], ra_[:], g5_sb[:, 1:2], None,
                                        ALU.mult)
                        v.scalar_tensor_tensor(ra_[:], rb_[:], g5_sb[:, 0:1],
                                               ra_[:], ALU.mult, ALU.add)
                        k.V.inc(v.tensor_add(s1t_[:], s1t_[:], ra_[:]))
                        k.w(y, k.V)
                        k.ST.inc(y.dma_start(
                            out=ds1[128 * e_:128 * (e_ + 1),
                                    512 * ch:512 * (ch + 1)],
                            in_=s1t_[:]), 16)
                k.barrier()
                es_d7.close()
            finalize(k, out_fm)
            return k

        es7 = ExitStack()
        wq1 = k.sb("wq1", (128, 8 * 4096), BF16, stack=es7)
        qz = k.sb("qz", (128, 8 * T), BF16, stack=es7)
        wstg7 = k.sb("wstg7", (128, 4096), stack=es7)
        xnrl = k.sb("xnrl", (128, T), stack=es7)
        qtmp = k.sb("qtmp", (128, T), stack=es7)
        h2t = k.sb("h2t", (128, 512), stack=es7)
        hmx = k.sb("hmx", (128, 2), stack=es7)
        b1q_sb = k.sb("b1q_sb", (128, 32), stack=es7)
        b2q_sb = k.sb("b2q_sb", (128, 8), stack=es7)
        ps7 = ExitStack()
        pH = [k.es_ps(ps7, f"pH{p_}", (128, 512)) for p_ in range(2)]
        ldb([(b1q_sb[:], b1q_i), (b2q_sb[:], b2q_i)], k.V, k.S, k.T)
        C128 = k.constf(12582912.0, 128)
        for i in range(8):
            ldb([(xnrl[:], xn1_hbm[128 * i:128 * (i + 1), :])], k.V, k.S)
            k.w(s, k.LDS, k.V)
            k.S.inc(s.activation(qtmp[:], xnrl[:], AF.Copy,
                                 scale=qpar[:, 0:1], bias=12582912.0))
            k.w(v, k.S)
            v.tensor_scalar(qtmp[:], qtmp[:], 12582912.0, qpar[:, 1:2],
                            ALU.subtract, ALU.max)
            k.V.inc(v.tensor_scalar(qz[:, T * i:T * (i + 1)], qtmp[:],
                                    qpar[:, 2:3], None, ALU.min))
        for ds_ in range(8):
            ldb([(wstg7[:], w1t_i[128 * ds_:128 * (ds_ + 1), :])], k.V, k.S)
            k.w(s, k.LDS, k.V)
            k.S.inc(s.activation(wstg7[:], wstg7[:], AF.Copy,
                                 scale=qpar[:, 4:5], bias=12582912.0))
            k.w(v, k.S)
            v.tensor_scalar(wstg7[:], wstg7[:], 12582912.0, -127.0,
                            ALU.subtract, ALU.max)
            k.V.inc(v.tensor_scalar(wq1[:, 4096 * ds_:4096 * (ds_ + 1)],
                                    wstg7[:], 127.0, None, ALU.min))
        k.V.inc(v.memset(hmx[:], -1e30))
        for m in range(32):
            for ch in range(4):
                pH_ = pH[ch % 2]
                k.w(t, k.V, k.S)
                for d_ in range(8):
                    mm = t.matmul(pH_[:],
                                  wq1[:, 4096 * d_ + 128 * m:
                                      4096 * d_ + 128 * (m + 1)],
                                  qz[:, T * d_ + 512 * ch:T * d_ + 512 * (ch + 1)],
                                  start=(d_ == 0), stop=(d_ == 7))
                k.T.inc(mm)
                k.w(s, k.T, k.LDS, k.V)
                k.S.inc(s.activation(h2t[:], pH_[:], AF.Gelu,
                                     scale=qpar[:, 3:4], bias=b1q_sb[:, m:m + 1]))
                k.w(v, k.S)
                v.tensor_reduce(grow[:, 0:1], h2t[:], AX.X, ALU.max)
                v.tensor_tensor(hmx[:, 0:1], hmx[:, 0:1], grow[:, 0:1], ALU.max)
                v.tensor_reduce(grow[:, 1:2], h2t[:], AX.X, ALU.min, negate=True)
                k.V.inc(v.tensor_tensor(hmx[:, 1:2], hmx[:, 1:2], grow[:, 1:2],
                                        ALU.max))
                ghh, gnh = k.gate(y, k.S)
                dd = y.dma_start(out=h_hbm[128 * m:128 * (m + 1),
                                           512 * ch:512 * (ch + 1)], in_=h2t[:])
                dd._wait_ge(ghh, gnh)
                k.LDS.inc(dd, 16)
        k.w(t, k.V)
        k.T.inc(t.transpose(pH[0][0:2, 0:128], hmx[:], eye_sb[:]))
        k.w(v, k.T)
        k.V.inc(v.tensor_reduce(grow[0:2, 0:1], pH[0][0:2, 0:128], AX.X, ALU.max))
        ghq, gnq = k.gate(y, k.V)
        for dst, src in ((q2row[0:1, 0:1], grow[0:1, 0:1]),
                         (q2row[0:1, 1:2], grow[1:2, 0:1])):
            dd = y.dma_start(out=dst, in_=src)
            dd._wait_ge(ghq, gnq)
            k.LDS.inc(dd, 16)
        ldb([(cc6_in, q2row[0:1, 0:2])], k.V)
        k.w(g, k.LDS)
        k.CC.inc(g.collective_compute(
            "AllReduce", ALU.max, [[0, 1, 2, 3, 4, 5, 6, 7]],
            [cc6_in.opt()], [cc6_out.opt()]))
        ldb([(scr[0:1, 2:4], cc6_out)], k.CC)
        k.w(v, k.LDS)
        v.tensor_tensor(scr[0:1, 8:9], scr[0:1, 2:3], scr[0:1, 3:4], ALU.add)
        v.tensor_scalar(scr[0:1, 8:9], scr[0:1, 8:9], 1.0 / 255, 1e-8,
                        ALU.mult, ALU.max)
        v.reciprocal(scr[0:1, 9:10], scr[0:1, 8:9])
        v.tensor_tensor(scr[0:1, 10:11], scr[0:1, 8:9], scr[0:1, 5:6], ALU.mult)
        k.V.inc(v.tensor_tensor(scr[0:1, 11:12], scr[0:1, 3:4], scr[0:1, 9:10],
                                ALU.mult))
        k.w(s, k.V)
        k.S.inc(s.activation(scr[0:1, 11:12], scr[0:1, 11:12], AF.Copy,
                             bias=12582912.0))
        k.w(v, k.S)
        v.tensor_scalar(scr[0:1, 11:12], scr[0:1, 11:12], 12582912.0, None,
                        ALU.subtract)
        v.tensor_scalar(scr[0:1, 12:13], scr[0:1, 11:12], -128.0, -128.0,
                        ALU.add, ALU.max)
        v.tensor_scalar(scr[0:1, 12:13], scr[0:1, 12:13], 127.0, None, ALU.min)
        v.tensor_scalar(scr[0:1, 13:14], scr[0:1, 12:13], -1.0, -128.0,
                        ALU.mult, ALU.add)
        k.V.inc(v.tensor_scalar(scr[0:1, 14:15], scr[0:1, 12:13], -1.0, 127.0,
                                ALU.mult, ALU.add))
        ghp, gnp = k.gate(y, k.V)
        for j_, sc_ in enumerate((9, 13, 14, 10)):
            dd = y.dma_start(out=q2row[0:1, j_:j_ + 1],
                             in_=scr[0:1, sc_:sc_ + 1])
            dd._wait_ge(ghp, gnp)
            k.LDS.inc(dd, 16)
        k.w(t, k.LDS, k.S)
        k.T.inc(t.matmul(pH[0][0:128, 0:4], onesrow[:], q2row[0:1, 0:4],
                         start=True, stop=True))
        k.w(s, k.T)
        k.S.inc(s.copy(qpar2[:], pH[0][0:128, 0:4]))
        k.barrier()
        ps7.close()
        es7.close()

        es8 = ExitStack()
        wq2 = k.sb("wq2", (128, 32 * 1024), BF16, stack=es8)
        qz2c = k.sb("qz2c", (128, 32 * 512), BF16, stack=es8)
        wstg8 = k.sb("wstg8", (128, 1024), stack=es8)
        hrl = k.sb("hrl", (128, 512), stack=es8)
        qt8 = k.sb("qt8", (128, 512), stack=es8)
        g1t = k.sb("g1t", (128, 512), stack=es8)
        s1t = k.sb("s1t", (128, 512), stack=es8)
        x1t = k.sb("x1t", (128, 512), stack=es8)
        rbt = k.sb("rbt", (128, 512), stack=es8)
        ot = k.sb("ot", (128, 512), stack=es8)
        ps8 = ExitStack()
        pG = [k.es_ps(ps8, f"pG{p_}", (128, 512)) for p_ in range(2)]
        C128b = k.constf(12582912.0, 128)
        for ms in range(32):
            ldb([(wstg8[:], w2t_i[128 * ms:128 * (ms + 1), :])], k.V, k.S)
            k.w(s, k.LDS, k.V)
            k.S.inc(s.activation(wstg8[:], wstg8[:], AF.Copy,
                                 scale=qpar[:, 5:6], bias=12582912.0))
            k.w(v, k.S)
            v.tensor_scalar(wstg8[:], wstg8[:], 12582912.0, -127.0,
                            ALU.subtract, ALU.max)
            k.V.inc(v.tensor_scalar(wq2[:, 1024 * ms:1024 * (ms + 1)],
                                    wstg8[:], 127.0, None, ALU.min))
        dg1 = k.dbg_out("g1", (1024, T))
        for ch in range(4):
            for m in range(32):
                ldb([(hrl[:], h_hbm[128 * m:128 * (m + 1),
                                    512 * ch:512 * (ch + 1)])], k.V, k.S)
                k.w(s, k.LDS, k.V)
                k.S.inc(s.activation(qt8[:], hrl[:], AF.Copy,
                                     scale=qpar2[:, 0:1], bias=12582912.0))
                k.w(v, k.S)
                v.tensor_scalar(qt8[:], qt8[:], 12582912.0, qpar2[:, 1:2],
                                ALU.subtract, ALU.max)
                k.V.inc(v.tensor_scalar(qz2c[:, 512 * m:512 * (m + 1)], qt8[:],
                                        qpar2[:, 2:3], None, ALU.min))
            for e_ in range(8):
                pG_ = pG[e_ % 2]
                k.w(t, k.V, k.S)
                for m in range(32):
                    mm = t.matmul(pG_[:],
                                  wq2[:, 1024 * m + 128 * e_:
                                      1024 * m + 128 * (e_ + 1)],
                                  qz2c[:, 512 * m:512 * (m + 1)],
                                  start=(m == 0), stop=(m == 31))
                k.T.inc(mm)
                k.w(s, k.T, k.V, k.ST)
                k.S.inc(s.activation(g1t[:], pG_[:], AF.Silu,
                                     scale=qpar2[:, 3:4], bias=b2q_sb[:, e_:e_ + 1]))
                ldb([(x1t[:], x1_hbm[128 * e_:128 * (e_ + 1),
                                     512 * ch:512 * (ch + 1)]),
                     (s1t[:], xn1_hbm[128 * e_:128 * (e_ + 1),
                                      512 * ch:512 * (ch + 1)]),
                     (rbt[:], cc4_out[128 * e_:128 * (e_ + 1),
                                      T + 512 * ch:T + 512 * (ch + 1)]),
                     (ot[:], cc4_out[128 * e_:128 * (e_ + 1),
                                     512 * ch:512 * (ch + 1)])],
                    k.V, k.S, k.CC, k.ST)
                k.w(v, k.LDS, k.S)
                v.tensor_scalar(ot[:], ot[:], g5_sb[:, 1:2], None, ALU.mult)
                v.scalar_tensor_tensor(ot[:], rbt[:], g5_sb[:, 0:1], ot[:],
                                       ALU.mult, ALU.add)
                v.tensor_add(s1t[:], s1t[:], ot[:])
                v.tensor_mul(ot[:], s1t[:], g1t[:])
                k.V.inc(v.tensor_add(ot[:], ot[:], x1t[:]))
                k.w(y, k.V)
                k.ST.inc(y.dma_start(
                    out=out_fm[128 * e_:128 * (e_ + 1), 512 * ch:512 * (ch + 1)],
                    in_=ot[:]), 16)
                if ds1 is not None:
                    k.w(y, k.V)
                    k.ST.inc(y.dma_start(
                        out=ds1[128 * e_:128 * (e_ + 1),
                                512 * ch:512 * (ch + 1)], in_=s1t[:]), 16)
                if dg1 is not None:
                    k.w(y, k.V, k.S)
                    k.ST.inc(y.dma_start(
                        out=dg1[128 * e_:128 * (e_ + 1),
                                512 * ch:512 * (ch + 1)], in_=g1t[:]), 16)
        k.barrier()
        ps8.close()
        es8.close()
        k.y.wait_ge(k.ST.h, k.ST.n)
        k.g.wait_ge(k.ST.h, k.ST.n)
        return k


def finalize(k, out_fm, src=None):
    k.barrier()
    if src is None:
        z = k.sb("zfin", (128, T))
        k.V.inc(k.v.memset(z[:], 0.0))
        k.w(k.y, k.V)
        src = [z[:] for _ in range(8)]
    for i in range(8):
        k.ST.inc(k.y.dma_start(out=out_fm[128 * i:128 * (i + 1), :], in_=src[i]), 16)
    k.y.wait_ge(k.ST.h, k.ST.n)
    k.g.wait_ge(k.ST.h, k.ST.n)



def _rotary_tables():
    DH, WINDOW = 64, 512
    half = np.arange(0, DH, 2, dtype=np.float32)
    inv = 1.0 / 10000.0 ** (half / DH)
    tpos = np.arange(K, dtype=np.float32)
    fr = np.concatenate([tpos[:, None] * inv[None, :]] * 2, -1)
    base = (half + 0.4 * DH) / (1.4 * DH)
    power = (tpos - K // 2) / float(WINDOW // 2)
    sc = np.concatenate([base[None, :] ** power[:, None]] * 2, -1)
    cos, sin = np.cos(fr), np.sin(fr)
    sgn = np.concatenate([-np.ones(DH // 2, np.float32),
                          np.ones(DH // 2, np.float32)])
    c1q = (cos * sc).astype(np.float32)
    c2q = (sin * sc * sgn[None, :]).astype(np.float32)
    c1k = (cos / sc).astype(np.float32)
    c2k = (sin / sc * sgn[None, :]).astype(np.float32)
    return c1q, c2q, c1k, c2k


def _prep_inputs(inputs):
    x = np.ascontiguousarray(inputs["L0_attn_x"], dtype=np.float32)
    maps = []
    eye = np.eye(128, dtype=np.float32)
    dm = np.zeros((2, 128, 256), np.float32)
    for o in range(2):
        kkk = np.arange(128)[:, None]
        qf = np.arange(256)[None, :]
        dm[o] = np.where(kkk + 128 * o <= qf, 0.0, NEG).astype(np.float32)
    wjT = np.ascontiguousarray(inputs["L0_attn_wj_w"].T)
    wjvT = np.ascontiguousarray(inputs["L0_attn_wjv_w"].T)
    woT = np.ascontiguousarray(inputs["L0_attn_wo_w"].T)
    perm64 = np.concatenate([np.arange(32, 64), np.arange(0, 32)])
    perm512 = np.concatenate([64 * h + perm64 for h in range(8)])
    com = {}
    for l in range(2):
        wq = np.ascontiguousarray(inputs["L0_ffn_wqkv"][l])
        com[f"wqkv{l}"] = wq
        com[f"wqkvp{l}"] = np.ascontiguousarray(
            np.concatenate([wq[:, :512][:, perm512],
                            wq[:, 512:1024][:, perm512]], 1))
        com[f"wol{l}"] = np.ascontiguousarray(inputs["L0_ffn_wo"][l])
        qs = inputs["L0_ffn_q_scale"][l].astype(np.float32)
        ks = inputs["L0_ffn_k_scale"][l].astype(np.float32)
        com[f"qs{l}"] = qs.reshape(1, 64).copy()
        com[f"qsp{l}"] = qs[perm64].reshape(1, 64).copy()
        com[f"ks{l}"] = ks.reshape(1, 64).copy()
        com[f"ksp{l}"] = ks[perm64].reshape(1, 64).copy()
        com[f"qscol{l}"] = np.stack(
            [np.tile(qs, 2), np.tile(qs[perm64], 2),
             np.tile(ks, 2), np.tile(ks[perm64], 2)], 1).astype(np.float32)
        lg = inputs["L0_ffn_ln_g"][l].astype(np.float32)
        lb = inputs["L0_ffn_ln_b"][l].astype(np.float32)
        com[f"lng{l}"] = np.ascontiguousarray(lg.reshape(8, 128).T)
        com[f"lnb{l}"] = np.ascontiguousarray(lb.reshape(8, 128).T)
        com[f"lngr{l}"] = lg.reshape(1, 1024).copy()
        com[f"lnbr{l}"] = lb.reshape(1, 1024).copy()
        w1 = inputs["L0_ffn_w1"][l].astype(np.float32)
        w1r = np.zeros((1024, 5632), np.float32)
        w1r[:, :2730] = w1[:, :2730]
        w1r[:, 2816:2816 + 2730] = w1[:, 2730:]
        com[f"w1r{l}"] = w1r
        b1 = inputs["L0_ffn_b1"][l].astype(np.float32)
        b1f = np.zeros((128, 44), np.float32)
        b1f[:, :22] = np.pad(b1[:2730], (0, 86)).reshape(22, 128).T
        b1f[:, 22:] = np.pad(b1[2730:], (0, 86)).reshape(22, 128).T
        com[f"b1f{l}"] = b1f
        w2 = inputs["L0_ffn_w2"][l].astype(np.float32)
        w2r = np.zeros((2816, 1024), np.float32)
        w2r[:2730] = w2
        com[f"w2r{l}"] = w2r
        com[f"b2f{l}"] = np.ascontiguousarray(
            inputs["L0_ffn_b2"][l].astype(np.float32).reshape(8, 128).T)
    com["p2h_i"] = np.stack([np.where(np.arange(128) < 64, 1.0, 0.0),
                             np.where(np.arange(128) >= 64, 1.0, 0.0)]).astype(np.float32)
    com["hd2_i"] = np.ascontiguousarray(com["p2h_i"].T)
    def fmaj(vec, nt):
        return np.ascontiguousarray(np.asarray(vec, np.float32).reshape(nt, 128).T)
    com["lnw_f"] = fmaj(inputs["L1_attn_ln_w"], 8)
    com["lnb_f"] = fmaj(inputs["L1_attn_ln_b"], 8)
    com["b1q_f"] = fmaj(inputs["L1_ffn_b1"], 32)
    com["b2q_f"] = fmaj(inputs["L1_ffn_b2"], 8)
    com["w1t_r"] = np.ascontiguousarray(inputs["L1_ffn_W1"].astype(np.float32).T)
    com["w2t_r"] = np.ascontiguousarray(inputs["L1_ffn_W2"].astype(np.float32).T)
    esel = np.zeros((32, 4096), np.float32)
    for n_ in range(16):
        esel[n_, 128 * n_:128 * (n_ + 1)] = 1.0
        esel[16 + n_, 2048 + 128 * n_:2048 + 128 * (n_ + 1)] = 1.0
    com["esel_b"] = esel.astype(BF)
    com["eyeb"] = np.eye(128, dtype=np.float32).astype(BF)
    c1q, c2q, c1k, c2k = _rotary_tables()
    for c in range(8):
        bi, hf = c // 2, c % 2
        o0, s0 = hf * T, (1 - hf) * T
        xb = x[bi]
        xT_keys = np.ascontiguousarray(
            np.concatenate([xb[s0:s0 + T], xb[o0:o0 + T]], 0).T)
        scm = np.zeros(K, np.float32)
        if hf == 0:
            scm[:T] = NEG
        sc_mask = np.ascontiguousarray(scm.reshape(32, 128).T)
        g0 = hf * T
        qpos = np.arange(g0, g0 + T)
        kpos = np.arange(g0 - 512, g0 + T)
        kposc = np.clip(kpos, 0, K - 1)
        rcq = np.stack([np.tile(c1q[qpos].T, (2, 1)),
                        np.tile(c2q[qpos].T, (2, 1))]).astype(BF)
        rck = np.stack([np.tile(c1k[kposc].T, (2, 1)),
                        np.tile(c2k[kposc].T, (2, 1))]).astype(BF)
        wm = np.zeros((16, 128, 640), np.float32)
        r_ = np.arange(128)[:, None]
        cb = np.arange(640)[None, :]
        for qt in range(16):
            a_ = g0 + 128 * qt
            valid = (cb >= r_) & (cb <= r_ + 512) & ((a_ - 512 + cb) >= 0)
            wm[qt] = np.where(valid, 0.0, NEG)
        cz = np.zeros((22, 128, 128), np.float32)
        cpart = np.arange(128)[:, None]
        rfree = np.arange(128)[None, :]
        for qt in range(4):
            a_ = g0 + 128 * qt
            for u in range(5):
                cbu = 128 * u + cpart
                valid = (cbu >= rfree) & (cbu <= rfree + 512) & ((a_ - 512 + cbu) >= 0)
                cz[qt * 5 + u] = valid
        cz[20] = (cpart >= rfree)
        cz[21] = (cpart <= rfree)
        gate = np.full((128, 1), 0.0 if hf == 0 else 1.0, np.float32)
        mp = dict(xT_keys=xT_keys, sc_mask=sc_mask,
                  sc_mask_row=scm.reshape(1, K).copy(), diag_mask=dm,
                  wjT=wjT, wjvT=wjvT, woT=woT, eye_f=eye,
                  rcq_t=rcq, rck_t=rck, wmask=wm, cz=cz.astype(BF),
                  gate128=gate)
        base = hf * 1024
        inp = inputs["L1_attn_in_proj_w"].astype(np.float32)
        mp["inpj"] = np.ascontiguousarray(
            np.concatenate([inp[:, base:base + 1024],
                            inp[:, 2048 + base:2048 + base + 1024]], 1))
        mp["xpj"] = np.ascontiguousarray(
            inputs["L1_attn_x_proj_w"][base:base + 1024].astype(np.float32))
        mp["dtwc"] = np.ascontiguousarray(
            inputs["L1_attn_dt_proj_w"][:, base:base + 1024].astype(np.float32))
        mp["opj"] = np.ascontiguousarray(
            inputs["L1_attn_out_proj_w"][base:base + 1024].astype(np.float32))
        al = inputs["L1_attn_A_log"][base:base + 1024].astype(np.float32)
        mp["alog_f"] = np.ascontiguousarray(
            al.reshape(8, 128, 16).transpose(1, 0, 2).reshape(128, 128))
        mp["dcol_f"] = np.ascontiguousarray(
            inputs["L1_attn_D_skip"][base:base + 1024]
            .astype(np.float32).reshape(8, 128).T)
        mp["cb_f"] = np.ascontiguousarray(
            inputs["L1_attn_conv_b"][base:base + 1024]
            .astype(np.float32).reshape(8, 128).T)
        mp["dtb_f"] = np.ascontiguousarray(
            inputs["L1_attn_dt_proj_b"][base:base + 1024]
            .astype(np.float32).reshape(8, 128).T)
        cwo = inputs["L1_attn_conv_w"][base:base + 1024].astype(np.float32)
        mp["cw_f"] = np.ascontiguousarray(
            cwo.reshape(8, 128, 4).transpose(1, 0, 2).reshape(128, 32))
        mp["w1slab"] = np.ascontiguousarray(com["w1t_r"][128 * c:128 * (c + 1)])
        mp["w2slab"] = np.ascontiguousarray(com["w2t_r"][512 * c:512 * (c + 1)])
        mp["g5c"] = np.ascontiguousarray(
            np.tile(np.array([[hf, 1 - hf]], np.float32), (128, 1)))
        mp.update(com)
        maps.append(mp)
    return maps


def kernel(**inputs):
    maps = _prep_inputs(inputs)
    k = build()
    res = run_bass_kernel_spmd(k.nc, maps, core_ids=list(range(8)))
    global _DBG_RES
    _DBG_RES = res.results
    out = np.zeros((4, K, 1024), np.float32)
    for c in range(8):
        bi, hf = c // 2, c % 2
        out[bi, hf * T:(hf + 1) * T, :] = res.results[c]["out_fm"].T
    return out


# version 06
# speedup vs baseline: 1.1243x; 1.1243x over previous
import numpy as np
import ml_dtypes
import concourse.bass as bass
import concourse.mybir as mybir
from concourse.bass_utils import run_bass_kernel_spmd
from contextlib import ExitStack
from concourse.bass_isa import ReduceOp

F32 = mybir.dt.float32
BF16 = mybir.dt.bfloat16
AF = mybir.ActivationFunctionType
ALU = mybir.AluOpType
AX = mybir.AxisListType
BF = ml_dtypes.bfloat16

T = 2048
K = 4096
NEG = -1e30

_DBG = []
_STAGES = 99
_DBG_RES = {}
_DBGQG = 0
_CHUNK = 8000
_PEROP = False
_BATCH_ENG = ("t", "s", "y", "g")


class Sem:
    def __init__(self, nc, es, name):
        self.h = es.enter_context(nc.semaphore(name))
        self.n = 0

    def inc(self, instr, kk=1):
        instr.then_inc(self.h, kk)
        self.n += kk
        return instr


class Rec:
    __slots__ = ("fn", "posts", "inst", "eng")

    def __init__(self, fn, eng=None):
        self.fn = fn
        self.posts = []
        self.inst = None
        self.eng = eng

    def then_inc(self, sem, kk=1):
        if self.inst is not None:
            self.inst.then_inc(sem, kk)
        else:
            self.posts.append((0, sem, kk))
        return self

    def _wait_ge(self, sem, n):
        if self.inst is not None:
            self.inst._wait_ge(sem, n)
        else:
            self.posts.append((1, sem, n))
        return self


class EngProxy:
    def __init__(self, ker, eng, name=""):
        self._k = ker
        self._eng = eng
        self._name = name

    def __getattr__(self, name):
        m = getattr(self._eng, name)
        if not callable(m):
            return m
        def call(*a, **kw):
            r = Rec(lambda m=m, a=a, kw=kw: m(*a, **kw), self._eng)
            if name in ("wait_ge", "wait_op"):
                r.eng = None
            self._k._push(self._name, r)
            return r
        return call


class Ker:
    def __init__(self):
        self.nc = bass.Bass("TRN2", target_bir_lowering=False,
                            detect_race_conditions=False)
        self.es = ExitStack()
        nc, es = self.nc, self.es
        self.t, self.v, self.s, self.g, self.y = (
            nc.tensor, nc.vector, nc.scalar, nc.gpsimd, nc.sync)
        self.LDS = Sem(nc, es, "lds")
        self.LDG = Sem(nc, es, "ldg")
        self.T = Sem(nc, es, "t")
        self.V = Sem(nc, es, "v")
        self.S = Sem(nc, es, "s")
        self.G = Sem(nc, es, "g")
        self.CC = Sem(nc, es, "cc")
        self.ST = Sem(nc, es, "st")
        self.GATE = Sem(nc, es, "gate")
        self.all_sems = [self.LDS, self.LDG, self.T, self.V, self.S, self.G, self.CC]
        self._dbg_outs = {}
        self._lastw = {}
        self._pend = {}

    def bind(self, blk):
        self._blk = blk
        self._secmap = {"t": blk.tensor, "v": blk.vector, "s": blk.scalar,
                        "g": blk.gpsimd, "y": blk.sync}
        self._run = []
        self._cur_eng = None
        self.t = EngProxy(self, self.nc.tensor, "t")
        self.v = EngProxy(self, self.nc.vector, "v")
        self.s = EngProxy(self, self.nc.scalar, "s")
        self.g = EngProxy(self, self.nc.gpsimd, "g")
        self.y = EngProxy(self, self.nc.sync, "y")
        return self.t, self.v, self.s, self.g, self.y

    def scope_mark(self, name):
        pass

    def _push(self, eng, r):
        if self._cur_eng != eng:
            self._flush_run()
            self._cur_eng = eng
        self._run.append(r)

    def _flush_run(self):
        run = getattr(self, "_run", None)
        if not run:
            return
        chunk = list(run)
        run.clear()
        sec = self._secmap[self._cur_eng]
        def emit_one(r):
            inst = r.fn()
            r.inst = inst
            for kind, sem, val in r.posts:
                if kind == 0:
                    inst.then_inc(sem, val)
                else:
                    inst._wait_ge(sem, val)
        if _PEROP or self._cur_eng not in _BATCH_ENG:
            for r_ in chunk:
                def f1(_e, r=r_):
                    emit_one(r)
                sec(f1)
            return
        i = 0
        n = len(chunk)
        while i < n:
            if chunk[i].eng is None:
                def f1(_e, r=chunk[i]):
                    emit_one(r)
                sec(f1)
                i += 1
                continue
            j = i
            while j < n and chunk[j].eng is not None:
                j += 1
            sub = chunk[i:j]
            def f(_e, sub=sub):
                for r in sub:
                    emit_one(r)
            sec(f)
            i = j

    def flush(self):
        self._flush_run()

    def sb(self, name, shape, dtype=F32, stack=None):
        self._flush_run()
        return (stack or self.es).enter_context(
            self.nc.sbuf_tensor(name, list(shape), dtype))

    def es_ps(self, stack, name, shape, dtype=F32):
        self._flush_run()
        return stack.enter_context(self.nc.psum_tensor(name, list(shape), dtype))

    _psn = [0]
    def ps(self, shape=(128, 512), dtype=F32, name=None):
        self._flush_run()
        self._psn[0] += 1
        return self.es.enter_context(self.nc.psum_tensor(
            name or f"psb{self._psn[0]}", list(shape), dtype))

    def barrier(self):
        for e in (self.t, self.v, self.s, self.g, self.y):
            self.w(e, *self.all_sems)

    def w(self, eng, *sems):
        lw = self._lastw
        key_e = getattr(eng, "_name", None) or id(eng)
        for sm in sems:
            if sm.n and lw.get((key_e, id(sm)), -1) < sm.n:
                eng.wait_ge(sm.h, sm.n)
                lw[(key_e, id(sm))] = sm.n

    def wn(self, eng, sem, n):
        lw = self._lastw
        key_e = getattr(eng, "_name", None) or id(eng)
        if n and lw.get((key_e, id(sem)), -1) < n:
            eng.wait_ge(sem.h, n)
            lw[(key_e, id(sem))] = n

    def constf(self, val, parts=128):
        key = (float(val), parts)
        if not hasattr(self, "_consts"):
            self._consts = {}
        if key not in self._consts:
            tname = f"c{len(self._consts)}"
            tt = self.sb(tname, (parts, 1))
            self.V.inc(self.v.memset(tt[:], float(val)))
            self._consts[key] = tt
        return self._consts[key][:]

    def gate(self, eng, *sems):
        self.w(eng, *sems)
        self.GATE.inc(eng.nop())
        return (self.GATE.h, self.GATE.n)

    def dbg_out(self, name, shape, dtype=F32):
        if name not in _DBG:
            return None
        ap = self.nc.dram_tensor("dbg_" + name, list(shape), dtype,
                                 kind="ExternalOutput").ap()
        self._dbg_outs[name] = ap
        return ap

    def dump(self, name, src_ap, col=None):
        ap = self._dbg_outs.get(name)
        if ap is None:
            return
        if col is not None:
            ap = ap[col[0]:col[1]] if isinstance(col, tuple) else ap[col]
        self.w(self.y, self.V, self.S, self.T, self.G)
        self.ST.inc(self.y.dma_start(out=ap, in_=src_ap), 16)


def build():
    k = Ker()
    nc, es = k.nc, k.es
    t, v, s, g, y = k.t, k.v, k.s, k.g, k.y

    def din(name, shape, dtype=F32):
        return nc.dram_tensor(name, list(shape), dtype, kind="ExternalInput").ap()

    xT_keys = din("xT_keys", (1024, K))
    sc_mask = din("sc_mask", (128, 32))
    sc_mask_row = din("sc_mask_row", (1, K))
    diag_mask = din("diag_mask", (2, 128, 256))
    wjT = din("wjT", (1024, 1024))
    wjvT = din("wjvT", (1024, 1024))
    woT = din("woT", (1024, 1024))
    eye_f = din("eye_f", (128, 128))
    wqkv_i = [din(f"wqkv{l}", (1024, 1536)) for l in range(2)]
    wqkvp_i = [din(f"wqkvp{l}", (1024, 1024)) for l in range(2)]
    wo_i = [din(f"wol{l}", (512, 1024)) for l in range(2)]
    qsr = [din(f"qs{l}", (1, 64)) for l in range(2)]
    qspr = [din(f"qsp{l}", (1, 64)) for l in range(2)]
    ksr = [din(f"ks{l}", (1, 64)) for l in range(2)]
    kspr = [din(f"ksp{l}", (1, 64)) for l in range(2)]
    rcq_t = din("rcq_t", (2, 128, T), BF16)
    rck_t = din("rck_t", (2, 128, 2560), BF16)
    wmask_i = din("wmask", (16, 128, 640))
    cz_i = din("cz", (22, 128, 128), BF16)
    gate_i = din("gate128", (128, 1))
    lng_i = [din(f"lng{l}", (128, 8)) for l in range(2)]
    lnb_i = [din(f"lnb{l}", (128, 8)) for l in range(2)]
    w1_i = [din(f"w1r{l}", (1024, 5632)) for l in range(2)]
    b1_i = [din(f"b1f{l}", (128, 44)) for l in range(2)]
    w2_i = [din(f"w2r{l}", (2816, 1024)) for l in range(2)]
    b2_i = [din(f"b2f{l}", (128, 8)) for l in range(2)]
    p2h_i = din("p2h_i", (2, 128))
    hd2_i = din("hd2_i", (128, 2))
    qscol_i = [din(f"qscol{l}", (128, 4)) for l in range(2)]
    lngr_i = [din(f"lngr{l}", (1, 1024)) for l in range(2)]
    lnbr_i = [din(f"lnbr{l}", (1, 1024)) for l in range(2)]
    lnw_i = din("lnw_f", (128, 8))
    lnb_i = din("lnb_f", (128, 8))
    inpj_i = din("inpj", (1024, 2048))
    xpj_i = din("xpj", (1024, 96))
    dtw_i = din("dtwc", (64, 1024))
    opj_i = din("opj", (1024, 1024))
    alog_i = din("alog_f", (128, 128))
    dcol_i = din("dcol_f", (128, 8))
    cbq_i = din("cb_f", (128, 8))
    dtb_i = din("dtb_f", (128, 8))
    cw_i = din("cw_f", (128, 32))
    w1t_i = din("w1t_r", (1024, 4096))
    w2t_i = din("w2t_r", (4096, 1024))
    w1slab_i = din("w1slab", (128, 4096))
    w2slab_i = din("w2slab", (512, 1024))
    b1q_i = din("b1q_f", (128, 32))
    b2q_i = din("b2q_f", (128, 8))
    esel_i = din("esel_b", (32, 4096), BF16)
    eyeb_i = din("eyeb", (128, 128), BF16)
    g5_i = din("g5c", (128, 2))
    out_fm = nc.dram_tensor("out_fm", [1024, T], F32, kind="ExternalOutput").ap()

    val_hbm = nc.dram_tensor("val_hbm", [K, 1024], BF16, kind="Internal").ap()
    ab_hbm = nc.dram_tensor("ab_hbm", [1, 2 * T], F32, kind="Internal").ap()
    xext_hbm = nc.dram_tensor("xext_hbm", [1024, 2561], BF16, kind="Internal").ap()
    s_hbm = nc.dram_tensor("s_hbm", [1024, T], BF16, kind="Internal").ap()
    cc_in = nc.dram_tensor("cc_in", [1024, 516], F32, kind="Internal").ap()
    cc_out = nc.dram_tensor("cc_out", [2, 1024, 516], F32, kind="Internal").ap()

    onesrow = k.sb("onesrow", (1, 128))
    onescol_b = k.sb("onescol_b", (128, 1), BF16)
    onescol_f = k.sb("onescol_f", (128, 1))
    eye_sb = k.sb("eye_sb", (128, 128))
    scmask_sb = k.sb("scmask_sb", (128, 32))
    sc_f = k.sb("sc_f", (128, 32))
    scG = k.sb("scG", (128, 32))
    scT_sb = k.sb("scT_sb", (32, 128))
    grow = k.sb("grow", (128, 4))


    es_pb = ExitStack()
    pb = [k.es_ps(es_pb, f'pb{i}', (128, 512)) for i in range(8)]

    sccols = k.sb("sccols", (128, 4))
    k.constf(1e-5, 1)
    k.constf(12582912.0, 1)
    k.constf(12582912.0, 128)
    k.constf(1.0, 128)
    mxrow = k.sb("mxrow", (128, 2))
    scr = k.sb("scr", (1, 16))
    qsrow = k.sb("qsrow", (1, 8))
    q2row = k.sb("q2row", (1, 4))
    qpar = k.sb("qpar", (128, 8))
    qpar2 = k.sb("qpar2", (128, 4))
    g5_sb = k.sb("g5_sb", (128, 2))
    es_vs = ExitStack()
    vstage = k.sb("vstage", (128, 2 * 1024), BF16, stack=es_vs)
    es_a = ExitStack()
    xnb = k.sb("xnb", (128, 8 * K), BF16, stack=es_a)
    es_1 = ExitStack()

    with nc.Block() as blk:
        t, v, s, g, y = k.bind(blk)

        k.V.inc(v.memset(onesrow[:], 1.0))
        k.V.inc(v.memset(onescol_b[:], 1.0))
        k.V.inc(v.memset(onescol_f[:], 1.0))
        k.LDS.inc(y.dma_start(out=scmask_sb[:], in_=sc_mask), 16)
        k.LDS.inc(y.dma_start(out=eye_sb[:], in_=eye_f), 16)
        def scope(name):
            k.scope_mark(name)

        scope("s1_ln")
        xslab = [k.sb("xslab0", (128, 8 * 512), stack=es_1)]
        sqb = [k.sb(f"sqb{p}", (128, 512), stack=es_1) for p in range(2)]
        lnrow = k.sb("lnrow", (1, 4 * 512), stack=es_1)
        rowA = k.sb("rowA", (1, 512), stack=es_1)
        rowB = k.sb("rowB", (1, 512), stack=es_1)
        tscr = k.sb("tscr", (128, 512), stack=es_1)

        for sl in range(8):
            xb = xslab[0]
            k.w(y, k.V, k.S, k.T)
            for i in range(8):
                k.LDS.inc(y.dma_start(
                    out=xb[:, 512 * i:512 * (i + 1)],
                    in_=xT_keys[128 * i:128 * (i + 1), 512 * sl:512 * (sl + 1)]), 16)
            k.w(t, k.LDS, k.V, k.S)
            for i in range(8):
                mm = t.matmul(pb[0][0:1, :], onescol_f[:],
                              xb[:, 512 * i:512 * (i + 1)],
                              start=(i == 0), stop=(i == 7))
            k.T.inc(mm)
            for i in range(8):
                sq = sqb[i % 2]
                k.w(s, k.T)
                k.S.inc(s.activation(sq[:], xb[:, 512 * i:512 * (i + 1)], AF.Square))
                k.w(t, k.S)
                k.T.inc(t.matmul(pb[1][0:1, :], onescol_f[:], sq[:],
                                 start=(i == 0), stop=(i == 7)))
            MU, M2, MSQ, VAR = (lnrow[0:1, 512 * i:512 * (i + 1)] for i in range(4))
            AA, BB = rowA[:], rowB[:]
            k.w(s, k.T)
            k.S.inc(s.activation(MU, pb[0][0:1, :], AF.Copy, scale=1.0 / 1024))
            k.S.inc(s.activation(M2, pb[1][0:1, :], AF.Copy, scale=1.0 / 1024))
            k.w(v, k.S)
            k.V.inc(v.tensor_mul(MSQ, MU, MU))
            k.V.inc(v.tensor_sub(VAR, M2, MSQ))
            k.w(s, k.V)
            k.S.inc(s.activation(MSQ, VAR, AF.Ln, bias=k.constf(1e-5, 1)))
            k.S.inc(s.activation(AA, MSQ, AF.Exp, scale=-0.5))
            k.w(v, k.S)
            k.V.inc(v.tensor_mul(BB, MU, AA))
            k.V.inc(v.tensor_scalar_mul(BB, BB, -1.0))
            if sl >= 4:
                k.w(y, k.S, k.V)
                k.LDS.inc(y.dma_start(
                    out=ab_hbm[0:1, 512 * (sl - 4):512 * (sl - 3)], in_=AA), 16)
                k.LDS.inc(y.dma_start(
                    out=ab_hbm[0:1, T + 512 * (sl - 4):T + 512 * (sl - 3)], in_=BB), 16)
            k.w(t, k.V, k.S)
            k.T.inc(t.matmul(pb[2][:, 0:512], onesrow[:], AA, start=True, stop=True))
            k.T.inc(t.matmul(pb[3][:, 0:512], onesrow[:], BB, start=True, stop=True))
            k.w(v, k.T)
            for i in range(8):
                v.tensor_mul(tscr[:], xb[:, 512 * i:512 * (i + 1)], pb[2][:, 0:512])
                k.V.inc(v.tensor_add(
                    xnb[:, K * i + 512 * sl:K * i + 512 * (sl + 1)], tscr[:], pb[3][:, 0:512]))
        for i in range(8):
            k.w(y, k.V)
            k.LDS.inc(y.dma_start(out=xext_hbm[128 * i:128 * (i + 1), :],
                                  in_=xnb[:, K * i + 1535:K * i + 4096]), 16)
        xn_dbg = k.dbg_out("xn", (1024, K), BF16)
        if xn_dbg is not None:
            for i in range(8):
                k.dump("xn", xnb[:, K * i:K * (i + 1)], col=(128 * i, 128 * (i + 1)))

        if _STAGES < 2:
            finalize(k, out_fm)
            return k

        scope("sA_sc")
        es_1.close()
        w3 = k.sb("w3", (128, 3 * 8192), BF16, stack=es_a)
        gh, gn = k.gate(g, k.T, k.V, k.S)
        for wi, wsrc in enumerate((wjT, wjvT, woT)):
            for i in range(8):
                dd = g.dma_start(
                    out=w3[:, 8192 * wi + 1024 * i:8192 * wi + 1024 * (i + 1)],
                    in_=wsrc[128 * i:128 * (i + 1), :])
                dd._wait_ge(gh, gn)
                k.LDG.inc(dd, 16)
        sqa = [k.sb(f"sqa{p}", (128, 512), stack=es_a) for p in range(2)]
        k.w(t, k.LDG)
        for j in range(32):
            for wi, pbank in ((0, (pb[0], pb[1])), (1, (pb[4], pb[5]))):
                for h in range(2):
                    k.w(t, k.V, k.S)
                    for i in range(8):
                        mm = t.matmul(
                            pbank[h][:, 0:512],
                            xnb[:, K * i + 128 * j:K * i + 128 * (j + 1)],
                            w3[:, 8192 * wi + 1024 * i + 512 * h:
                               8192 * wi + 1024 * i + 512 * (h + 1)],
                            start=(i == 0), stop=(i == 7))
                    k.T.inc(mm)
            k.w(s, k.T, k.V)
            k.S.inc(s.activation(sqa[0][:], pb[0][:, 0:512], AF.Square,
                                 accum_out=sccols[:, 0:1]))
            k.S.inc(s.activation(sqa[1][:], pb[1][:, 0:512], AF.Square,
                                 accum_out=sccols[:, 1:2]))
            k.w(v, k.S)
            v.tensor_add(sccols[:, 2:3], sccols[:, 0:1], sccols[:, 1:2])
            k.V.inc(v.tensor_scalar_mul(sc_f[:, j:j + 1], sccols[:, 2:3], 1.0 / 32.0))
            k.w(v, k.T, k.LDS)
            k.V.inc(v.tensor_copy(vstage[:, 1024 * (j % 2):1024 * (j % 2) + 512],
                                  pb[4][:, 0:512]))
            k.w(s, k.T, k.LDS)
            k.S.inc(s.copy(vstage[:, 1024 * (j % 2) + 512:1024 * (j % 2) + 1024],
                           pb[5][:, 0:512]))
            k.w(y, k.V, k.S)
            k.LDS.inc(y.dma_start(out=val_hbm[128 * j:128 * (j + 1), :],
                                  in_=vstage[:, 1024 * (j % 2):1024 * (j % 2) + 1024]), 16)
        sc_dbg = k.dbg_out("sc", (128, 32))
        if sc_dbg is not None:
            k.dump("sc", sc_f[:])
        val_dbg = k.dbg_out("val", (K, 1024), BF16)
        if val_dbg is not None:
            ghd_, gnd_ = k.gate(y, k.LDS, k.V, k.S, k.T)
            dd = y.dma_start(out=val_dbg, in_=val_hbm)
            dd._wait_ge(ghd_, gnd_)
            k.ST.inc(dd, 16)
        xn2_dbg = k.dbg_out("xn2", (1024, K), BF16)
        if xn2_dbg is not None:
            for i in range(8):
                k.dump("xn2", xnb[:, K * i:K * (i + 1)],
                       col=(128 * i, 128 * (i + 1)))
        sq0_dbg = k.dbg_out("sq0", (128, 512))
        if sq0_dbg is not None:
            k.dump("sq0", sqa[0][:])
        w3_dbg = k.dbg_out("w3a", (128, 2048), BF16)
        if w3_dbg is not None:
            k.dump("w3a", w3[:, 0:2048])

        scope("sA_bc")
        k.barrier()
        es_a.close()
        es_3 = ExitStack()
        scq_bc = k.sb("scq_bc", (128, K), stack=es_3)
        mrow = k.sb("mrow", (1, K), stack=es_3)
        scrow = k.sb("scrow", (1, K), stack=es_3)
        dmask_sb = k.sb("dmask_sb", (128, 2 * 256), stack=es_3)
        linv_sb = k.sb("linv_sb", (128, 256), stack=es_3)
        sres = k.sb("sres", (128, 2 * 256), BF16, stack=es_3)
        for i_ in range(2):
            dd = y.dma_start(out=dmask_sb[:, 256 * i_:256 * (i_ + 1)],
                             in_=diag_mask[i_])
            dd._wait_ge(gh, gn)
            k.LDS.inc(dd, 16)
        gh, gn = k.gate(y, k.T, k.V, k.S, k.G)
        dd = y.dma_start(out=mrow[:], in_=sc_mask_row)
        dd._wait_ge(gh, gn)
        k.LDS.inc(dd, 16)
        k.w(t, k.V, k.S)
        k.T.inc(t.transpose(pb[6][0:32, 0:128], sc_f[:], eye_sb[:]))
        k.w(s, k.T)
        k.S.inc(s.copy(scT_sb[:], pb[6][0:32, 0:128]))
        k.w(y, k.S)
        k.LDS.inc(y.dma_start(out=scrow[:], in_=scT_sb[:]), 16)
        k.w(v, k.LDS)
        k.V.inc(v.tensor_add(mrow[:], scrow[:], mrow[:]))
        k.V.inc(v.tensor_reduce(grow[0:1, 0:1], mrow[:], AX.X, ALU.max))
        k.w(t, k.V)
        k.T.inc(t.matmul(pb[6][:, 0:1], onesrow[:], grow[0:1, 0:1],
                         start=True, stop=True))
        k.w(s, k.T)
        k.S.inc(s.copy(grow[:, 1:2], pb[6][:, 0:1]))
        k.w(v, k.S)
        k.V.inc(v.tensor_scalar(scG[:], sc_f[:], grow[:, 1:2], None, ALU.subtract))
        for j in range(32):
            bank = pb[6 + (j % 2)]
            k.w(t, k.S, k.LDS)
            k.T.inc(t.matmul(bank[:, 0:128], onesrow[:],
                             scrow[0:1, 128 * j:128 * (j + 1)],
                             start=True, stop=True))
            k.w(s, k.T)
            k.S.inc(s.copy(scq_bc[:, 128 * j:128 * (j + 1)], bank[:, 0:128]))

        dscG = k.dbg_out("scG", (128, 32))
        if dscG is not None:
            k.dump("scG", scG[:])
        dpt = k.dbg_out("pt0", (128, 32 * 256), BF16)
        dptp = k.dbg_out("ptpre", (128, 32 * 256), BF16)
        dacc = k.dbg_out("acc0", (128, 8 * 256), BF16)
        dlin = k.dbg_out("linv0", (128, 256))
        ptbuf = k.sb("ptbuf", (128, 32 * 256), BF16, stack=es_3)
        k.V.inc(v.memset(ptbuf[:], 0.0))
        accb = k.sb("accb", (128, 8 * 256), BF16, stack=es_3)
        expo = k.sb("expo", (128, 256), stack=es_3)
        woT_sb = k.sb("woT_sb", (128, 8192), BF16, stack=es_3)
        gh, gn = k.gate(g, k.T, k.V, k.S)
        for i in range(8):
            dd = g.dma_start(out=woT_sb[:, 1024 * i:1024 * (i + 1)],
                             in_=woT[128 * i:128 * (i + 1), :])
            dd._wait_ge(gh, gn)
            k.LDG.inc(dd, 16)
        woTb = woT_sb
        k.w(t, k.LDG)
        scope("sA_pv")
        for qg in range(8):
            njt = 16 + 2 * qg + 2
            qcol = 2048 + 256 * qg
            for j in range(njt):
                k.w(v, k.S)
                ts = v.tensor_scalar(expo[:], scq_bc[:, qcol:qcol + 256],
                                     scG[:, j:j + 1], scmask_sb[:, j:j + 1],
                                     ALU.mult, ALU.add)
                if j >= 16 + 2 * qg:
                    o = j - (16 + 2 * qg)
                    ts = v.tensor_add(expo[:], expo[:],
                                      dmask_sb[:, 256 * o:256 * (o + 1)])
                k.V.inc(ts)
                k.w(s, k.V)
                if qg and j == 0:
                    k.w(s, k.T)
                k.S.inc(s.activation(ptbuf[:, 256 * j:256 * (j + 1)], expo[:], AF.Exp))
            if qg == _DBGQG:
                k.dump("ptpre", ptbuf[:])
            k.w(t, k.S, k.V)
            for j in range(njt):
                mm = t.matmul(pb[2][0:1, 0:256], onescol_b[:],
                              ptbuf[:, 256 * j:256 * (j + 1)],
                              start=(j == 0), stop=(j == njt - 1))
            k.T.inc(mm)
            k.w(v, k.T)
            k.V.inc(v.reciprocal(linv_sb[0:1, 0:256], pb[2][0:1, 0:256]))
            k.w(t, k.V)
            k.T.inc(t.matmul(pb[3][:, 0:256], onesrow[:], linv_sb[0:1, 0:256],
                             start=True, stop=True))
            k.w(s, k.T)
            k.S.inc(s.copy(linv_sb[:, 0:256], pb[3][:, 0:256]))
            k.w(t, k.S)
            for j in range(njt):
                k.w(y, k.T)
                k.LDS.inc(y.dma_start(out=vstage[:, 1024 * (j % 2):1024 * (j % 2 + 1)],
                                      in_=val_hbm[128 * j:128 * (j + 1), :]), 16)
                k.w(t, k.LDS)
                for m in range(8):
                    mm = t.matmul(pb[m][:, 0:256],
                                  vstage[:, 1024 * (j % 2) + 128 * m:1024 * (j % 2) + 128 * (m + 1)],
                                  ptbuf[:, 256 * j:256 * (j + 1)],
                                  start=(j == 0), stop=(j == njt - 1))
                k.T.inc(mm)
            k.w(v, k.T)
            for m in range(8):
                k.V.inc(v.tensor_copy(accb[:, 256 * m:256 * (m + 1)],
                                      pb[m][:, 0:256]))
            if qg == _DBGQG:
                k.dump("pt0", ptbuf[:])
                k.dump("acc0", accb[:])
                k.dump("linv0", linv_sb[:, 0:256])
            for m in range(8):
                k.w(t, k.V, k.S)
                bank = pb[6 + (m % 2)]
                for i in range(8):
                    mm = t.matmul(bank[:, 0:256], woTb[:, 1024 * i + 128 * m:1024 * i + 128 * (m + 1)],
                                  accb[:, 256 * i:256 * (i + 1)],
                                  start=(i == 0), stop=(i == 7))
                k.T.inc(mm)
                k.w(v, k.T, k.LDS)
                k.V.inc(v.tensor_mul(sres[:, 256 * (m % 2):256 * (m % 2 + 1)],
                                     bank[:, 0:256], linv_sb[:, 0:256]))
                k.w(y, k.V)
                k.LDS.inc(y.dma_start(
                    out=s_hbm[128 * m:128 * (m + 1), 256 * qg:256 * (qg + 1)],
                    in_=sres[:, 256 * (m % 2):256 * (m % 2 + 1)]), 16)
        s_dbg = k.dbg_out("s0", (1024, T), BF16)
        if s_dbg is not None:
            k.w(k.y, k.LDS)
            k.ST.inc(k.y.dma_start(out=s_dbg, in_=s_hbm), 16)



        if _STAGES < 3:
            finalize(k, out_fm)
            return k
        scope("s2_pre")
        k.barrier()
        es_3.close()
        es_vs.close()
        es_pb.close()
        es_f = ExitStack()
        xi = k.sb("xi", (128, 8 * 2049), stack=es_f)
        xslf = k.sb("xslf", (128, 8 * 513), BF16, stack=es_f)
        escr = k.sb("escr", (128, 2 * 512), stack=es_f)
        sqn = k.sb("sqn", (128, 513), stack=es_f)
        p2h = k.sb("p2h", (2, 128), stack=es_f)
        hd2 = k.sb("hd2", (128, 2), stack=es_f)
        gsc = k.sb("gsc", (128, 1), stack=es_f)
        qscols = k.sb("qscols", (128, 4), stack=es_f)
        gtile = k.sb("gtile", (128, 2 * 512), BF16, stack=es_f)
        ggscr = k.sb("ggscr", (128, 512), stack=es_f)
        gcol = k.sb("gcol", (128, 8), stack=es_f)
        lgr = k.sb("lgr", (1, 2 * 1024), stack=es_f)
        b1sb = k.sb("b1sb", (128, 44), stack=es_f)
        b2sb = k.sb("b2sb", (128, 8), stack=es_f)

        gated_hbm = nc.dram_tensor("gated_hbm", [4, 22, 128, 512], BF16,
                                   kind="Internal").ap()
        x1_hbm = nc.dram_tensor("x1_hbm", [1024, T], F32, kind="Internal").ap()
        cc0_in = [nc.dram_tensor(f"cc0i{l_}", [1024, 1], F32, kind="Internal").ap()
                  for l_ in range(2)]
        cc0_out = [nc.dram_tensor(f"cc0o{l_}", [2, 1024, 1], F32, kind="Internal").ap()
                   for l_ in range(2)]
        cc1_in = nc.dram_tensor("cc1i", [1024, 513], BF16, kind="Internal").ap()
        cc1_out = nc.dram_tensor("cc1o", [2, 1024, 513], BF16, kind="Internal").ap()

        ghx, gnx = k.gate(y, k.T, k.V, k.S, k.G)
        for nm_ap, dstt in ((p2h_i, p2h), (gate_i, gsc), (hd2_i, hd2)):
            dd = y.dma_start(out=dstt[:], in_=nm_ap)
            dd._wait_ge(ghx, gnx)
            k.LDS.inc(dd, 16)

        CZMAP = {}
        for qt_ in range(16):
            for u_ in range(5):
                if qt_ < 4:
                    CZMAP[(qt_, u_)] = qt_ * 5 + u_
                elif u_ == 0:
                    CZMAP[(qt_, u_)] = 20
                elif u_ == 4:
                    CZMAP[(qt_, u_)] = 21

        ps_init = ExitStack()
        pA0 = k.es_ps(ps_init, "pA0", (128, 512))
        pB0 = k.es_ps(ps_init, "pB0", (128, 512))

        es_ab = ExitStack()
        abrows = k.sb("abrows", (1, 2 * T), stack=es_ab)
        gh0, gn0 = k.gate(y, k.T, k.V, k.S)
        dd = y.dma_start(out=abrows[:], in_=ab_hbm)
        dd._wait_ge(gh0, gn0)
        k.LDS.inc(dd, 16)
        for sl in range(4):
            k.w(t, k.V, k.S, k.LDS)
            k.T.inc(t.matmul(pA0[:, 0:512], onesrow[:],
                             abrows[0:1, 512 * sl:512 * (sl + 1)],
                             start=True, stop=True))
            k.T.inc(t.matmul(pB0[:, 0:512], onesrow[:],
                             abrows[0:1, T + 512 * sl:T + 512 * (sl + 1)],
                             start=True, stop=True))
            for i in range(8):
                k.w(y, k.V)
                k.LDS.inc(y.dma_start(
                    out=escr[:, 0:512],
                    in_=xT_keys[128 * i:128 * (i + 1),
                                2048 + 512 * sl:2048 + 512 * (sl + 1)]), 16)
                k.w(v, k.T, k.LDS)
                v.tensor_mul(escr[:, 512:1024], escr[:, 0:512], pA0[:, 0:512])
                k.V.inc(v.tensor_add(
                    xi[:, 2049 * i + 1 + 512 * sl:2049 * i + 1 + 512 * (sl + 1)],
                    escr[:, 512:1024], pB0[:, 0:512]))
        for i in range(8):
            k.V.inc(v.memset(xi[:, 2049 * i:2049 * i + 1], 0.0))
        k.barrier()
        es_ab.close()
        ps_init.close()

        for l in range(2):
            es_al = ExitStack()
            qkw = k.sb(f"qkw{l}", (128, 8 * 2560), BF16, stack=es_al)
            kat = k.sb(f"kat{l}", (65, 8 * 12 * 128), BF16, stack=es_al)
            vat = k.sb(f"vat{l}", (128, 12 * 520), BF16, stack=es_al)
            qat = k.sb(f"qat{l}", (65, 8 * 512), BF16, stack=es_al)
            rcs = k.sb(f"rcs{l}", (128, 4 * 512), BF16, stack=es_al)
            wmk = k.sb(f"wmk{l}", (128, 640), stack=es_al)
            czq = k.sb(f"czq{l}", (128, 5 * 128), BF16, stack=es_al)
            nsb = k.sb(f"nsb{l}", (2, 512), stack=es_al)
            ptb = k.sb(f"ptb{l}", (128, 640), BF16, stack=es_al)
            obuf = k.sb(f"obuf{l}", (65, 1024), BF16, stack=es_al)
            lrow = k.sb(f"lrow{l}", (1, 1024), stack=es_al)
            negma = k.sb(f"negma{l}", (128, 8), stack=es_al)
            nrow8 = k.sb(f"nrow8{l}", (8, 128), BF16, stack=es_al)
            o_fm = k.sb(f"o_fm{l}", (128, 4 * 512), BF16, stack=es_al)
            wo_sb = k.sb(f"wo_sb{l}", (128, 4 * 1024), BF16, stack=es_al)
            k.V.inc(v.memset(kat[:], 0.0))
            k.V.inc(v.memset(kat[64:65, :], 1.0))
            k.V.inc(v.memset(vat[:], 0.0))
            k.V.inc(v.memset(qat[:], 0.0))
            ghf, gnf = k.gate(g, k.T, k.V, k.S)
            for i in range(8):
                dd = g.dma_start(out=qkw[:, 2560 * i:2560 * i + 1536],
                                 in_=wqkv_i[l][128 * i:128 * (i + 1), :])
                dd._wait_ge(ghf, gnf)
                k.LDG.inc(dd, 16)
                dd = g.dma_start(out=qkw[:, 2560 * i + 1536:2560 * (i + 1)],
                                 in_=wqkvp_i[l][128 * i:128 * (i + 1), :])
                dd._wait_ge(ghf, gnf)
                k.LDG.inc(dd, 16)
            for i in range(4):
                dd = g.dma_start(out=wo_sb[:, 1024 * i:1024 * (i + 1)],
                                 in_=wo_i[l][128 * i:128 * (i + 1), :])
                dd._wait_ge(ghf, gnf)
                k.LDG.inc(dd, 16)
            k.w(y, k.V, k.S, k.T)
            k.LDS.inc(y.dma_start(out=qscols[:], in_=qscol_i[l]), 16)
            k.LDS.inc(y.dma_start(out=lgr[0:1, 0:1024], in_=lngr_i[l]), 16)
            k.LDS.inc(y.dma_start(out=lgr[0:1, 1024:2048], in_=lnbr_i[l]), 16)
            k.LDS.inc(y.dma_start(out=b1sb[:], in_=b1_i[l]), 16)
            k.LDS.inc(y.dma_start(out=b2sb[:], in_=b2_i[l]), 16)

            scope(f"s2_att{l}")
            for half in range(2):
                for esi in range(3):
                    esl = 2 * half + esi
                    lt0 = 4 * esl - 8 * half
                    k.w(y, k.T, k.V, k.S, k.CC)
                    for i in range(8):
                        if l == 0 or esl > 0:
                            k.LDS.inc(y.dma_start(
                                out=xslf[:, 513 * i:513 * (i + 1)],
                                in_=xext_hbm[128 * i:128 * (i + 1),
                                             512 * esl:512 * esl + 513]), 16)
                        else:
                            k.LDS.inc(y.dma_start(
                                out=xslf[:, 513 * i:513 * (i + 1)],
                                in_=cc1_out[0, 128 * i:128 * (i + 1), :]), 16)
                    if esl == 0:
                        k.w(v, k.LDS)
                        for i in range(8):
                            k.V.inc(v.tensor_scalar_mul(
                                xslf[:, 513 * i:513 * (i + 1)],
                                xslf[:, 513 * i:513 * (i + 1)], gsc[:]))
                    elif esl == 1:
                        k.w(v, k.LDS)
                        for i in range(8):
                            k.V.inc(v.tensor_scalar_mul(
                                xslf[:, 513 * i:513 * i + 1],
                                xslf[:, 513 * i:513 * i + 1], gsc[:]))
                    k.w(y, k.V, k.S)
                    k.LDS.inc(y.dma_start(out=rcs[:, 1024:1536],
                                          in_=rck_t[0, :, 512 * esl:512 * (esl + 1)]), 16)
                    k.LDS.inc(y.dma_start(out=rcs[:, 1536:2048],
                                          in_=rck_t[1, :, 512 * esl:512 * (esl + 1)]), 16)
                    build_q = (esl >= 1) and (esl - 1 in (2 * half, 2 * half + 1))
                    osl = esl - 1
                    if build_q:
                        k.LDS.inc(y.dma_start(out=rcs[:, 0:512],
                                              in_=rcq_t[0, :, 512 * osl:512 * (osl + 1)]), 16)
                        k.LDS.inc(y.dma_start(out=rcs[:, 512:1024],
                                              in_=rcq_t[1, :, 512 * osl:512 * (osl + 1)]), 16)
                    k.w(v, k.LDS)
                    k.V.inc(v.tensor_scalar(rcs[:, 1024:1536], rcs[:, 1024:1536],
                                            qscols[:, 2:3], None, ALU.mult))
                    k.V.inc(v.tensor_scalar(rcs[:, 1536:2048], rcs[:, 1536:2048],
                                            qscols[:, 3:4], None, ALU.mult))
                    if build_q:
                        k.V.inc(v.tensor_scalar(rcs[:, 0:512], rcs[:, 0:512],
                                                qscols[:, 0:1], 8.0, ALU.mult, ALU.mult))
                        k.V.inc(v.tensor_scalar(rcs[:, 512:1024], rcs[:, 512:1024],
                                                qscols[:, 1:2], 8.0, ALU.mult, ALU.mult))
                    ps_b = ExitStack()
                    _pn = f"{l}_{half}_{esi}"
                    pA = k.es_ps(ps_b, "pA" + _pn, (128, 512))
                    pB = k.es_ps(ps_b, "pB" + _pn, (128, 512))
                    pC = k.es_ps(ps_b, "pC" + _pn, (2, 512))
                    pD = k.es_ps(ps_b, "pD" + _pn, (128, 512))
                    pE = k.es_ps(ps_b, "pE" + _pn, (128, 512))
                    for side in range(2):
                        if side == 1 and not build_q:
                            continue
                        wof = 512 if side == 0 else 0
                        tabo = 1024 if side == 0 else 0
                        for ft in range(4):
                            k.w(t, k.V, k.S, k.LDS, k.LDG)
                            for i in range(8):
                                off = 1 if i < 4 else 0
                                mm = t.matmul(pA[:, 0:512],
                                              qkw[:, 2560 * i + wof + 128 * ft:
                                                  2560 * i + wof + 128 * (ft + 1)],
                                              xslf[:, 513 * i + off:513 * i + off + 512],
                                              start=(i == 0), stop=(i == 7))
                            k.T.inc(mm)
                            for i in range(8):
                                off = 1 if i < 4 else 0
                                mm = t.matmul(pB[:, 0:512],
                                              qkw[:, 2560 * i + 1536 + wof + 128 * ft:
                                                  2560 * i + 1536 + wof + 128 * (ft + 1)],
                                              xslf[:, 513 * i + off:513 * i + off + 512],
                                              start=(i == 0), stop=(i == 7))
                            k.T.inc(mm)
                            k.w(s, k.T)
                            k.S.inc(s.activation(sqn[:, 0:512], pA[:, 0:512], AF.Square))
                            k.w(t, k.S)
                            k.T.inc(t.matmul(pC[:], hd2[:], sqn[:, 0:512],
                                             start=True, stop=True))
                            k.w(s, k.T)
                            k.S.inc(s.copy(nsb[:], pC[:]))
                            k.w(v, k.S)
                            k.V.inc(v.tensor_scalar_max(nsb[:], nsb[:], 1e-24))
                            k.w(s, k.V)
                            k.S.inc(s.activation(nsb[:], nsb[:], AF.Ln))
                            k.S.inc(s.activation(nsb[:], nsb[:], AF.Exp, scale=-0.5))
                            k.w(t, k.S)
                            k.T.inc(t.matmul(pD[:, 0:512], p2h[:], nsb[:],
                                             start=True, stop=True))
                            k.w(v, k.T)
                            v.tensor_mul(escr[:, 0:512], pA[:, 0:512],
                                         rcs[:, tabo:tabo + 512])
                            v.tensor_mul(escr[:, 512:1024], pB[:, 0:512],
                                         rcs[:, tabo + 512:tabo + 1024])
                            v.tensor_add(escr[:, 0:512], escr[:, 0:512],
                                         escr[:, 512:1024])
                            k.V.inc(v.tensor_mul(escr[:, 0:512], escr[:, 0:512],
                                                 pD[:, 0:512]))
                            for hh in range(2):
                                h = 2 * ft + hh
                                if side == 0:
                                    dstt = kat[0:64, 1536 * h + 128 * lt0:
                                               1536 * h + 128 * lt0 + 512]
                                else:
                                    dstt = qat[0:64, 512 * h:512 * (h + 1)]
                                k.V.inc(v.tensor_copy(
                                    dstt, escr[64 * hh:64 * (hh + 1), 0:512]))
                    for tt_ in range(4):
                        k.w(t, k.V, k.S, k.LDS, k.LDG)
                        for i in range(8):
                            off = 1 if i < 4 else 0
                            mm = t.matmul(pE[:, 0:512],
                                          xslf[:, 513 * i + off + 128 * tt_:
                                              513 * i + off + 128 * (tt_ + 1)],
                                          qkw[:, 2560 * i + 1024:2560 * i + 1536],
                                          start=(i == 0), stop=(i == 7))
                        k.T.inc(mm)
                        k.w(v, k.T)
                        lt = lt0 + tt_
                        dstv = vat[:, 520 * lt:520 * (lt + 1)].rearrange(
                            "p (h c) -> p h c", c=65)[:, :, 0:64]
                        v.tensor_copy(dstv,
                                      pE[:, 0:512].rearrange("p (h c) -> p h c", c=64))
                        k.V.inc(v.memset(
                            vat[:, 520 * lt:520 * (lt + 1)].rearrange(
                                "p (h c) -> p h c", c=65)[:, :, 64:65], 1.0))

                    k.barrier()
                    ps_b.close()
                    if not build_q:
                        continue
                    psa = ExitStack()
                    pSIM = k.es_ps(psa, "pSIM" + _pn, (128, 2048))
                    pST = pSIM
                    pPV = k.es_ps(psa, "pPV" + _pn, (65, 128))
                    pBC = k.es_ps(psa, "pBC" + _pn, (64, 1024))
                    for qt in range(4 * osl, 4 * osl + 4):
                        qlo = 128 * (qt % 4)
                        blo = 128 * (qt - 8 * half)
                        k.w(y, k.V, k.S, k.T)
                        k.LDS.inc(y.dma_start(out=wmk[:], in_=wmask_i[qt]), 16)
                        for u_ in range(5):
                            ci_ = CZMAP.get((qt, u_))
                            if ci_ is not None:
                                k.LDS.inc(y.dma_start(
                                    out=czq[:, 128 * u_:128 * (u_ + 1)],
                                    in_=cz_i[ci_]), 16)
                        for hp in range(4):
                            k.w(t, k.V, k.S, k.LDS)
                            for hh in range(2):
                                h = 2 * hp + hh
                                t.matmul(pSIM[:, 1024 * hh:1024 * hh + 512],
                                         qat[0:64, 512 * h + qlo:512 * h + qlo + 128],
                                         kat[0:64, 1536 * h + blo:1536 * h + blo + 512],
                                         start=True, stop=True)
                                mm2 = t.matmul(
                                    pSIM[:, 1024 * hh + 512:1024 * hh + 640],
                                    qat[0:64, 512 * h + qlo:512 * h + qlo + 128],
                                    kat[0:64, 1536 * h + blo + 512:1536 * h + blo + 640],
                                    start=True, stop=True)
                            k.T.inc(mm2)
                            k.w(v, k.T)
                            for hh in range(2):
                                v.tensor_add(pSIM[:, 1024 * hh:1024 * hh + 640],
                                             pSIM[:, 1024 * hh:1024 * hh + 640],
                                             wmk[:])
                            k.V.inc(v.tensor_reduce(
                                negma[:, 2 * hp:2 * hp + 2],
                                pSIM[:, 0:2048].rearrange(
                                    "p (a c) -> p a c", c=1024)[:, :, 0:640],
                                AX.X, ALU.max, negate=True))
                        k.w(t, k.V)
                        k.T.inc(t.transpose(pPV[0:8, 0:128], negma[:], eye_sb[:]))
                        k.w(s, k.T, k.LDS)
                        k.S.inc(s.copy(nrow8[:], pPV[0:8, 0:128]))
                        ghn, gnn = k.gate(y, k.S, k.T)
                        dd = y.dma_start(
                            out=qat[64:65, :].rearrange(
                                "p (h c) -> p h c", c=512)[:, :, qlo:qlo + 128],
                            in_=nrow8[:])
                        dd._wait_ge(ghn, gnn)
                        k.LDS.inc(dd, 16)
                        for h in range(8):
                            k.w(t, k.S, k.V, k.LDS)
                            for u in range(5):
                                mm = t.matmul(
                                    pST[:, 128 * u:128 * (u + 1)],
                                    kat[0:65, 1536 * h + blo + 128 * u:
                                        1536 * h + blo + 128 * (u + 1)],
                                    qat[0:65, 512 * h + qlo:512 * h + qlo + 128],
                                    start=(u in (0, 4)), stop=(u in (3, 4)))
                            k.T.inc(mm)
                            k.w(s, k.T)
                            k.S.inc(s.activation(ptb[:], pST[:, 0:640], AF.Exp))
                            k.w(v, k.S)
                            vv = None
                            for u in range(5):
                                ci = CZMAP.get((qt, u))
                                if ci is None:
                                    continue
                                vv = v.tensor_mul(ptb[:, 128 * u:128 * (u + 1)],
                                                  ptb[:, 128 * u:128 * (u + 1)],
                                                  czq[:, 128 * u:128 * (u + 1)])
                            k.V.inc(vv)
                            k.w(t, k.V)
                            for u in range(5):
                                lt = qt - 8 * half + u
                                mm = t.matmul(
                                    pPV[:, 0:128],
                                    vat[:, 520 * lt + 65 * h:520 * lt + 65 * (h + 1)],
                                    ptb[:, 128 * u:128 * (u + 1)],
                                    start=(u == 0), stop=(u == 4))
                            k.T.inc(mm)
                            k.w(s, k.T)
                            k.S.inc(s.copy(obuf[:, 128 * h:128 * (h + 1)], pPV[:]))
                        k.w(v, k.S)
                        k.V.inc(v.reciprocal(lrow[:], obuf[64:65, :]))
                        k.w(t, k.V)
                        k.T.inc(t.matmul(pBC[:, 0:512], onesrow[0:1, 0:64],
                                         lrow[0:1, 0:512], start=True, stop=True))
                        k.T.inc(t.matmul(pBC[:, 512:1024], onesrow[0:1, 0:64],
                                         lrow[0:1, 512:1024], start=True, stop=True))
                        k.w(v, k.T)
                        for par in range(2):
                            inap = obuf[0:64, :].rearrange(
                                "p (a b c) -> p a b c", b=2, c=128)[:, :, par, :]
                            lbap = pBC[0:64, :].rearrange(
                                "p (a b c) -> p a b c", b=2, c=128)[:, :, par, :]
                            oap = o_fm[64 * par:64 * (par + 1), :].rearrange(
                                "p (a c) -> p a c", c=512)[:, :, qlo:qlo + 128]
                            k.V.inc(v.tensor_mul(oap, inap, lbap))
                    if l == 0 and osl == 0:
                        dof = k.dbg_out("ofm0", (128, 4 * 512), BF16)
                        if dof is not None:
                            k.dump("ofm0", o_fm[:])
                        dqa = k.dbg_out("qat0", (65, 8 * 512), BF16)
                        if dqa is not None:
                            k.dump("qat0", qat[:])
                        dka = k.dbg_out("kat0", (65, 8 * 12 * 128), BF16)
                        if dka is not None:
                            k.dump("kat0", kat[:])
                        dva = k.dbg_out("vat0", (128, 12 * 520), BF16)
                        if dva is not None:
                            k.dump("vat0", vat[:])
                    psa.close()
                    ps_wo = ExitStack()
                    pWA = k.es_ps(ps_wo, "pWA" + _pn, (128, 512))
                    pWB = k.es_ps(ps_wo, "pWB" + _pn, (128, 512))
                    for m in range(8):
                        bank = pWA if m % 2 == 0 else pWB
                        k.w(t, k.V, k.S, k.LDG)
                        for i in range(4):
                            mm = t.matmul(
                                bank[:, 0:512],
                                wo_sb[:, 1024 * i + 128 * m:1024 * i + 128 * (m + 1)],
                                o_fm[:, 512 * i:512 * (i + 1)],
                                start=(i == 0), stop=(i == 3))
                        k.T.inc(mm)
                        k.w(v, k.T)
                        k.V.inc(v.tensor_add(
                            xi[:, 2049 * m + 1 + 512 * osl:2049 * m + 1 + 512 * (osl + 1)],
                            xi[:, 2049 * m + 1 + 512 * osl:2049 * m + 1 + 512 * (osl + 1)],
                            bank[:, 0:512]))
                    k.barrier()
                    ps_wo.close()

            datn = k.dbg_out(f"attn{l}", (1024, T))
            if datn is not None:
                for i in range(8):
                    k.w(k.y, k.V, k.S, k.T)
                    k.ST.inc(k.y.dma_start(out=datn[128 * i:128 * (i + 1), :],
                                           in_=xi[:, 2049 * i + 1:2049 * (i + 1)]), 16)
            k.w(y, k.V, k.S, k.T)
            for i in range(8):
                k.LDS.inc(y.dma_start(out=cc0_in[l][128 * i:128 * (i + 1), :],
                                      in_=xi[:, 2049 * i + 2048:2049 * i + 2049]), 16)
            k.w(g, k.LDS)
            k.CC.inc(g.collective_compute(
                "AllGather", ALU.bypass, [[0, 1], [2, 3], [4, 5], [6, 7]],
                [cc0_in[l].opt()], [cc0_out[l].opt()]))
            k.w(y, k.CC)
            for i in range(8):
                k.LDS.inc(y.dma_start(out=gcol[:, i:i + 1],
                                      in_=cc0_out[l][0, 128 * i:128 * (i + 1), :]), 16)
            k.w(v, k.LDS)
            k.V.inc(v.tensor_scalar_mul(gcol[:], gcol[:], gsc[:]))
            for i in range(8):
                k.V.inc(v.tensor_copy(xi[:, 2049 * i:2049 * i + 1], gcol[:, i:i + 1]))

            scope(f"s2_mlp{l}")
            k.barrier()
            es_al.close()
            es_fl = ExitStack()
            pA = k.es_ps(es_fl, f"pFA{l}", (128, 512))
            pB = k.es_ps(es_fl, f"pFB{l}", (128, 512))
            pR1 = k.es_ps(es_fl, f"pR1{l}", (1, 512))
            pR2 = k.es_ps(es_fl, f"pR2{l}", (1, 512))
            xfb = k.sb(f"xfb{l}", (128, 8 * 2048), BF16, stack=es_fl)
            lnrow2 = k.sb(f"lnrow2{l}", (1, 4 * 512), stack=es_fl)
            rowA2 = k.sb(f"rowA2{l}", (1, 512), stack=es_fl)
            rowB2 = k.sb(f"rowB2{l}", (1, 512), stack=es_fl)
            onesr512 = k.sb(f"onesr512{l}", (1, 512), stack=es_fl)
            gat22 = k.sb(f"gat22{l}", (128, 22 * 512), BF16, stack=es_fl)
            w1t = k.sb(f"w1t{l}", (128, 2 * 1024), BF16, stack=es_fl)
            w2t = k.sb(f"w2t{l}", (128, 22 * 128), BF16, stack=es_fl)
            k.V.inc(v.memset(onesr512[:], 1.0))
            for sl in (1, 2, 3, 0):
                k.w(t, k.V, k.S)
                for i in range(8):
                    off = 1 if i < 4 else 0
                    mm = t.matmul(pR1[:], onescol_f[:],
                                  xi[:, 2049 * i + off + 512 * sl:
                                      2049 * i + off + 512 * sl + 512],
                                  start=(i == 0), stop=(i == 7))
                k.T.inc(mm)
                for i in range(8):
                    off = 1 if i < 4 else 0
                    k.w(s, k.T)
                    k.S.inc(s.activation(sqn[:, 0:512],
                                         xi[:, 2049 * i + off + 512 * sl:
                                             2049 * i + off + 512 * sl + 512],
                                         AF.Square))
                    k.w(t, k.S)
                    k.T.inc(t.matmul(pR2[:], onescol_f[:], sqn[:, 0:512],
                                     start=(i == 0), stop=(i == 7)))
                MU, M2, MSQ, VAR = (lnrow2[0:1, 512 * i_:512 * (i_ + 1)]
                                    for i_ in range(4))
                k.w(s, k.T)
                k.S.inc(s.activation(MU, pR1[:], AF.Copy, scale=1.0 / 1024))
                k.S.inc(s.activation(M2, pR2[:], AF.Copy, scale=1.0 / 1024))
                k.w(v, k.S)
                k.V.inc(v.tensor_mul(MSQ, MU, MU))
                k.V.inc(v.tensor_sub(VAR, M2, MSQ))
                k.w(s, k.V)
                k.S.inc(s.activation(MSQ, VAR, AF.Ln, bias=k.constf(1e-5, 1)))
                k.S.inc(s.activation(rowA2[:], MSQ, AF.Exp, scale=-0.5))
                k.w(v, k.S)
                k.V.inc(v.tensor_mul(rowB2[:], MU, rowA2[:]))
                k.V.inc(v.tensor_scalar_mul(rowB2[:], rowB2[:], -1.0))
                for i in range(8):
                    off = 1 if i < 4 else 0
                    k.w(t, k.V, k.S)
                    k.T.inc(t.matmul(pA[:, 0:512], lgr[0:1, 128 * i:128 * (i + 1)],
                                     rowA2[:], start=True, stop=True))
                    t.matmul(pB[:, 0:512], lgr[0:1, 128 * i:128 * (i + 1)],
                             rowB2[:], start=True, stop=False)
                    k.T.inc(t.matmul(pB[:, 0:512],
                                     lgr[0:1, 1024 + 128 * i:1024 + 128 * (i + 1)],
                                     onesr512[:], start=False, stop=True))
                    k.w(v, k.T)
                    v.tensor_mul(escr[:, 0:512],
                                 xi[:, 2049 * i + off + 512 * sl:
                                     2049 * i + off + 512 * sl + 512],
                                 pA[:, 0:512])
                    k.V.inc(v.tensor_add(
                        xfb[:, 2048 * i + 512 * sl:2048 * i + 512 * (sl + 1)],
                        escr[:, 0:512], pB[:, 0:512]))

            for m in range(22):
                ghw, gnw = k.gate(g, k.T)
                for sidx, soff in ((0, 0), (1, 2816)):
                    dd = g.dma_start(
                        out=w1t[:, 1024 * sidx:1024 * (sidx + 1)].rearrange(
                            "p (a c) -> p a c", c=128),
                        in_=w1_i[l][:, soff + 128 * m:soff + 128 * (m + 1)].rearrange(
                            "(a p) c -> p a c", p=128))
                    dd._wait_ge(ghw, gnw)
                    k.LDG.inc(dd, 16)
                for sl in range(4):
                    k.w(t, k.LDG, k.V, k.S)
                    for i in range(8):
                        mm = t.matmul(pA[:, 0:512], w1t[:, 128 * i:128 * (i + 1)],
                                      xfb[:, 2048 * i + 512 * sl:2048 * i + 512 * (sl + 1)],
                                      start=(i == 0), stop=(i == 7))
                    k.T.inc(mm)
                    for i in range(8):
                        mm = t.matmul(pB[:, 0:512],
                                      w1t[:, 1024 + 128 * i:1024 + 128 * (i + 1)],
                                      xfb[:, 2048 * i + 512 * sl:2048 * i + 512 * (sl + 1)],
                                      start=(i == 0), stop=(i == 7))
                    k.T.inc(mm)
                    k.w(s, k.T)
                    k.S.inc(s.activation(ggscr[:, 0:512], pB[:, 0:512], AF.Gelu,
                                         bias=b1sb[:, 22 + m:23 + m]))
                    k.w(v, k.S, k.LDS)
                    k.V.inc(v.scalar_tensor_tensor(
                        gtile[:, 512 * (sl % 2):512 * (sl % 2) + 512],
                        pA[:, 0:512], b1sb[:, m:m + 1],
                        ggscr[:, 0:512], ALU.add, ALU.mult))
                    k.w(y, k.V)
                    k.LDS.inc(y.dma_start(
                        out=gated_hbm[sl, m],
                        in_=gtile[:, 512 * (sl % 2):512 * (sl % 2) + 512]), 16)

            for m2 in range(8):
                ghw, gnw = k.gate(g, k.T)
                dd = g.dma_start(
                    out=w2t[:].rearrange("p (a c) -> p a c", c=128),
                    in_=w2_i[l][:, 128 * m2:128 * (m2 + 1)].rearrange(
                        "(a p) c -> p a c", p=128))
                dd._wait_ge(ghw, gnw)
                k.LDG.inc(dd, 16)
                for sl in range(4):
                    k.w(y, k.T, k.V)
                    k.LDS.inc(y.dma_start(
                        out=gat22[:].rearrange("p (a c) -> p a c", c=512),
                        in_=gated_hbm[sl].transpose([1, 0, 2])), 16)
                    k.w(t, k.LDS, k.LDG, k.V)
                    for kk2 in range(22):
                        mm = t.matmul(pA[:, 0:512], w2t[:, 128 * kk2:128 * (kk2 + 1)],
                                      gat22[:, 512 * kk2:512 * (kk2 + 1)],
                                      start=(kk2 == 0), stop=(kk2 == 21))
                    k.T.inc(mm)
                    k.w(v, k.T)
                    k.V.inc(v.scalar_tensor_tensor(
                        xi[:, 2049 * m2 + 1 + 512 * sl:2049 * m2 + 1 + 512 * (sl + 1)],
                        pA[:, 0:512], b2sb[:, m2:m2 + 1],
                        xi[:, 2049 * m2 + 1 + 512 * sl:2049 * m2 + 1 + 512 * (sl + 1)],
                        ALU.add, ALU.add))

            k.barrier()
            es_fl.close()
            if l == 0:
                for i in range(8):
                    k.w(s, k.V, k.T)
                    k.S.inc(s.copy(xslf[:, 513 * i:513 * (i + 1)],
                                   xi[:, 2049 * i + 1536:2049 * i + 2049]))
                k.w(y, k.S)
                for i in range(8):
                    k.LDS.inc(y.dma_start(out=cc1_in[128 * i:128 * (i + 1), :],
                                          in_=xslf[:, 513 * i:513 * (i + 1)]), 16)
                k.w(g, k.LDS)
                k.CC.inc(g.collective_compute(
                    "AllGather", ALU.bypass, [[0, 1], [2, 3], [4, 5], [6, 7]],
                    [cc1_in.opt()], [cc1_out.opt()]))
                for seg in range(4):
                    for i in range(8):
                        k.w(s, k.V, k.LDS)
                        k.S.inc(s.copy(xslf[:, 513 * i:513 * i + 512],
                                       xi[:, 2049 * i + 1 + 512 * seg:
                                           2049 * i + 1 + 512 * (seg + 1)]))
                    k.w(y, k.S)
                    for i in range(8):
                        k.LDS.inc(y.dma_start(
                            out=xext_hbm[128 * i:128 * (i + 1),
                                         513 + 512 * seg:513 + 512 * (seg + 1)],
                            in_=xslf[:, 513 * i:513 * i + 512]), 16)
            dffn = k.dbg_out("ffn1", (1024, T)) if l == 0 else k.dbg_out("ffn2", (1024, T))
            if dffn is not None:
                for i in range(8):
                    k.w(k.y, k.V, k.S, k.T)
                    k.ST.inc(k.y.dma_start(out=dffn[128 * i:128 * (i + 1), :],
                                           in_=xi[:, 2049 * i + 1:2049 * (i + 1)]), 16)
        scope("s4_gate")
        xn1_hbm = nc.dram_tensor("xn1_hbm", [1024, T], F32, kind="Internal").ap()
        z_hbm = nc.dram_tensor("z_hbm", [1024, K], BF16, kind="Internal").ap()
        xc_hbm = nc.dram_tensor("xc_hbm", [1024, K], BF16, kind="Internal").ap()
        h_hbm = nc.dram_tensor("h_hbm", [4096, T], F32, kind="Internal").ap()
        cc2_in = nc.dram_tensor("cc2i", [1024, T], BF16, kind="Internal").ap()
        cc2_out = nc.dram_tensor("cc2o", [2, 1024, T], BF16, kind="Internal").ap()
        cc3_in = nc.dram_tensor("cc3i", [1, 4], F32, kind="Internal").ap()
        cc3_out = nc.dram_tensor("cc3o", [1, 4], F32, kind="Internal").ap()
        cc4_in = nc.dram_tensor("cc4i", [1024, K], F32, kind="Internal").ap()
        cc4_out = nc.dram_tensor("cc4o", [1024, K], F32, kind="Internal").ap()
        cc5_in = nc.dram_tensor("cc5i", [96, K], F32, kind="Internal").ap()
        cc5_out = nc.dram_tensor("cc5o", [96, K], F32, kind="Internal").ap()
        cc6_in = nc.dram_tensor("cc6i", [1, 2], F32, kind="Internal").ap()
        cc6_out = nc.dram_tensor("cc6o", [1, 2], F32, kind="Internal").ap()

        def ldb(pairs, *sems):
            gh_, gn_ = k.gate(y, k.LDS, *sems)
            for dst_, src_ in pairs:
                dd_ = y.dma_start(out=dst_, in_=src_)
                dd_._wait_ge(gh_, gn_)
                k.LDS.inc(dd_, 16)

        k.barrier()
        es_4 = ExitStack()
        x1sl = k.sb("x1sl", (128, 8 * 512), stack=es_4)
        xnsl = k.sb("xnsl", (128, 512), stack=es_4)
        xnbf = k.sb("xnbf", (128, 512), BF16, stack=es_4)
        ssb4 = k.sb("ssb4", (128, 512), BF16, stack=es_4)
        xsb4 = k.sb("xsb4", (128, 512), stack=es_4)
        lnr4 = k.sb("lnr4", (1, 4 * 512), stack=es_4)
        rA4 = k.sb("rA4", (1, 512), stack=es_4)
        rB4 = k.sb("rB4", (1, 512), stack=es_4)
        wstg = k.sb("wstg", (128, 4096), stack=es_4)
        ps4 = ExitStack()
        pR1 = k.es_ps(ps4, "pR14", (1, 512))
        pR2 = k.es_ps(ps4, "pR24", (1, 512))
        pA4 = k.es_ps(ps4, "pA44", (128, 512))
        pB4 = k.es_ps(ps4, "pB44", (128, 512))
        k.V.inc(v.memset(mxrow[:], -1e30))
        for sl in range(4):
            for i in range(8):
                ldb([(xsb4[:], xT_keys[128 * i:128 * (i + 1),
                                       2048 + 512 * sl:2048 + 512 * (sl + 1)]),
                     (ssb4[:], s_hbm[128 * i:128 * (i + 1),
                                     512 * sl:512 * (sl + 1)])],
                    k.V, k.S)
                k.w(s, k.V)
                k.S.inc(s.activation(escr[:, 0:512],
                                     xi[:, 2049 * i + 1 + 512 * sl:
                                         2049 * i + 1 + 512 * (sl + 1)], AF.Silu))
                k.w(v, k.S, k.LDS)
                v.tensor_mul(escr[:, 512:1024], ssb4[:], escr[:, 0:512])
                k.V.inc(v.tensor_add(x1sl[:, 512 * i:512 * (i + 1)],
                                     xsb4[:], escr[:, 512:1024]))
                k.w(t, k.V, k.S)
                k.T.inc(t.matmul(pR1[:], onescol_f[:],
                                 x1sl[:, 512 * i:512 * (i + 1)],
                                 start=(i == 0), stop=(i == 7)))
                k.w(s, k.T)
                k.S.inc(s.activation(sqn[:, 0:512],
                                     x1sl[:, 512 * i:512 * (i + 1)], AF.Square))
                k.w(t, k.S)
                k.T.inc(t.matmul(pR2[:], onescol_f[:], sqn[:, 0:512],
                                 start=(i == 0), stop=(i == 7)))
            MU, M2, MSQ, VAR = (lnr4[0:1, 512 * i_:512 * (i_ + 1)]
                                for i_ in range(4))
            k.w(s, k.T)
            k.S.inc(s.activation(MU, pR1[:], AF.Copy, scale=1.0 / 1024))
            k.S.inc(s.activation(M2, pR2[:], AF.Copy, scale=1.0 / 1024))
            k.w(v, k.S)
            k.V.inc(v.tensor_mul(MSQ, MU, MU))
            k.V.inc(v.tensor_sub(VAR, M2, MSQ))
            k.w(s, k.V)
            k.S.inc(s.activation(MSQ, VAR, AF.Ln, bias=k.constf(1e-5, 1)))
            k.S.inc(s.activation(rA4[:], MSQ, AF.Exp, scale=-0.5))
            k.w(v, k.S)
            k.V.inc(v.tensor_mul(rB4[:], MU, rA4[:]))
            k.V.inc(v.tensor_scalar_mul(rB4[:], rB4[:], -1.0))
            k.w(t, k.V)
            k.T.inc(t.matmul(pA4[:], onesrow[:], rA4[:], start=True, stop=True))
            k.T.inc(t.matmul(pB4[:], onesrow[:], rB4[:], start=True, stop=True))
            for i in range(8):
                k.w(v, k.T, k.LDS)
                v.tensor_mul(xnsl[:], x1sl[:, 512 * i:512 * (i + 1)], pA4[:])
                k.V.inc(v.tensor_add(xnsl[:], xnsl[:], pB4[:]))
                v.tensor_reduce(grow[:, 0:1], xnsl[:], AX.X, ALU.max)
                v.tensor_tensor(mxrow[:, 0:1], mxrow[:, 0:1], grow[:, 0:1],
                                ALU.max)
                v.tensor_reduce(grow[:, 1:2], xnsl[:], AX.X, ALU.min, negate=True)
                k.V.inc(v.tensor_tensor(mxrow[:, 1:2], mxrow[:, 1:2],
                                        grow[:, 1:2], ALU.max))
                k.w(s, k.V, k.LDS)
                k.S.inc(s.copy(xnbf[:], xnsl[:]))
                ghs, gns = k.gate(y, k.V, k.S)
                for dst, src in (
                        (x1_hbm[128 * i:128 * (i + 1), 512 * sl:512 * (sl + 1)],
                         x1sl[:, 512 * i:512 * (i + 1)]),
                        (xn1_hbm[128 * i:128 * (i + 1), 512 * sl:512 * (sl + 1)],
                         xnsl[:]),
                        (cc2_in[128 * i:128 * (i + 1), 512 * sl:512 * (sl + 1)],
                         xnbf[:])):
                    dd = y.dma_start(out=dst, in_=src)
                    dd._wait_ge(ghs, gns)
                    k.LDS.inc(dd, 16)
        dx1 = k.dbg_out("x1", (1024, T))
        if dx1 is not None:
            ghd, gnd = k.gate(y, k.LDS, k.V, k.S)
            dd = y.dma_start(out=dx1, in_=x1_hbm)
            dd._wait_ge(ghd, gnd)
            k.ST.inc(dd, 16)
        dxn1 = k.dbg_out("xn1", (1024, T))
        if dxn1 is not None:
            ghd, gnd = k.gate(y, k.LDS, k.V, k.S)
            dd = y.dma_start(out=dxn1, in_=xn1_hbm)
            dd._wait_ge(ghd, gnd)
            k.ST.inc(dd, 16)
        if _STAGES < 5:
            ps4.close()
            es_4.close()
            k.barrier()
            es_f.close()
            finalize(k, out_fm)
            return k

        k.w(t, k.V)
        k.T.inc(t.transpose(pA4[0:2, 0:128], mxrow[:], eye_sb[:]))
        k.w(v, k.T)
        k.V.inc(v.tensor_reduce(grow[0:2, 0:1], pA4[0:2, 0:128], AX.X, ALU.max))
        ldb([(wstg[:], w1slab_i)], k.V, k.S)
        k.w(v, k.LDS)
        k.V.inc(v.tensor_reduce(grow[:, 2:3], wstg[:], AX.X, ALU.max,
                                apply_absolute_value=True))
        ldb([(wstg[:], w2slab_i.rearrange("(a p) c -> p a c", p=128))], k.V)
        k.w(v, k.LDS)
        k.V.inc(v.tensor_reduce(grow[:, 3:4], wstg[:], AX.X, ALU.max,
                                apply_absolute_value=True))
        k.w(t, k.V)
        k.T.inc(t.transpose(pB4[0:2, 0:128], grow[:, 2:4], eye_sb[:]))
        k.w(v, k.T)
        k.V.inc(v.tensor_reduce(grow[0:2, 1:2], pB4[0:2, 0:128], AX.X, ALU.max))
        ghq, gnq = k.gate(y, k.V)
        for dst, src in ((qsrow[0:1, 0:1], grow[0:1, 1:2]),
                         (qsrow[0:1, 1:2], grow[1:2, 1:2]),
                         (qsrow[0:1, 2:3], grow[0:1, 0:1]),
                         (qsrow[0:1, 3:4], grow[1:2, 0:1])):
            dd = y.dma_start(out=dst, in_=src)
            dd._wait_ge(ghq, gnq)
            k.LDS.inc(dd, 16)
        ldb([(cc3_in, qsrow[0:1, 0:4])], k.V)
        k.w(g, k.LDS)
        k.CC.inc(g.collective_compute(
            "AllGather", ALU.bypass, [[0, 1], [2, 3], [4, 5], [6, 7]],
            [cc2_in.opt()], [cc2_out.opt()]))
        k.CC.inc(g.collective_compute(
            "AllReduce", ALU.max, [[0, 1, 2, 3, 4, 5, 6, 7]],
            [cc3_in.opt()], [cc3_out.opt()]))
        k.barrier()
        ps4.close()
        es_4.close()
        es_f.close()

        ldb([(scr[0:1, 0:4], cc3_out)], k.CC)
        k.w(v, k.LDS)
        v.tensor_scalar(scr[0:1, 4:5], scr[0:1, 0:1], 1.0 / 127, 1e-8,
                        ALU.mult, ALU.max)
        v.tensor_scalar(scr[0:1, 5:6], scr[0:1, 1:2], 1.0 / 127, 1e-8,
                        ALU.mult, ALU.max)
        v.reciprocal(scr[0:1, 6:7], scr[0:1, 4:5])
        v.reciprocal(scr[0:1, 7:8], scr[0:1, 5:6])
        v.tensor_tensor(scr[0:1, 8:9], scr[0:1, 2:3], scr[0:1, 3:4], ALU.add)
        v.tensor_scalar(scr[0:1, 8:9], scr[0:1, 8:9], 1.0 / 255, 1e-8,
                        ALU.mult, ALU.max)
        v.reciprocal(scr[0:1, 9:10], scr[0:1, 8:9])
        v.tensor_tensor(scr[0:1, 10:11], scr[0:1, 8:9], scr[0:1, 4:5], ALU.mult)
        k.V.inc(v.tensor_tensor(scr[0:1, 11:12], scr[0:1, 3:4], scr[0:1, 9:10],
                                ALU.mult))
        k.w(s, k.V)
        k.S.inc(s.activation(scr[0:1, 11:12], scr[0:1, 11:12], AF.Copy,
                             bias=12582912.0))
        k.w(v, k.S)
        v.tensor_scalar(scr[0:1, 11:12], scr[0:1, 11:12], 12582912.0, None,
                        ALU.subtract)
        v.tensor_scalar(scr[0:1, 12:13], scr[0:1, 11:12], -128.0, -128.0,
                        ALU.add, ALU.max)
        v.tensor_scalar(scr[0:1, 12:13], scr[0:1, 12:13], 127.0, None, ALU.min)
        v.tensor_scalar(scr[0:1, 13:14], scr[0:1, 12:13], -1.0, -128.0,
                        ALU.mult, ALU.add)
        k.V.inc(v.tensor_scalar(scr[0:1, 14:15], scr[0:1, 12:13], -1.0, 127.0,
                                ALU.mult, ALU.add))
        ghp, gnp = k.gate(y, k.V)
        for j_, sc_ in enumerate((9, 13, 14, 10, 6, 7)):
            dd = y.dma_start(out=qsrow[0:1, j_:j_ + 1], in_=scr[0:1, sc_:sc_ + 1])
            dd._wait_ge(ghp, gnp)
            k.LDS.inc(dd, 16)
        ps5q = ExitStack()
        pQ5 = k.es_ps(ps5q, "pQ5", (128, 8))
        k.w(t, k.LDS, k.S)
        k.T.inc(t.matmul(pQ5[:], onesrow[:], qsrow[:], start=True, stop=True))
        k.w(s, k.T)
        k.S.inc(s.copy(qpar[:], pQ5[:]))
        k.barrier()
        ps5q.close()

        scope("s5_ln")
        es5x = ExitStack()
        xcdt = k.sb("xcdt", (128, 8 * K), BF16, stack=es5x)
        opj_sb = k.sb("opj_sb", (128, 8 * 1024), BF16, stack=es5x)
        lnw_c = k.sb("lnw_c", (128, 8), stack=es5x)
        lnb_c = k.sb("lnb_c", (128, 8), stack=es5x)
        dcol_c = k.sb("dcol_c", (128, 8), stack=es5x)
        cb_c = k.sb("cb_c", (128, 8), stack=es5x)
        dtb_c = k.sb("dtb_c", (128, 8), stack=es5x)
        cw_c = k.sb("cw_c", (128, 32), stack=es5x)
        A_sb = k.sb("A_sb", (128, 128), stack=es5x)
        carry_all = k.sb("carry_all", (128, 128), stack=es5x)
        eyeb_sb = k.sb("eyeb_sb", (128, 128), BF16, stack=es5x)
        ldb([(lnw_c[:], lnw_i), (lnb_c[:], lnb_i), (dcol_c[:], dcol_i),
             (cb_c[:], cbq_i), (dtb_c[:], dtb_i), (cw_c[:], cw_i),
             (A_sb[:], alog_i), (eyeb_sb[:], eyeb_i), (g5_sb[:], g5_i)],
            k.V, k.S, k.T)
        k.w(s, k.LDS)
        k.S.inc(s.activation(A_sb[:], A_sb[:], AF.Exp))
        k.w(v, k.S)
        k.V.inc(v.tensor_scalar_mul(A_sb[:], A_sb[:], -1.0))
        es5g = ExitStack()
        xg = k.sb("xg", (128, 8 * K), BF16, stack=es5g)
        ghg, gng = k.gate(y, k.CC, k.V, k.S, k.T)
        for i in range(8):
            for pp in range(2):
                dd = y.dma_start(
                    out=xg[:, K * i + T * pp:K * i + T * (pp + 1)],
                    in_=cc2_out[pp, 128 * i:128 * (i + 1), :])
                dd._wait_ge(ghg, gng)
                k.LDS.inc(dd, 16)
        es5g1 = ExitStack()
        sq5 = k.sb("sq5", (128, 512), stack=es5g1)
        lnr5 = k.sb("lnr5", (1, 4 * 512), stack=es5g1)
        rA5 = k.sb("rA5", (1, 512), stack=es5g1)
        rB5 = k.sb("rB5", (1, 512), stack=es5g1)
        xesc = k.sb("xesc", (128, 512), stack=es5g1)
        ps5 = ExitStack()
        pR1b = k.es_ps(ps5, "pR1b", (1, 512))
        pR2b = k.es_ps(ps5, "pR2b", (1, 512))
        pA5 = k.es_ps(ps5, "pA5", (128, 512))
        pB5 = k.es_ps(ps5, "pB5", (128, 512))
        for ch in range(8):
            k.w(t, k.V, k.S, k.LDS)
            for i in range(8):
                mm = t.matmul(pR1b[:], onescol_b[:],
                              xg[:, K * i + 512 * ch:K * i + 512 * (ch + 1)],
                              start=(i == 0), stop=(i == 7))
            k.T.inc(mm)
            for i in range(8):
                k.w(s, k.T)
                k.S.inc(s.activation(sq5[:],
                                     xg[:, K * i + 512 * ch:K * i + 512 * (ch + 1)],
                                     AF.Square))
                k.w(t, k.S)
                k.T.inc(t.matmul(pR2b[:], onescol_f[:], sq5[:],
                                 start=(i == 0), stop=(i == 7)))
            MU, M2, MSQ, VAR = (lnr5[0:1, 512 * i_:512 * (i_ + 1)]
                                for i_ in range(4))
            k.w(s, k.T)
            k.S.inc(s.activation(MU, pR1b[:], AF.Copy, scale=1.0 / 1024))
            k.S.inc(s.activation(M2, pR2b[:], AF.Copy, scale=1.0 / 1024))
            k.w(v, k.S)
            k.V.inc(v.tensor_mul(MSQ, MU, MU))
            k.V.inc(v.tensor_sub(VAR, M2, MSQ))
            k.w(s, k.V)
            k.S.inc(s.activation(MSQ, VAR, AF.Ln, bias=k.constf(1e-5, 1)))
            k.S.inc(s.activation(rA5[:], MSQ, AF.Exp, scale=-0.5))
            k.w(v, k.S)
            k.V.inc(v.tensor_mul(rB5[:], MU, rA5[:]))
            k.V.inc(v.tensor_scalar_mul(rB5[:], rB5[:], -1.0))
            k.w(t, k.V)
            k.T.inc(t.matmul(pA5[:], onesrow[:], rA5[:], start=True, stop=True))
            k.T.inc(t.matmul(pB5[:], onesrow[:], rB5[:], start=True, stop=True))
            for i in range(8):
                k.w(v, k.T)
                v.tensor_mul(xesc[:],
                             xg[:, K * i + 512 * ch:K * i + 512 * (ch + 1)],
                             pA5[:])
                v.tensor_add(xesc[:], xesc[:], pB5[:])
                k.V.inc(v.tensor_scalar(
                    xg[:, K * i + 512 * ch:K * i + 512 * (ch + 1)], xesc[:],
                    lnw_c[:, i:i + 1], lnb_c[:, i:i + 1], ALU.mult, ALU.add))
        k.barrier()
        ps5.close()
        es5g1.close()
        dxln = k.dbg_out("xln", (1024, K), BF16)
        if dxln is not None:
            for i in range(8):
                k.dump("xln", xg[:, K * i:K * (i + 1)],
                       col=(128 * i, 128 * (i + 1)))

        scope("s5_inproj")
        es5g2 = ExitStack()
        ipt = [k.sb(f"ipt{p_}", (128, 8 * 128), BF16, stack=es5g2)
               for p_ in range(2)]
        xitmp = k.sb("xitmp", (128, 3 + K), stack=es5g2)
        cvt = k.sb("cvt", (128, K), stack=es5g2)
        szt = k.sb("szt", (128, 512), BF16, stack=es5g2)
        ps5i = ExitStack()
        pIP = [k.es_ps(ps5i, f"pIP{p_}", (128, 512)) for p_ in range(2)]
        k.V.inc(v.memset(xitmp[:, 0:3], 0.0))
        for co in range(16):
            pp = co % 2
            ghw, gnw = k.gate(g, k.T)
            for i in range(8):
                dd = g.dma_start(out=ipt[pp][:, 128 * i:128 * (i + 1)],
                                 in_=inpj_i[128 * i:128 * (i + 1),
                                            128 * co:128 * (co + 1)])
                dd._wait_ge(ghw, gnw)
                k.LDG.inc(dd, 16)
            for ch in range(8):
                pI_ = pIP[ch % 2]
                k.w(t, k.LDG, k.S)
                for i in range(8):
                    mm = t.matmul(pI_[:], ipt[pp][:, 128 * i:128 * (i + 1)],
                                  xg[:, K * i + 512 * ch:K * i + 512 * (ch + 1)],
                                  start=(i == 0), stop=(i == 7))
                k.T.inc(mm)
                if co < 8:
                    k.w(s, k.T, k.V)
                    k.S.inc(s.copy(xitmp[:, 3 + 512 * ch:3 + 512 * (ch + 1)],
                                   pI_[:]))
                else:
                    k.w(s, k.T, k.LDS)
                    k.S.inc(s.activation(szt[:], pI_[:], AF.Silu))
                    ghz, gnz = k.gate(y, k.S)
                    dd = y.dma_start(
                        out=z_hbm[128 * (co - 8):128 * (co - 7),
                                  512 * ch:512 * (ch + 1)], in_=szt[:])
                    dd._wait_ge(ghz, gnz)
                    k.LDS.inc(dd, 16)
            if co < 8:
                k.w(v, k.S, k.LDS)
                v.tensor_scalar(cvt[:], xitmp[:, 3:3 + K],
                                cw_c[:, 4 * co + 3:4 * co + 4], None, ALU.mult)
                for s_ in (1, 2, 3):
                    vv = v.scalar_tensor_tensor(
                        cvt[:], xitmp[:, 3 - s_:3 - s_ + K],
                        cw_c[:, 4 * co + 3 - s_:4 * co + 4 - s_], cvt[:],
                        ALU.mult, ALU.add)
                k.V.inc(vv)
                k.w(s, k.V)
                k.S.inc(s.activation(xcdt[:, K * co:K * (co + 1)], cvt[:],
                                     AF.Silu, bias=cb_c[:, co:co + 1]))
                ghx5, gnx5 = k.gate(y, k.S)
                dd = y.dma_start(out=xc_hbm[128 * co:128 * (co + 1), :],
                                 in_=xcdt[:, K * co:K * (co + 1)])
                dd._wait_ge(ghx5, gnx5)
                k.LDS.inc(dd, 16)
        k.barrier()
        ps5i.close()
        es5g2.close()
        es5g.close()

        scope("s5_dbc")
        es5bc = ExitStack()
        bcbf = k.sb("bcbf", (32, K), BF16, stack=es5bc)
        esel_sb = k.sb("esel_sb", (32, 4096), BF16, stack=es5bc)
        ldb([(esel_sb[:], esel_i)], k.V, k.S, k.T)
        es5d = ExitStack()
        dbc_sb = k.sb("dbc_sb", (96, K), stack=es5d)
        dtr_bf = k.sb("dtr_bf", (64, K), BF16, stack=es5d)
        dbst = k.sb("dbst", (96, 512), stack=es5d)
        xpj_sb = k.sb("xpj_sb", (128, 8 * 96), BF16, stack=es5d)
        dtw_sb = k.sb("dtw_sb", (64, 8 * 128), BF16, stack=es5d)
        dtt1 = k.sb("dtt1", (128, 512), stack=es5d)
        dtt2 = k.sb("dtt2", (128, 512), stack=es5d)
        ghw, gnw = k.gate(g, k.T, k.V, k.S)
        for i in range(8):
            for dst, src in ((xpj_sb[:, 96 * i:96 * (i + 1)],
                              xpj_i[128 * i:128 * (i + 1), :]),
                             (opj_sb[:, 1024 * i:1024 * (i + 1)],
                              opj_i[128 * i:128 * (i + 1), :]),
                             (dtw_sb[:, 128 * i:128 * (i + 1)],
                              dtw_i[:, 128 * i:128 * (i + 1)])):
                dd = g.dma_start(out=dst, in_=src)
                dd._wait_ge(ghw, gnw)
                k.LDG.inc(dd, 16)
        ps5d = ExitStack()
        pDB = [k.es_ps(ps5d, f"pDB{p_}", (96, 512)) for p_ in range(2)]
        for ch in range(8):
            pD_ = pDB[ch % 2]
            k.w(t, k.S, k.LDG)
            for i in range(8):
                mm = t.matmul(pD_[:], xpj_sb[:, 96 * i:96 * (i + 1)],
                              xcdt[:, K * i + 512 * ch:K * i + 512 * (ch + 1)],
                              start=(i == 0), stop=(i == 7))
            k.T.inc(mm)
            k.w(s, k.T, k.LDS)
            k.S.inc(s.copy(dbst[:], pD_[:]))
            ghb, gnb = k.gate(y, k.S)
            dd = y.dma_start(out=cc5_in[:, 512 * ch:512 * (ch + 1)], in_=dbst[:])
            dd._wait_ge(ghb, gnb)
            k.LDS.inc(dd, 16)
        k.w(g, k.LDS)
        k.CC.inc(g.collective_compute(
            "AllReduce", ALU.add, [[0, 1], [2, 3], [4, 5], [6, 7]],
            [cc5_in.opt()], [cc5_out.opt()]))
        ldb([(dbc_sb[:], cc5_out)], k.CC)
        k.w(s, k.LDS)
        k.S.inc(s.copy(bcbf[:], dbc_sb[64:96, :]))
        k.S.inc(s.copy(dtr_bf[:], dbc_sb[0:64, :]))
        ddbc = k.dbg_out("dbc", (96, K))
        if ddbc is not None:
            k.dump("dbc", dbc_sb[:])
        dxc = k.dbg_out("xc", (1024, K), BF16)
        if dxc is not None:
            ghd, gnd = k.gate(y, k.LDS, k.V, k.S)
            dd = y.dma_start(out=dxc, in_=xc_hbm)
            dd._wait_ge(ghd, gnd)
            k.ST.inc(dd, 16)

        scope("s5_dt")
        pDT = [k.es_ps(ps5d, f"pDT{p_}", (128, 512)) for p_ in range(2)]
        k.w(v, k.T, k.LDS)
        for c_ in range(8):
            for ch in range(8):
                pD_ = pDT[ch % 2]
                k.w(t, k.S, k.V)
                k.T.inc(t.matmul(pD_[:], dtw_sb[:, 128 * c_:128 * (c_ + 1)],
                                 dtr_bf[:, 512 * ch:512 * (ch + 1)],
                                 start=True, stop=True))
                k.w(s, k.T, k.V)
                s.activation(dtt1[:], pD_[:], AF.Relu, bias=dtb_c[:, c_:c_ + 1])
                s.activation(dtt2[:], pD_[:], AF.Abs, bias=dtb_c[:, c_:c_ + 1])
                s.activation(dtt2[:], dtt2[:], AF.Exp, scale=-1.0)
                k.S.inc(s.activation(dtt2[:], dtt2[:], AF.Ln,
                                     bias=k.constf(1.0, 128)))
                k.w(v, k.S)
                k.V.inc(v.tensor_add(
                    xcdt[:, K * c_ + 512 * ch:K * c_ + 512 * (ch + 1)],
                    dtt1[:], dtt2[:]))
        ddt = k.dbg_out("dt", (1024, K), BF16)
        if ddt is not None:
            for i in range(8):
                k.dump("dt", xcdt[:, K * i:K * (i + 1)],
                       col=(128 * i, 128 * (i + 1)))
        k.barrier()
        ps5d.close()
        es5d.close()
        if _STAGES < 6:
            finalize(k, out_fm)
            return k

        scope("s5_scan")
        es5s = ExitStack()
        Bq = k.sb("Bq", (128, 16 * 1024), BF16, stack=es5s)
        Cq = k.sb("Cq", (128, 16 * 1024), BF16, stack=es5s)
        abuf = [k.sb("abuf0", (128, 1024), stack=es5s)] * 2
        bbuf = k.sb("bbuf", (128, 1024), stack=es5s)
        hbuf = k.sb("hbuf", (128, 1024), stack=es5s)
        hcb = [k.sb(f"hcb{p_}", (128, 1024), BF16, stack=es5s)
               for p_ in range(2)]
        wqc = k.sb("wqc", (128, 1024), BF16, stack=es5s)
        xcrl = k.sb("xcrl", (128, 1024), BF16, stack=es5s)
        szrl = k.sb("szrl", (128, 1024), BF16, stack=es5s)
        yq = k.sb("yq", (128, 8 * 1024), BF16, stack=es5s)
        s1st = k.sb("s1st", (128, 512), stack=es5s)
        ps5s = ExitStack()
        pBC = k.es_ps(ps5s, "pBC", (128, 1024))
        pYS = k.es_ps(ps5s, "pYS", (128, 1024))
        pOP = [k.es_ps(ps5s, f"pOP{p_}", (128, 512)) for p_ in range(2)]
        k.V.inc(v.memset(carry_all[:], 0.0))
        ab_g = [0, 0]
        hc_g = [0, 0]
        for q in range(4):
            for n in range(16):
                k.w(t, k.S, k.V)
                for half_ in range(2):
                    mm = t.matmul(pBC[:, 512 * half_:512 * (half_ + 1)],
                                  esel_sb[:, 128 * n:128 * (n + 1)],
                                  bcbf[:, 1024 * q + 512 * half_:
                                       1024 * q + 512 * (half_ + 1)],
                                  start=True, stop=True)
                k.T.inc(mm)
                k.w(s, k.T)
                k.S.inc(s.copy(Bq[:, 1024 * n:1024 * (n + 1)], pBC[:]))
                k.w(t, k.S)
                for half_ in range(2):
                    mm = t.matmul(pBC[:, 512 * half_:512 * (half_ + 1)],
                                  esel_sb[:, 2048 + 128 * n:2048 + 128 * (n + 1)],
                                  bcbf[:, 1024 * q + 512 * half_:
                                       1024 * q + 512 * (half_ + 1)],
                                  start=True, stop=True)
                k.T.inc(mm)
                k.w(s, k.T)
                k.S.inc(s.copy(Cq[:, 1024 * n:1024 * (n + 1)], pBC[:]))
            for c_ in range(8):
                ldb([(xcrl[:], xc_hbm[128 * c_:128 * (c_ + 1),
                                      1024 * q:1024 * (q + 1)]),
                     (szrl[:], z_hbm[128 * c_:128 * (c_ + 1),
                                     1024 * q:1024 * (q + 1)])],
                    k.V, k.T)
                k.w(v, k.LDS, k.S)
                k.V.inc(v.tensor_mul(
                    wqc[:], xcdt[:, K * c_ + 1024 * q:K * c_ + 1024 * (q + 1)],
                    xcrl[:]))
                for n in range(16):
                    par = n % 2
                    ab, hc = abuf[par], hcb[par]
                    k.wn(s, k.V, ab_g[par])
                    k.S.inc(s.activation(
                        ab[:], xcdt[:, K * c_ + 1024 * q:K * c_ + 1024 * (q + 1)],
                        AF.Exp, scale=A_sb[:, 16 * c_ + n:16 * c_ + n + 1]))
                    k.wn(v, k.S, k.S.n)
                    k.wn(v, k.T, hc_g[par])
                    v.tensor_mul(bbuf[:], wqc[:], Bq[:, 1024 * n:1024 * (n + 1)])
                    init = carry_all[:, 16 * c_ + n:16 * c_ + n + 1] if q else 0.0
                    v.tensor_tensor_scan(hbuf[:], ab[:], bbuf[:], init,
                                         ALU.mult, ALU.add)
                    v.tensor_copy(carry_all[:, 16 * c_ + n:16 * c_ + n + 1],
                                  hbuf[:, 1023:1024])
                    k.V.inc(v.tensor_mul(hc[:], hbuf[:],
                                         Cq[:, 1024 * n:1024 * (n + 1)]))
                    ab_g[par] = k.V.n
                    k.wn(t, k.V, k.V.n)
                    for half_ in range(2):
                        mm = t.matmul(pYS[:, 512 * half_:512 * (half_ + 1)],
                                      eyeb_sb[:],
                                      hc[:, 512 * half_:512 * (half_ + 1)],
                                      start=(n == 0), stop=(n == 15))
                    k.T.inc(mm)
                    hc_g[par] = k.T.n
                k.w(v, k.T, k.LDS)
                v.scalar_tensor_tensor(bbuf[:], xcrl[:], dcol_c[:, c_:c_ + 1],
                                       pYS[:], ALU.mult, ALU.add)
                k.V.inc(v.tensor_mul(yq[:, 1024 * c_:1024 * (c_ + 1)],
                                     bbuf[:], szrl[:]))
            for e_ in range(8):
                for half_ in range(2):
                    pO_ = pOP[half_]
                    k.w(t, k.V, k.S)
                    for c_ in range(8):
                        mm = t.matmul(
                            pO_[:],
                            opj_sb[:, 1024 * c_ + 128 * e_:
                                   1024 * c_ + 128 * (e_ + 1)],
                            yq[:, 1024 * c_ + 512 * half_:
                               1024 * c_ + 512 * (half_ + 1)],
                            start=(c_ == 0), stop=(c_ == 7))
                    k.T.inc(mm)
                    k.w(s, k.T, k.LDS)
                    k.S.inc(s.copy(s1st[:], pO_[:]))
                    gho, gno = k.gate(y, k.S)
                    dd = y.dma_start(
                        out=cc4_in[128 * e_:128 * (e_ + 1),
                                   1024 * q + 512 * half_:
                                   1024 * q + 512 * (half_ + 1)],
                        in_=s1st[:])
                    dd._wait_ge(gho, gno)
                    k.LDS.inc(dd, 16)
        k.w(g, k.LDS)
        k.CC.inc(g.collective_compute(
            "AllReduce", ALU.add, [[0, 1], [2, 3], [4, 5], [6, 7]],
            [cc4_in.opt()], [cc4_out.opt()]))
        k.barrier()
        ps5s.close()
        es5s.close()
        es5bc.close()
        es5x.close()
        ds1 = k.dbg_out("s1", (1024, T))
        if _STAGES < 7:
            if ds1 is not None:
                es_d7 = ExitStack()
                s1t_ = k.sb("s1t_", (128, 512), stack=es_d7)
                ra_ = k.sb("ra_", (128, 512), stack=es_d7)
                rb_ = k.sb("rb_", (128, 512), stack=es_d7)
                for e_ in range(8):
                    for ch in range(4):
                        ldb([(ra_[:], cc4_out[128 * e_:128 * (e_ + 1),
                                             512 * ch:512 * (ch + 1)]),
                             (rb_[:], cc4_out[128 * e_:128 * (e_ + 1),
                                             T + 512 * ch:T + 512 * (ch + 1)]),
                             (s1t_[:], xn1_hbm[128 * e_:128 * (e_ + 1),
                                               512 * ch:512 * (ch + 1)])],
                            k.V, k.S, k.CC)
                        k.w(v, k.LDS)
                        v.tensor_scalar(ra_[:], ra_[:], g5_sb[:, 1:2], None,
                                        ALU.mult)
                        v.scalar_tensor_tensor(ra_[:], rb_[:], g5_sb[:, 0:1],
                                               ra_[:], ALU.mult, ALU.add)
                        k.V.inc(v.tensor_add(s1t_[:], s1t_[:], ra_[:]))
                        k.w(y, k.V)
                        k.ST.inc(y.dma_start(
                            out=ds1[128 * e_:128 * (e_ + 1),
                                    512 * ch:512 * (ch + 1)],
                            in_=s1t_[:]), 16)
                k.barrier()
                es_d7.close()
            finalize(k, out_fm)
            return k

        scope("s5_mlp1")
        es7 = ExitStack()
        wq1 = k.sb("wq1", (128, 8 * 4096), BF16, stack=es7)
        qz = k.sb("qz", (128, 8 * T), BF16, stack=es7)
        wstg7 = k.sb("wstg7", (128, 4096), stack=es7)
        xnrl = k.sb("xnrl", (128, T), stack=es7)
        qtmp = k.sb("qtmp", (128, T), stack=es7)
        h2t = k.sb("h2t", (128, 512), stack=es7)
        hmx = k.sb("hmx", (128, 2), stack=es7)
        b1q_sb = k.sb("b1q_sb", (128, 32), stack=es7)
        b2q_sb = k.sb("b2q_sb", (128, 8), stack=es7)
        ps7 = ExitStack()
        pH = [k.es_ps(ps7, f"pH{p_}", (128, 512)) for p_ in range(2)]
        ldb([(b1q_sb[:], b1q_i), (b2q_sb[:], b2q_i)], k.V, k.S, k.T)
        C128 = k.constf(12582912.0, 128)
        for i in range(8):
            ldb([(xnrl[:], xn1_hbm[128 * i:128 * (i + 1), :])], k.V, k.S)
            k.w(s, k.LDS, k.V)
            k.S.inc(s.activation(qtmp[:], xnrl[:], AF.Copy,
                                 scale=qpar[:, 0:1], bias=12582912.0))
            k.w(v, k.S)
            v.tensor_scalar(qtmp[:], qtmp[:], 12582912.0, qpar[:, 1:2],
                            ALU.subtract, ALU.max)
            k.V.inc(v.tensor_scalar(qz[:, T * i:T * (i + 1)], qtmp[:],
                                    qpar[:, 2:3], None, ALU.min))
        for ds_ in range(8):
            ldb([(wstg7[:], w1t_i[128 * ds_:128 * (ds_ + 1), :])], k.V, k.S)
            k.w(s, k.LDS, k.V)
            k.S.inc(s.activation(wstg7[:], wstg7[:], AF.Copy,
                                 scale=qpar[:, 4:5], bias=12582912.0))
            k.w(v, k.S)
            v.tensor_scalar(wstg7[:], wstg7[:], 12582912.0, -127.0,
                            ALU.subtract, ALU.max)
            k.V.inc(v.tensor_scalar(wq1[:, 4096 * ds_:4096 * (ds_ + 1)],
                                    wstg7[:], 127.0, None, ALU.min))
        k.V.inc(v.memset(hmx[:], -1e30))
        for m in range(32):
            for ch in range(4):
                pH_ = pH[ch % 2]
                k.w(t, k.V, k.S)
                for d_ in range(8):
                    mm = t.matmul(pH_[:],
                                  wq1[:, 4096 * d_ + 128 * m:
                                      4096 * d_ + 128 * (m + 1)],
                                  qz[:, T * d_ + 512 * ch:T * d_ + 512 * (ch + 1)],
                                  start=(d_ == 0), stop=(d_ == 7))
                k.T.inc(mm)
                k.w(s, k.T, k.LDS, k.V)
                k.S.inc(s.activation(h2t[:], pH_[:], AF.Gelu,
                                     scale=qpar[:, 3:4], bias=b1q_sb[:, m:m + 1]))
                k.w(v, k.S)
                v.tensor_reduce(grow[:, 0:1], h2t[:], AX.X, ALU.max)
                v.tensor_tensor(hmx[:, 0:1], hmx[:, 0:1], grow[:, 0:1], ALU.max)
                v.tensor_reduce(grow[:, 1:2], h2t[:], AX.X, ALU.min, negate=True)
                k.V.inc(v.tensor_tensor(hmx[:, 1:2], hmx[:, 1:2], grow[:, 1:2],
                                        ALU.max))
                ghh, gnh = k.gate(y, k.S)
                dd = y.dma_start(out=h_hbm[128 * m:128 * (m + 1),
                                           512 * ch:512 * (ch + 1)], in_=h2t[:])
                dd._wait_ge(ghh, gnh)
                k.LDS.inc(dd, 16)
        k.w(t, k.V)
        k.T.inc(t.transpose(pH[0][0:2, 0:128], hmx[:], eye_sb[:]))
        k.w(v, k.T)
        k.V.inc(v.tensor_reduce(grow[0:2, 0:1], pH[0][0:2, 0:128], AX.X, ALU.max))
        ghq, gnq = k.gate(y, k.V)
        for dst, src in ((q2row[0:1, 0:1], grow[0:1, 0:1]),
                         (q2row[0:1, 1:2], grow[1:2, 0:1])):
            dd = y.dma_start(out=dst, in_=src)
            dd._wait_ge(ghq, gnq)
            k.LDS.inc(dd, 16)
        ldb([(cc6_in, q2row[0:1, 0:2])], k.V)
        k.w(g, k.LDS)
        k.CC.inc(g.collective_compute(
            "AllReduce", ALU.max, [[0, 1, 2, 3, 4, 5, 6, 7]],
            [cc6_in.opt()], [cc6_out.opt()]))
        ldb([(scr[0:1, 2:4], cc6_out)], k.CC)
        k.w(v, k.LDS)
        v.tensor_tensor(scr[0:1, 8:9], scr[0:1, 2:3], scr[0:1, 3:4], ALU.add)
        v.tensor_scalar(scr[0:1, 8:9], scr[0:1, 8:9], 1.0 / 255, 1e-8,
                        ALU.mult, ALU.max)
        v.reciprocal(scr[0:1, 9:10], scr[0:1, 8:9])
        v.tensor_tensor(scr[0:1, 10:11], scr[0:1, 8:9], scr[0:1, 5:6], ALU.mult)
        k.V.inc(v.tensor_tensor(scr[0:1, 11:12], scr[0:1, 3:4], scr[0:1, 9:10],
                                ALU.mult))
        k.w(s, k.V)
        k.S.inc(s.activation(scr[0:1, 11:12], scr[0:1, 11:12], AF.Copy,
                             bias=12582912.0))
        k.w(v, k.S)
        v.tensor_scalar(scr[0:1, 11:12], scr[0:1, 11:12], 12582912.0, None,
                        ALU.subtract)
        v.tensor_scalar(scr[0:1, 12:13], scr[0:1, 11:12], -128.0, -128.0,
                        ALU.add, ALU.max)
        v.tensor_scalar(scr[0:1, 12:13], scr[0:1, 12:13], 127.0, None, ALU.min)
        v.tensor_scalar(scr[0:1, 13:14], scr[0:1, 12:13], -1.0, -128.0,
                        ALU.mult, ALU.add)
        k.V.inc(v.tensor_scalar(scr[0:1, 14:15], scr[0:1, 12:13], -1.0, 127.0,
                                ALU.mult, ALU.add))
        ghp, gnp = k.gate(y, k.V)
        for j_, sc_ in enumerate((9, 13, 14, 10)):
            dd = y.dma_start(out=q2row[0:1, j_:j_ + 1],
                             in_=scr[0:1, sc_:sc_ + 1])
            dd._wait_ge(ghp, gnp)
            k.LDS.inc(dd, 16)
        k.w(t, k.LDS, k.S)
        k.T.inc(t.matmul(pH[0][0:128, 0:4], onesrow[:], q2row[0:1, 0:4],
                         start=True, stop=True))
        k.w(s, k.T)
        k.S.inc(s.copy(qpar2[:], pH[0][0:128, 0:4]))
        k.barrier()
        ps7.close()
        es7.close()

        scope("s5_mlp2")
        es8 = ExitStack()
        wq2 = k.sb("wq2", (128, 32 * 1024), BF16, stack=es8)
        qz2c = k.sb("qz2c", (128, 32 * 512), BF16, stack=es8)
        wstg8 = k.sb("wstg8", (128, 1024), stack=es8)
        hrl = k.sb("hrl", (128, 512), stack=es8)
        qt8 = k.sb("qt8", (128, 512), stack=es8)
        g1t = k.sb("g1t", (128, 512), stack=es8)
        s1t = k.sb("s1t", (128, 512), stack=es8)
        x1t = k.sb("x1t", (128, 512), stack=es8)
        rbt = k.sb("rbt", (128, 512), stack=es8)
        ot = k.sb("ot", (128, 512), stack=es8)
        ps8 = ExitStack()
        pG = [k.es_ps(ps8, f"pG{p_}", (128, 512)) for p_ in range(2)]
        C128b = k.constf(12582912.0, 128)
        for ms in range(32):
            ldb([(wstg8[:], w2t_i[128 * ms:128 * (ms + 1), :])], k.V, k.S)
            k.w(s, k.LDS, k.V)
            k.S.inc(s.activation(wstg8[:], wstg8[:], AF.Copy,
                                 scale=qpar[:, 5:6], bias=12582912.0))
            k.w(v, k.S)
            v.tensor_scalar(wstg8[:], wstg8[:], 12582912.0, -127.0,
                            ALU.subtract, ALU.max)
            k.V.inc(v.tensor_scalar(wq2[:, 1024 * ms:1024 * (ms + 1)],
                                    wstg8[:], 127.0, None, ALU.min))
        dg1 = k.dbg_out("g1", (1024, T))
        for ch in range(4):
            for m in range(32):
                ldb([(hrl[:], h_hbm[128 * m:128 * (m + 1),
                                    512 * ch:512 * (ch + 1)])], k.V, k.S)
                k.w(s, k.LDS, k.V)
                k.S.inc(s.activation(qt8[:], hrl[:], AF.Copy,
                                     scale=qpar2[:, 0:1], bias=12582912.0))
                k.w(v, k.S)
                v.tensor_scalar(qt8[:], qt8[:], 12582912.0, qpar2[:, 1:2],
                                ALU.subtract, ALU.max)
                k.V.inc(v.tensor_scalar(qz2c[:, 512 * m:512 * (m + 1)], qt8[:],
                                        qpar2[:, 2:3], None, ALU.min))
            for e_ in range(8):
                pG_ = pG[e_ % 2]
                k.w(t, k.V, k.S)
                for m in range(32):
                    mm = t.matmul(pG_[:],
                                  wq2[:, 1024 * m + 128 * e_:
                                      1024 * m + 128 * (e_ + 1)],
                                  qz2c[:, 512 * m:512 * (m + 1)],
                                  start=(m == 0), stop=(m == 31))
                k.T.inc(mm)
                k.w(s, k.T, k.V, k.ST)
                k.S.inc(s.activation(g1t[:], pG_[:], AF.Silu,
                                     scale=qpar2[:, 3:4], bias=b2q_sb[:, e_:e_ + 1]))
                ldb([(x1t[:], x1_hbm[128 * e_:128 * (e_ + 1),
                                     512 * ch:512 * (ch + 1)]),
                     (s1t[:], xn1_hbm[128 * e_:128 * (e_ + 1),
                                      512 * ch:512 * (ch + 1)]),
                     (rbt[:], cc4_out[128 * e_:128 * (e_ + 1),
                                      T + 512 * ch:T + 512 * (ch + 1)]),
                     (ot[:], cc4_out[128 * e_:128 * (e_ + 1),
                                     512 * ch:512 * (ch + 1)])],
                    k.V, k.S, k.CC, k.ST)
                k.w(v, k.LDS, k.S)
                v.tensor_scalar(ot[:], ot[:], g5_sb[:, 1:2], None, ALU.mult)
                v.scalar_tensor_tensor(ot[:], rbt[:], g5_sb[:, 0:1], ot[:],
                                       ALU.mult, ALU.add)
                v.tensor_add(s1t[:], s1t[:], ot[:])
                v.tensor_mul(ot[:], s1t[:], g1t[:])
                k.V.inc(v.tensor_add(ot[:], ot[:], x1t[:]))
                k.w(y, k.V)
                k.ST.inc(y.dma_start(
                    out=out_fm[128 * e_:128 * (e_ + 1), 512 * ch:512 * (ch + 1)],
                    in_=ot[:]), 16)
                if ds1 is not None:
                    k.w(y, k.V)
                    k.ST.inc(y.dma_start(
                        out=ds1[128 * e_:128 * (e_ + 1),
                                512 * ch:512 * (ch + 1)], in_=s1t[:]), 16)
                if dg1 is not None:
                    k.w(y, k.V, k.S)
                    k.ST.inc(y.dma_start(
                        out=dg1[128 * e_:128 * (e_ + 1),
                                512 * ch:512 * (ch + 1)], in_=g1t[:]), 16)
        scope(None)
        k.barrier()
        ps8.close()
        es8.close()
        k.y.wait_ge(k.ST.h, k.ST.n)
        k.g.wait_ge(k.ST.h, k.ST.n)
        k.flush()
        return k


def finalize(k, out_fm, src=None):
    k.barrier()
    if src is None:
        z = k.sb("zfin", (128, T))
        k.V.inc(k.v.memset(z[:], 0.0))
        k.w(k.y, k.V)
        src = [z[:] for _ in range(8)]
    for i in range(8):
        k.ST.inc(k.y.dma_start(out=out_fm[128 * i:128 * (i + 1), :], in_=src[i]), 16)
    k.y.wait_ge(k.ST.h, k.ST.n)
    k.g.wait_ge(k.ST.h, k.ST.n)
    k.flush()



def _rotary_tables():
    DH, WINDOW = 64, 512
    half = np.arange(0, DH, 2, dtype=np.float32)
    inv = 1.0 / 10000.0 ** (half / DH)
    tpos = np.arange(K, dtype=np.float32)
    fr = np.concatenate([tpos[:, None] * inv[None, :]] * 2, -1)
    base = (half + 0.4 * DH) / (1.4 * DH)
    power = (tpos - K // 2) / float(WINDOW // 2)
    sc = np.concatenate([base[None, :] ** power[:, None]] * 2, -1)
    cos, sin = np.cos(fr), np.sin(fr)
    sgn = np.concatenate([-np.ones(DH // 2, np.float32),
                          np.ones(DH // 2, np.float32)])
    c1q = (cos * sc).astype(np.float32)
    c2q = (sin * sc * sgn[None, :]).astype(np.float32)
    c1k = (cos / sc).astype(np.float32)
    c2k = (sin / sc * sgn[None, :]).astype(np.float32)
    return c1q, c2q, c1k, c2k


def _prep_inputs(inputs):
    x = np.ascontiguousarray(inputs["L0_attn_x"], dtype=np.float32)
    maps = []
    eye = np.eye(128, dtype=np.float32)
    dm = np.zeros((2, 128, 256), np.float32)
    for o in range(2):
        kkk = np.arange(128)[:, None]
        qf = np.arange(256)[None, :]
        dm[o] = np.where(kkk + 128 * o <= qf, 0.0, NEG).astype(np.float32)
    wjT = np.ascontiguousarray(inputs["L0_attn_wj_w"].T)
    wjvT = np.ascontiguousarray(inputs["L0_attn_wjv_w"].T)
    woT = np.ascontiguousarray(inputs["L0_attn_wo_w"].T)
    perm64 = np.concatenate([np.arange(32, 64), np.arange(0, 32)])
    perm512 = np.concatenate([64 * h + perm64 for h in range(8)])
    com = {}
    for l in range(2):
        wq = np.ascontiguousarray(inputs["L0_ffn_wqkv"][l])
        com[f"wqkv{l}"] = wq
        com[f"wqkvp{l}"] = np.ascontiguousarray(
            np.concatenate([wq[:, :512][:, perm512],
                            wq[:, 512:1024][:, perm512]], 1))
        com[f"wol{l}"] = np.ascontiguousarray(inputs["L0_ffn_wo"][l])
        qs = inputs["L0_ffn_q_scale"][l].astype(np.float32)
        ks = inputs["L0_ffn_k_scale"][l].astype(np.float32)
        com[f"qs{l}"] = qs.reshape(1, 64).copy()
        com[f"qsp{l}"] = qs[perm64].reshape(1, 64).copy()
        com[f"ks{l}"] = ks.reshape(1, 64).copy()
        com[f"ksp{l}"] = ks[perm64].reshape(1, 64).copy()
        com[f"qscol{l}"] = np.stack(
            [np.tile(qs, 2), np.tile(qs[perm64], 2),
             np.tile(ks, 2), np.tile(ks[perm64], 2)], 1).astype(np.float32)
        lg = inputs["L0_ffn_ln_g"][l].astype(np.float32)
        lb = inputs["L0_ffn_ln_b"][l].astype(np.float32)
        com[f"lng{l}"] = np.ascontiguousarray(lg.reshape(8, 128).T)
        com[f"lnb{l}"] = np.ascontiguousarray(lb.reshape(8, 128).T)
        com[f"lngr{l}"] = lg.reshape(1, 1024).copy()
        com[f"lnbr{l}"] = lb.reshape(1, 1024).copy()
        w1 = inputs["L0_ffn_w1"][l].astype(np.float32)
        w1r = np.zeros((1024, 5632), np.float32)
        w1r[:, :2730] = w1[:, :2730]
        w1r[:, 2816:2816 + 2730] = w1[:, 2730:]
        com[f"w1r{l}"] = w1r
        b1 = inputs["L0_ffn_b1"][l].astype(np.float32)
        b1f = np.zeros((128, 44), np.float32)
        b1f[:, :22] = np.pad(b1[:2730], (0, 86)).reshape(22, 128).T
        b1f[:, 22:] = np.pad(b1[2730:], (0, 86)).reshape(22, 128).T
        com[f"b1f{l}"] = b1f
        w2 = inputs["L0_ffn_w2"][l].astype(np.float32)
        w2r = np.zeros((2816, 1024), np.float32)
        w2r[:2730] = w2
        com[f"w2r{l}"] = w2r
        com[f"b2f{l}"] = np.ascontiguousarray(
            inputs["L0_ffn_b2"][l].astype(np.float32).reshape(8, 128).T)
    com["p2h_i"] = np.stack([np.where(np.arange(128) < 64, 1.0, 0.0),
                             np.where(np.arange(128) >= 64, 1.0, 0.0)]).astype(np.float32)
    com["hd2_i"] = np.ascontiguousarray(com["p2h_i"].T)
    def fmaj(vec, nt):
        return np.ascontiguousarray(np.asarray(vec, np.float32).reshape(nt, 128).T)
    com["lnw_f"] = fmaj(inputs["L1_attn_ln_w"], 8)
    com["lnb_f"] = fmaj(inputs["L1_attn_ln_b"], 8)
    com["b1q_f"] = fmaj(inputs["L1_ffn_b1"], 32)
    com["b2q_f"] = fmaj(inputs["L1_ffn_b2"], 8)
    com["w1t_r"] = np.ascontiguousarray(inputs["L1_ffn_W1"].astype(np.float32).T)
    com["w2t_r"] = np.ascontiguousarray(inputs["L1_ffn_W2"].astype(np.float32).T)
    esel = np.zeros((32, 4096), np.float32)
    for n_ in range(16):
        esel[n_, 128 * n_:128 * (n_ + 1)] = 1.0
        esel[16 + n_, 2048 + 128 * n_:2048 + 128 * (n_ + 1)] = 1.0
    com["esel_b"] = esel.astype(BF)
    com["eyeb"] = np.eye(128, dtype=np.float32).astype(BF)
    c1q, c2q, c1k, c2k = _rotary_tables()
    for c in range(8):
        bi, hf = c // 2, c % 2
        o0, s0 = hf * T, (1 - hf) * T
        xb = x[bi]
        xT_keys = np.ascontiguousarray(
            np.concatenate([xb[s0:s0 + T], xb[o0:o0 + T]], 0).T)
        scm = np.zeros(K, np.float32)
        if hf == 0:
            scm[:T] = NEG
        sc_mask = np.ascontiguousarray(scm.reshape(32, 128).T)
        g0 = hf * T
        qpos = np.arange(g0, g0 + T)
        kpos = np.arange(g0 - 512, g0 + T)
        kposc = np.clip(kpos, 0, K - 1)
        rcq = np.stack([np.tile(c1q[qpos].T, (2, 1)),
                        np.tile(c2q[qpos].T, (2, 1))]).astype(BF)
        rck = np.stack([np.tile(c1k[kposc].T, (2, 1)),
                        np.tile(c2k[kposc].T, (2, 1))]).astype(BF)
        wm = np.zeros((16, 128, 640), np.float32)
        r_ = np.arange(128)[:, None]
        cb = np.arange(640)[None, :]
        for qt in range(16):
            a_ = g0 + 128 * qt
            valid = (cb >= r_) & (cb <= r_ + 512) & ((a_ - 512 + cb) >= 0)
            wm[qt] = np.where(valid, 0.0, NEG)
        cz = np.zeros((22, 128, 128), np.float32)
        cpart = np.arange(128)[:, None]
        rfree = np.arange(128)[None, :]
        for qt in range(4):
            a_ = g0 + 128 * qt
            for u in range(5):
                cbu = 128 * u + cpart
                valid = (cbu >= rfree) & (cbu <= rfree + 512) & ((a_ - 512 + cbu) >= 0)
                cz[qt * 5 + u] = valid
        cz[20] = (cpart >= rfree)
        cz[21] = (cpart <= rfree)
        gate = np.full((128, 1), 0.0 if hf == 0 else 1.0, np.float32)
        mp = dict(xT_keys=xT_keys, sc_mask=sc_mask,
                  sc_mask_row=scm.reshape(1, K).copy(), diag_mask=dm,
                  wjT=wjT, wjvT=wjvT, woT=woT, eye_f=eye,
                  rcq_t=rcq, rck_t=rck, wmask=wm, cz=cz.astype(BF),
                  gate128=gate)
        base = hf * 1024
        inp = inputs["L1_attn_in_proj_w"].astype(np.float32)
        mp["inpj"] = np.ascontiguousarray(
            np.concatenate([inp[:, base:base + 1024],
                            inp[:, 2048 + base:2048 + base + 1024]], 1))
        mp["xpj"] = np.ascontiguousarray(
            inputs["L1_attn_x_proj_w"][base:base + 1024].astype(np.float32))
        mp["dtwc"] = np.ascontiguousarray(
            inputs["L1_attn_dt_proj_w"][:, base:base + 1024].astype(np.float32))
        mp["opj"] = np.ascontiguousarray(
            inputs["L1_attn_out_proj_w"][base:base + 1024].astype(np.float32))
        al = inputs["L1_attn_A_log"][base:base + 1024].astype(np.float32)
        mp["alog_f"] = np.ascontiguousarray(
            al.reshape(8, 128, 16).transpose(1, 0, 2).reshape(128, 128))
        mp["dcol_f"] = np.ascontiguousarray(
            inputs["L1_attn_D_skip"][base:base + 1024]
            .astype(np.float32).reshape(8, 128).T)
        mp["cb_f"] = np.ascontiguousarray(
            inputs["L1_attn_conv_b"][base:base + 1024]
            .astype(np.float32).reshape(8, 128).T)
        mp["dtb_f"] = np.ascontiguousarray(
            inputs["L1_attn_dt_proj_b"][base:base + 1024]
            .astype(np.float32).reshape(8, 128).T)
        cwo = inputs["L1_attn_conv_w"][base:base + 1024].astype(np.float32)
        mp["cw_f"] = np.ascontiguousarray(
            cwo.reshape(8, 128, 4).transpose(1, 0, 2).reshape(128, 32))
        mp["w1slab"] = np.ascontiguousarray(com["w1t_r"][128 * c:128 * (c + 1)])
        mp["w2slab"] = np.ascontiguousarray(com["w2t_r"][512 * c:512 * (c + 1)])
        mp["g5c"] = np.ascontiguousarray(
            np.tile(np.array([[hf, 1 - hf]], np.float32), (128, 1)))
        mp.update(com)
        maps.append(mp)
    return maps


def kernel(**inputs):
    maps = _prep_inputs(inputs)
    k = build()
    res = run_bass_kernel_spmd(k.nc, maps, core_ids=list(range(8)))
    global _DBG_RES
    _DBG_RES = res.results
    out = np.zeros((4, K, 1024), np.float32)
    for c in range(8):
        bi, hf = c // 2, c % 2
        out[bi, hf * T:(hf + 1) * T, :] = res.results[c]["out_fm"].T
    return out
